# version 02
# baseline (speedup 1.0000x reference)
import numpy as np
import concourse.bass as bass
import concourse.mybir as mybir
from concourse.bass_utils import run_bass_kernel_spmd
from contextlib import ExitStack
from concourse.aot_env import allow_ici_all_to_all

F32 = mybir.dt.float32
BF16 = mybir.dt.bfloat16
AF = mybir.ActivationFunctionType
ALU = mybir.AluOpType
AX = mybir.AxisListType

D = 2048
DC = 16
TB = 1024
TLOC = 2048
T = 8192
NCORE = 8
NQT = 16
SCALE = float(1.0 / np.sqrt(128.0))
CSHIFT = -12.0
EPS1 = 1e-5
EPS2 = 1e-6
G8 = [[0, 1, 2, 3, 4, 5, 6, 7]]
SEG_OF_TS = [0, 2, 1, 3]
PAIR_OF_SEG = [(0, 0), (0, 1), (1, 0), (1, 1)]


class Sem:
    __slots__ = ("h", "n", "name")

    def __init__(self, h, name):
        self.h = h
        self.n = 0
        self.name = name


class Q:
    def __init__(self, eng):
        self.e = eng
        self.seen = {}

    def wait(self, *toks):
        for t in toks:
            if t is None:
                continue
            s, v = t
            if v > 0 and self.seen.get(s.name, 0) < v:
                self.e.wait_ge(s.h, v)
                self.seen[s.name] = v


def build(dbg=(), upto=None, noa2a=False):
    dbg = set(dbg)
    _a2a_ctx = allow_ici_all_to_all()
    _a2a_ctx.__enter__()
    nc = bass.Bass("TRN2", target_bir_lowering=False, debug=False)

    def din(name, shape, dtype=F32):
        return nc.dram_tensor(name, shape, dtype, kind="ExternalInput").ap()

    x_loc = din("x_loc", [TLOC, D])
    wq = din("wq", [D, D]); wk = din("wk", [D, D]); wv = din("wv", [D, D])
    wo = din("wo", [D, D]); wg = din("wg", [D, D])
    bqr = din("bqr", [128, DC]); bkr = din("bkr", [128, DC]); bvr = din("bvr", [128, DC])
    bo_row = din("bo_row", [1, D])
    gatebr = din("gatebr", [128, DC])
    ing2 = din("ing2", [2, D]); inb2 = din("inb2", [2, D])
    png2 = din("png2", [2, D]); pnb2 = din("pnb2", [2, D])
    outgB = din("outgB", [128, D]); outbB = din("outbB", [128, D])
    d1w_s = din("d1w_s", [D, 1024]); d1br = din("d1br", [128, 8])
    d2w_s = din("d2w_s", [1024, D]); d2b_row = din("d2b_row", [1, D])
    aggw_s = din("aggw_s", [D, 256]); aggbr = din("aggbr", [128, 2])
    bcw_s = din("bcw_s", [D, 256]); bcbr = din("bcbr", [128, 2])
    ident_bf = din("ident_bf", [128, 128], BF16)
    band_bf = din("band_bf", [128, 128], BF16)
    allones_bf = din("allones_bf", [128, 128], BF16)
    ones_f32_row = din("ones_f32_row", [1, 128])
    ones_or_zero = din("ones_or_zero", [1, 2])
    is_last = din("is_last", [2, 1])
    cshift_col = din("cshift_col", [128, 1])

    y_loc = nc.dram_tensor("y_loc", [TLOC, D], F32, kind="ExternalOutput").ap()

    dbg_specs = {
        "dbg_xn1T": ([128, DC * TLOC], BF16),
        "dbg_kout": ([D, TLOC], BF16),
        "dbg_vout": ([D, TLOC], BF16),
        "dbg_qt": ([128, 4 * T], BF16),
        "dbg_oout": ([NCORE * 128, TB], BF16),
        "dbg_x1": ([TLOC, D], F32),
        "dbg_vin": ([D, TLOC], BF16),
        "dbg_kin": ([D, TLOC], BF16),
        "dbg_qt2": ([128, 4 * T], BF16),
        "dbg_kout2": ([D, TLOC], BF16),
        "dbg_vout2": ([D, TLOC], BF16),
        "dbg_th": ([2, D], F32),
        "dbg_bc": ([2, D], F32),
    }
    dbg_out = {k: nc.dram_tensor(k, *dbg_specs[k], kind="ExternalOutput").ap()
               for k in dbg if k in dbg_specs}

    def dsc(name, shape, dtype):
        return nc.dram_tensor(name, shape, dtype)

    wq3 = dsc("wq3", [128, DC * D], BF16)
    wk3 = dsc("wk3", [128, DC * D], BF16)
    wv3 = dsc("wv3", [128, DC * D], BF16)
    wo2 = dsc("wo2", [128, DC * D], BF16)
    wg2 = dsc("wg2", [128, DC * D], BF16)
    d1w3 = dsc("d1w3", [128, 8 * 16 * 128], BF16)
    d2w3 = dsc("d2w3", [128, 8 * D], BF16)
    aggw3 = dsc("aggw3", [128, DC * 256], BF16)
    bcw3 = dsc("bcw3", [128, DC * 256], BF16)
    k_in = dsc("k_in", [D, TLOC], BF16)
    v_in = dsc("v_in", [D, TLOC], BF16)
    k_out = dsc("k_out", [D, TLOC], BF16)
    v_out = dsc("v_out", [D, TLOC], BF16)
    o_in = [dsc(f"o_in{s}", [NCORE * 128, TB], BF16) for s in range(4)]
    o_out = [dsc(f"o_out{s}", [NCORE * 128, TB], BF16) for s in range(4)]
    x1s = dsc("x1s", [TLOC, D], F32)
    ag_in = [dsc(f"ag_in{i}", [2, 256], F32) for i in range(2)]
    ag_out = [dsc(f"ag_out{i}", [16, 256], F32) for i in range(2)]
    ar_in = [dsc(f"ar_in{i}", [2, D], F32) for i in range(5)]
    ar_out = [dsc(f"ar_out{i}", [2, D], F32) for i in range(5)]

    st = ExitStack()
    with st:
        def sb(stk, name, shape, dtype):
            return stk.enter_context(nc.sbuf_tensor(name, shape, dtype))

        def ps(stk, name, shape, dtype):
            return stk.enter_context(nc.psum_tensor(name, shape, dtype))

        bqr_sb = sb(st, "bqr_sb", [128, DC], F32)
        bkr_sb = sb(st, "bkr_sb", [128, DC], F32)
        bvr_sb = sb(st, "bvr_sb", [128, DC], F32)
        gatebr_sb = sb(st, "gatebr_sb", [128, DC], F32)
        d1br_sb = sb(st, "d1br_sb", [128, 8], F32)
        aggbr_sb = sb(st, "aggbr_sb", [128, 2], F32)
        bcbr_sb = sb(st, "bcbr_sb", [128, 2], F32)
        ident_sb = sb(st, "ident_sb", [128, 128], BF16)
        band_sb = sb(st, "band_sb", [128, 128], BF16)
        allones_sb = sb(st, "allones_sb", [128, 128], BF16)
        onesr_sb = sb(st, "onesr_sb", [1, 128], F32)
        onesz_sb = sb(st, "onesz_sb", [1, 2], F32)
        islast_sb = sb(st, "islast_sb", [2, 1], F32)
        cshift_sb = sb(st, "cshift_sb", [128, 1], F32)
        thT = sb(st, "thT", [128, DC * 2], BF16)
        thTf = sb(st, "thTf", [128, 4], BF16)
        gT = sb(st, "gT", [128, 8 * 2], BF16)
        bcT = sb(st, "bcT", [128, DC * 2], F32)
        lnrow = sb(st, "lnrow", [2, 8], F32)
        th_tm = sb(st, "th_tm", [2, D], F32)
        scrA = sb(st, "scrA", [2, D], F32)
        biasrow = sb(st, "biasrow", [1, D], F32)
        rowbf = sb(st, "rowbf", [2, D], BF16)
        ing2_sb = sb(st, "ing2_sb", [2, D], BF16)
        inb2_sb = sb(st, "inb2_sb", [2, D], BF16)
        png2_sb = sb(st, "png2_sb", [2, D], BF16)
        pnb2_sb = sb(st, "pnb2_sb", [2, D], BF16)
        arena1 = sb(st, "arena1", [128, 4 * T], BF16)
        qt = arena1
        xn2T = arena1

        def sem(name):
            return Sem(st.enter_context(nc.semaphore(name)), name)

        ld0 = sem("ld0"); cast = sem("cast")
        xA = sem("xA"); vA = sem("vA"); sA = sem("sA"); pA = sem("pA")
        wB = sem("wB"); pB = sem("pB"); aB = sem("aB"); kst = sem("kst")
        cc = sem("cc")
        kvC = sem("kvC"); pvt = sem("pvt"); avt = sem("avt")
        cS = sem("cS"); cE = sem("cE"); cM = sem("cM"); cP = sem("cP")
        cL = sem("cL"); cLc = sem("cLc"); cLr = sem("cLr"); cNm = sem("cNm")
        cOs = sem("cOs")
        dLd = sem("dLd"); dPe = sem("dPe"); dDv = sem("dDv"); dAc = sem("dAc")
        dSt = sem("dSt")
        tH = sem("tH"); tHd = sem("tHd")
        eLd = sem("eLd"); ePe = sem("ePe"); eAc = sem("eAc"); eDv = sem("eDv")
        ySt = sem("ySt")
        DMA_SEMS = [ld0, cast, xA, wB, kst, kvC, cOs, dLd, dSt, tHd, eLd, ySt]

        with nc.Block() as block:
            wrap = {}

            def em(engname, fn):
                def body(eng):
                    q = wrap.get(engname)
                    if q is None:
                        q = wrap[engname] = Q(eng)
                    fn(q)
                getattr(block, engname)(body)

            def tok(s, inst, k=1):
                inst.then_inc(s.h, k)
                s.n += k
                return (s, s.n)

            def one(engname, s, deps, buildfn, k=1):
                out = {}

                def f(q):
                    q.wait(*deps)
                    out["t"] = tok(s, buildfn(q), k)
                em(engname, f)
                return out["t"]

            last_cc = {"t": None}

            def ccdep():
                return [(s, s.n) for s in DMA_SEMS]

            def dma(engname, s, deps, buildfn):
                return one(engname, s, list(deps) + [last_cc["t"]], buildfn, k=16)

            def coll(deps, buildfn):
                t = one("gpsimd", cc, list(deps) + ccdep() + [last_cc["t"]], buildfn)
                last_cc["t"] = t
                return t

            def p0_sync(q):
                for dst, src in [
                    (bqr_sb, bqr), (bkr_sb, bkr), (bvr_sb, bvr),
                    (gatebr_sb, gatebr),
                    (biasrow, bo_row), (d1br_sb, d1br),
                    (aggbr_sb, aggbr), (bcbr_sb, bcbr),
                    (ident_sb, ident_bf), (band_sb, band_bf),
                    (allones_sb, allones_bf), (onesr_sb, ones_f32_row),
                    (onesz_sb, ones_or_zero), (islast_sb, is_last),
                    (cshift_sb, cshift_col),
                ]:
                    tok(ld0, q.e.dma_start(out=dst[:, :], in_=src), 16)
            em("sync", p0_sync)
            LD0 = (ld0, ld0.n)

            cast_toks = {}

            def p0_g(q):
                for dst, srcp in [(ing2_sb, ing2), (inb2_sb, inb2),
                                  (png2_sb, png2), (pnb2_sb, pnb2)]:
                    tok(ld0, q.e.dma_start(out=dst[:, :], in_=srcp), 16)
                for nm, w3, w in [("wk", wk3, wk), ("wv", wv3, wv), ("wq", wq3, wq)]:
                    w3v = w3.rearrange("p (f c o) -> p f c o", c=DC, o=128)
                    for cc_ in range(DC):
                        t = tok(cast, q.e.dma_start(
                            out=w3v[:, :, cc_, :],
                            in_=w[128 * cc_:128 * (cc_ + 1), :].rearrange(
                                "p (f o) -> p f o", o=128)), 16)
                    cast_toks[nm] = t
            em("gpsimd", p0_g)

            def late_casts(q):
                q.wait(last_cc["t"])
                for hp in range(DC):
                    t = tok(cast, q.e.dma_start(
                        out=wo2[:, D * hp:D * (hp + 1)],
                        in_=wo[128 * hp:128 * (hp + 1), :]), 16)
                cast_toks["wo"] = t
                wg2v = wg2.rearrange("p (f c o) -> p f c o", c=DC, o=128)
                for dc_ in range(DC):
                    t = tok(cast, q.e.dma_start(
                        out=wg2v[:, :, dc_, :],
                        in_=wg[128 * dc_:128 * (dc_ + 1), :].rearrange(
                            "p (f o) -> p f o", o=128)), 16)
                cast_toks["wg"] = t
                for nm, w3, w in [("agg", aggw3, aggw_s), ("bc", bcw3, bcw_s)]:
                    w3v = w3.rearrange("p (g c o) -> p g c o", c=DC, o=128)
                    for dc_ in range(DC):
                        t = tok(cast, q.e.dma_start(
                            out=w3v[:, :, dc_, :],
                            in_=w[128 * dc_:128 * (dc_ + 1), :].rearrange(
                                "p (g o) -> p g o", o=128)), 16)
                    cast_toks[nm] = t
                d1v = d1w3.rearrange("p (g c o) -> p g c o", c=DC, o=128)
                for cc_ in range(DC):
                    t = tok(cast, q.e.dma_start(
                        out=d1v[:, :, cc_, :],
                        in_=d1w_s[128 * cc_:128 * (cc_ + 1), :].rearrange(
                            "p (g o) -> p g o", o=128)), 16)
                cast_toks["d1"] = t
                d2v = d2w3.rearrange("p (t g o) -> p t g o", g=8, o=512)
                for oc_ in range(8):
                    t = tok(cast, q.e.dma_start(
                        out=d2v[:, :, oc_, :],
                        in_=d2w_s[128 * oc_:128 * (oc_ + 1), :].rearrange(
                            "p (t o) -> p t o", o=512)), 16)
                cast_toks["d2"] = t

            def emit_ln(src, dst, W, eps, deps, sq_buf, serial_prev, v_sem, s_sem,
                        mu, m2, var, rstd):
                t1 = one("vector", v_sem, list(deps) + [serial_prev],
                         lambda q: q.e.reduce_sum(mu, src, axis=AX.X))
                t2s = one("scalar", s_sem, list(deps) + [serial_prev],
                          lambda q: q.e.activation(sq_buf, src, AF.Square))
                t2 = one("vector", v_sem, [t2s, t1],
                         lambda q: q.e.reduce_sum(m2, sq_buf, axis=AX.X))
                t3 = one("vector", v_sem, [t2],
                         lambda q: q.e.tensor_scalar(mu, mu, 1.0 / W, 0.0, ALU.mult, ALU.add))
                t4 = one("vector", v_sem, [t3],
                         lambda q: q.e.tensor_scalar(m2, m2, 1.0 / W, 0.0, ALU.mult, ALU.add))
                t5 = one("vector", v_sem, [t4],
                         lambda q: q.e.tensor_tensor(var, mu, mu, ALU.mult))
                t7 = one("vector", v_sem, [t5],
                         lambda q: q.e.tensor_tensor(var, m2, var, ALU.subtract))
                t8 = one("vector", v_sem, [t7],
                         lambda q: q.e.tensor_scalar(var, var, eps, 0.0, ALU.add, ALU.add))
                t9 = one("vector", v_sem, [t8], lambda q: q.e.reciprocal(rstd, var))
                t10 = one("scalar", s_sem, [t9], lambda q: q.e.activation(rstd, rstd, AF.Sqrt))
                return one("vector", v_sem, [t10],
                           lambda q: q.e.tensor_scalar(dst, src, mu[:, 0:1], rstd[:, 0:1],
                                                       ALU.subtract, ALU.mult))

            stAB = ExitStack()
            xn1T = sb(stAB, "xn1T", [128, DC * TLOC], BF16)
            xsbA = sb(stAB, "xsbA", [128, D], F32)
            xn1s = [sb(stAB, f"xn1s{i}", [128, D], BF16) for i in range(2)]
            lnst = sb(stAB, "lnst", [128, 8], F32)
            wstg = [sb(stAB, f"wstg{i}", [128, DC * 128], BF16) for i in range(2)]
            kvstg = [sb(stAB, f"kvstg{i}", [128, 512], BF16) for i in range(2)]
            psA = [ps(stAB, f"psA{i}", [128, 128], BF16) for i in range(2)]
            psB = [ps(stAB, f"psB{i}", [128, 512], F32) for i in range(2)]
            MU = lnst[:, 0:1]; M2 = lnst[:, 1:2]; VAR = lnst[:, 2:3]; RSTD = lnst[:, 3:4]

            a_fin = [None] * 16
            a_tr = []
            a_cp = []
            for g in range(16):
                p = g % 2
                tx = one("sync", xA, [a_fin[g - 1] if g >= 1 else None],
                         lambda q, g=g: q.e.dma_start(
                             out=xsbA[:, :], in_=x_loc[128 * g:128 * (g + 1), :]), k=16)
                a_fin[g] = emit_ln(xsbA[:, :], xn1s[p][:, :], D, EPS1,
                                   [tx, LD0], xn1s[p][:, :],
                                   a_fin[g - 1] if g >= 1 else None,
                                   vA, sA, MU, M2, VAR, RSTD)
                for dc in range(DC):
                    pp = dc % 2
                    idx = g * DC + dc
                    t = one("tensor", pA,
                            [a_fin[g], a_cp[idx - 2] if idx >= 2 else LD0],
                            lambda q, p=p, dc=dc, pp=pp: q.e.transpose(
                                psA[pp][:, :], xn1s[p][:, 128 * dc:128 * (dc + 1)],
                                ident_sb[:, :]))
                    a_tr.append(t)
                    a_cp.append(one("scalar", sA, [t],
                                    lambda q, g=g, dc=dc, pp=pp: q.e.copy(
                                        xn1T[:, TLOC * dc + 128 * g: TLOC * dc + 128 * (g + 1)],
                                        psA[pp][:, :])))

            if "dbg_xn1T" in dbg_out:
                one("sync", ySt, [a_cp[-1]],
                    lambda q: q.e.dma_start(out=dbg_out["dbg_xn1T"], in_=xn1T[:, :]), k=16)

            qseg_tok = [None] * 4
            kv_a2a = {}
            wload = []
            wmm_done = []
            proj_pe = []
            proj_ev = []
            kv_store = {}

            def proj(mat):
                w3 = {"k": wk3, "v": wv3, "q": wq3}[mat]
                br = {"k": bkr_sb, "v": bvr_sb, "q": bqr_sb}[mat]
                dst = {"k": k_in, "v": v_in}.get(mat)
                qtv = qt.rearrange("p (s k) -> p s k", k=16)
                for phi in range(DC):
                    wp = len(wload) % 2
                    wfree = wmm_done[len(wload) - 2] if len(wload) >= 2 else None
                    tw = one("sync", wB, [cast_toks["w" + mat], wfree],
                             lambda q, w3=w3, phi=phi, wp=wp: q.e.dma_start(
                                 out=wstg[wp][:, :],
                                 in_=w3[:, D * phi:D * (phi + 1)]), k=16)
                    wload.append(tw)
                    for ts4 in range(4):
                        idx = len(proj_pe)
                        bp = idx % 2
                        psum_free = proj_ev[idx - 2] if idx >= 2 else None

                        def bpe(q, phi=phi, ts4=ts4, wp=wp, bp=bp, tw=tw,
                                psum_free=psum_free):
                            q.wait(tw, a_cp[(4 * ts4 + 3) * DC + DC - 1], psum_free)
                            for dc in range(DC):
                                inst = q.e.matmul(
                                    psB[bp][:, :],
                                    wstg[wp][:, 128 * dc:128 * (dc + 1)],
                                    xn1T[:, TLOC * dc + 512 * ts4: TLOC * dc + 512 * (ts4 + 1)],
                                    start=(dc == 0), stop=(dc == DC - 1))
                            bpe.tok = tok(pB, inst)
                        em("tensor", bpe)
                        proj_pe.append(bpe.tok)

                        if mat == "q":
                            seg = SEG_OF_TS[ts4]
                            qseg_tok[seg] = one("scalar", aB, [proj_pe[idx]],
                                                lambda q, bp=bp, phi=phi, seg=seg, br=br:
                                                q.e.activation(
                                                    qtv[:, T // 16 * seg: T // 16 * seg + 512, phi],
                                                    psB[bp][:, :], AF.Identity,
                                                    bias=br[:, phi:phi + 1]))
                            proj_ev.append(qseg_tok[seg])
                        else:
                            sp = idx % 2
                            tev = one("scalar", aB,
                                      [proj_pe[idx], kv_store.get(idx - 2)],
                                      lambda q, bp=bp, phi=phi, br=br, sp=sp:
                                      q.e.activation(kvstg[sp][:, :], psB[bp][:, :],
                                                     AF.Identity, bias=br[:, phi:phi + 1]))
                            proj_ev.append(tev)
                            kv_store[idx] = one("scalar", kst, [tev],
                                                lambda q, phi=phi, ts4=ts4, sp=sp, dst=dst:
                                                q.e.dma_start(
                                                    out=dst[128 * phi:128 * (phi + 1),
                                                            512 * ts4:512 * (ts4 + 1)],
                                                    in_=kvstg[sp][:, :]), k=16)
                    wmm_done.append(proj_pe[-1])

            proj("k")
            proj("v")
            proj("q")
            for mat in ("k", "v"):
                srcb, dstb = (k_in, k_out) if mat == "k" else (v_in, v_out)
                if noa2a:
                    kv_a2a[mat] = one("gpsimd", cc, [(kst, kst.n)],
                                      lambda q, srcb=srcb, dstb=dstb: q.e.dma_start(
                                          out=dstb[:, :], in_=srcb[:, :]), k=16)
                else:
                    kv_a2a[mat] = coll([],
                                       lambda q, srcb=srcb, dstb=dstb: q.e.collective_compute(
                                           "AllToAll", ALU.bypass, replica_groups=G8,
                                           ins=[srcb.ap().opt()], outs=[dstb.ap().opt()]))
            em("gpsimd", late_casts)
            B_PE_END = proj_pe[-1]
            B_EV_END = proj_ev[-1]

            if "dbg_kout" in dbg_out:
                one("sync", ySt, [kv_a2a["k"]],
                    lambda q: q.e.dma_start(out=dbg_out["dbg_kout"], in_=k_out[:, :]), k=16)
                one("sync", ySt, [kv_a2a["v"]],
                    lambda q: q.e.dma_start(out=dbg_out["dbg_vout"], in_=v_out[:, :]), k=16)
            if "dbg_vin" in dbg_out:
                one("sync", ySt, [(kst, kst.n)],
                    lambda q: q.e.dma_start(out=dbg_out["dbg_vin"], in_=v_in[:, :]), k=16)
                one("sync", ySt, [(kst, kst.n)],
                    lambda q: q.e.dma_start(out=dbg_out["dbg_kin"], in_=k_in[:, :]), k=16)
            if "dbg_qt" in dbg_out:
                one("sync", ySt, [qseg_tok[0], qseg_tok[1], qseg_tok[2], qseg_tok[3],
                                  kv_a2a["k"], kv_a2a["v"]],
                    lambda q: q.e.dma_start(out=dbg_out["dbg_qt"], in_=qt[:, :]), k=16)

            if upto == "B":
                def finB(q):
                    q.wait((ySt, ySt.n), (cc, cc.n), (kst, kst.n), (aB, aB.n))
                em("sync", finB)
                em("gpsimd", finB)
                stAB.close()
                _a2a_ctx.__exit__(None, None, None)
                return nc

            stAB.close()
            stC = ExitStack()
            kt = [sb(stC, f"kt{i}", [128, T], BF16) for i in range(2)]
            vt = sb(stC, "vt", [128, T], BF16)
            vp = [sb(stC, f"vp{i}", [128, T], BF16) for i in range(2)]
            PT = [sb(stC, f"PT{i}", [128, 1024], BF16) for i in range(2)]
            acc16 = sb(stC, "acc16", [128, 1024], BF16)
            OTs = [sb(stC, f"OTs{i}", [128, 512], BF16) for i in range(4)]
            lsb = sb(stC, "lsb", [128, 512], F32)
            binv = sb(stC, "binv", [128, 512], F32)
            s_ps = [ps(stC, f"s_ps{i}", [128, 1024], F32) for i in range(2)]
            o_ps = ps(stC, "o_ps", [128, 512], F32)
            lb_ps = ps(stC, "lb_ps", [128, 512], F32)
            vtr_ps = ps(stC, "vtr_ps", [128, 256], BF16)

            pair_pe_end = [None] * 4
            oa2a = [None] * 4
            vevac_end = [None] * 4
            vtr_pe_end = [None] * 4
            kvload_tok = [None] * 4
            sps_free = {0: None, 1: None}
            pt_free = {0: [], 1: []}
            ot_free = {i: None for i in range(4)}
            state = {"last_nm": None, "last_lmm": None, "last_lc": None, "last_lr": None}

            def emit_pv(rec, j):
                it, p2, t_m, t_e, first_wait = rec

                def cpv(q, it=it, p2=p2, t_m=t_m, j=j, first_wait=first_wait):
                    q.wait(t_m)
                    if it == 0:
                        q.wait(first_wait)
                    q.e.matmul(o_ps[:, :], vp[j][:, 256 * it:256 * it + 128],
                               PT[p2][:, 0:512], start=(it == 0), stop=False)
                    i1 = q.e.matmul(o_ps[:, :], vp[j][:, 256 * it + 128:256 * it + 256],
                                    PT[p2][:, 512:1024], start=False, stop=False)
                    cpv.tok = tok(cP, i1)
                em("tensor", cpv)
                pt_free[p2] = [t_m, cpv.tok]
                return cpv.tok

            def emit_cld(seg):
                lh, b = PAIR_OF_SEG[seg]
                j = seg % 2

                def cld(q, seg=seg, lh=lh, b=b, j=j):
                    q.wait(kv_a2a["k"], kv_a2a["v"], B_PE_END, last_cc["t"])
                    if seg >= 1:
                        q.wait(vtr_pe_end[seg - 1])
                    if seg >= 2:
                        q.wait(pair_pe_end[seg - 2], vevac_end[seg - 2])
                    for r in range(NCORE):
                        tok(kvC, q.e.dma_start(
                            out=kt[j][:, TB * r:TB * (r + 1)],
                            in_=k_out[256 * r + 128 * lh:256 * r + 128 * (lh + 1),
                                      TB * b:TB * (b + 1)]), 16)
                        tok(kvC, q.e.dma_start(
                            out=vt[:, TB * r:TB * (r + 1)],
                            in_=v_out[256 * r + 128 * lh:256 * r + 128 * (lh + 1),
                                      TB * b:TB * (b + 1)]), 16)
                em("sync", cld)
                kvload_tok[seg] = (kvC, kvC.n)

            emit_cld(0)
            for seg in range(4):
                lh, b = PAIR_OF_SEG[seg]
                j = seg % 2
                KV_TOK = kvload_tok[seg]

                vtr_e = []
                for kc in range(64):
                    pp = kc % 2
                    frees = ([vtr_e[kc - 2]] if kc >= 2
                             else [B_EV_END, (avt, avt.n), (cE, cE.n)])
                    t = one("tensor", pvt, [KV_TOK] + frees,
                            lambda q, kc=kc, pp=pp: q.e.transpose(
                                vtr_ps[:, 128 * pp:128 * (pp + 1)],
                                vt[:, 128 * kc:128 * (kc + 1)], ident_sb[:, :]))
                    vtr_e.append(one("scalar", avt, [t],
                                     lambda q, j=j, kc=kc, pp=pp: q.e.copy(
                                         vp[j][:, 128 * kc:128 * (kc + 1)],
                                         vtr_ps[:, 128 * pp:128 * (pp + 1)])))
                vevac_end[seg] = vtr_e[-1]
                vtr_pe_end[seg] = (pvt, pvt.n)

                gi = 0
                pend_pv = []
                for jq in range(NQT):
                    L = 2 * jq + 2
                    q0 = 512 * jq
                    acc_first_wait = state["last_lmm"]
                    iter_m = []
                    for it in range(L):
                        p2 = gi % 2
                        k0 = 256 * it
                        k1 = k0 + 128

                        def cs(q, it=it, p2=p2, gi=gi, j=j, q0=q0, seg=seg,
                               sfree=sps_free[p2]):
                            q.wait(qseg_tok[seg], sfree)
                            if seg == 0 and gi < 2:
                                q.wait(B_EV_END)
                            q.e.matmul(s_ps[p2][:, 0:512],
                                       kt[j][:, 256 * it:256 * it + 128],
                                       qt[:, T * seg + q0: T * seg + q0 + 512],
                                       start=True, stop=True)
                            i1 = q.e.matmul(s_ps[p2][:, 512:1024],
                                            kt[j][:, 256 * it + 128:256 * it + 256],
                                            qt[:, T * seg + q0: T * seg + q0 + 512],
                                            start=True, stop=True)
                            cs.tok = tok(cS, i1)
                        em("tensor", cs)
                        t_s = cs.tok

                        if pend_pv:
                            emit_pv(pend_pv.pop(0), j)

                        t_e = one("scalar", cE, [t_s] + pt_free[p2],
                                  lambda q, p2=p2: q.e.activation(
                                      PT[p2][:, :], s_ps[p2][:, :], AF.Exp,
                                      bias=cshift_sb[:, 0:1], scale=SCALE))
                        sps_free[p2] = t_e

                        def cmk(q, p2=p2, it=it, q0=q0, k0=k0, k1=k1, t_e=t_e,
                                first=(it == 0), faw=acc_first_wait):
                            last = t_e
                            q.wait(last)
                            if k0 >= q0:
                                off = k0 - q0
                                if off > 0:
                                    last = tok(cM, q.e.memset(PT[p2][:, 0:off], 0.0))
                                    q.wait(last)
                                last = tok(cM, q.e.tensor_tensor(
                                    PT[p2][:, off:off + 128], PT[p2][:, off:off + 128],
                                    band_sb[:, :], ALU.mult))
                                q.wait(last)
                            if k1 >= q0:
                                off = k1 - q0
                                if off > 0:
                                    last = tok(cM, q.e.memset(
                                        PT[p2][:, 512:512 + off], 0.0))
                                    q.wait(last)
                                last = tok(cM, q.e.tensor_tensor(
                                    PT[p2][:, 512 + off:512 + off + 128],
                                    PT[p2][:, 512 + off:512 + off + 128],
                                    band_sb[:, :], ALU.mult))
                                q.wait(last)
                            if first:
                                q.wait(faw)
                                cmk.tok = tok(cM, q.e.tensor_copy(acc16[:, :], PT[p2][:, :]))
                            else:
                                cmk.tok = tok(cM, q.e.tensor_tensor(
                                    acc16[:, :], acc16[:, :], PT[p2][:, :], ALU.add))
                        em("vector", cmk)
                        iter_m.append(cmk.tok)
                        pend_pv.append((it, p2, cmk.tok, t_e, state["last_nm"]))
                        gi += 1

                    while pend_pv:
                        emit_pv(pend_pv.pop(0), j)
                    pv_done = (cP, cP.n)

                    def clm(q, am=iter_m[-1], lcp=state["last_lc"]):
                        q.wait(am, lcp)
                        q.e.matmul(lb_ps[:, :], allones_sb[:, :], acc16[:, 0:512],
                                   start=True, stop=False)
                        i1 = q.e.matmul(lb_ps[:, :], allones_sb[:, :], acc16[:, 512:1024],
                                        start=False, stop=True)
                        clm.tok = tok(cL, i1)
                    em("tensor", clm)
                    state["last_lmm"] = clm.tok
                    state["last_lc"] = one("scalar", cLc, [clm.tok, state["last_lr"]],
                                           lambda q: q.e.copy(lsb[:, :], lb_ps[:, :]))
                    state["last_lr"] = one("vector", cLr,
                                           [state["last_lc"], state["last_nm"]],
                                           lambda q: q.e.reciprocal(binv[:, :], lsb[:, :]))
                    t_nm = one("vector", cNm,
                               [state["last_lr"], pv_done, ot_free[jq % 4]],
                               lambda q, jq=jq: q.e.tensor_tensor(
                                   OTs[jq % 4][:, :], o_ps[:, :], binv[:, :], ALU.mult))
                    state["last_nm"] = t_nm
                    ot_free[jq % 4] = dma("sync", cOs, [t_nm],
                                          lambda q, jq=jq, seg=seg: q.e.dma_start(
                                              out=o_in[seg][128 * (jq // 2):128 * (jq // 2 + 1),
                                                            512 * (jq % 2):512 * (jq % 2) + 512],
                                              in_=OTs[jq % 4][:, :]))

                pair_pe_end[seg] = (cP, cP.n)
                if seg < 3:
                    emit_cld(seg + 1)
                oa2a[seg] = coll([], lambda q, seg=seg: q.e.collective_compute(
                    "AllToAll", ALU.bypass, replica_groups=G8,
                    ins=[o_in[seg].ap().opt()], outs=[o_out[seg].ap().opt()]))

            if upto == "C":
                def finC(q):
                    q.wait((ySt, ySt.n), (cc, cc.n), (cOs, cOs.n), (kvC, kvC.n),
                           (cast, cast.n), (cP, cP.n), (cNm, cNm.n))
                em("sync", finC)
                em("gpsimd", finC)
                stC.close()
                _a2a_ctx.__exit__(None, None, None)
                return nc

            if "dbg_oout" in dbg_out:
                one("sync", ySt, [oa2a[0]],
                    lambda q: q.e.dma_start(out=dbg_out["dbg_oout"], in_=o_out[0][:, :]), k=16)
            if "dbg_qt2" in dbg_out:
                one("sync", ySt, [oa2a[3]],
                    lambda q: q.e.dma_start(out=dbg_out["dbg_qt2"], in_=qt[:, :]), k=16)
                one("sync", ySt, [oa2a[3]],
                    lambda q: q.e.dma_start(out=dbg_out["dbg_kout2"], in_=k_out[:, :]), k=16)
                one("sync", ySt, [oa2a[3]],
                    lambda q: q.e.dma_start(out=dbg_out["dbg_vout2"], in_=v_out[:, :]), k=16)

            stC.close()
            C_END = [(cP, cP.n), (cNm, cNm.n), (cOs, cOs.n), (cLc, cLc.n),
                     (cLr, cLr.n), (avt, avt.n), (cM, cM.n), (cE, cE.n)]
            stD = ExitStack()
            wob = sb(stD, "wob", [128, DC * D], BF16)
            osb = [sb(stD, f"osb{l_}", [128, 1024], BF16) for l_ in range(2)]
            xsb2 = sb(stD, "xsb2", [128, D], F32)
            x1sb = sb(stD, "x1sb", [128, D], F32)
            xn2st = [sb(stD, f"xn2st{i}", [128, D], BF16) for i in range(2)]
            lnstD = sb(stD, "lnstD", [128, 8], F32)
            aggslab = sb(stD, "aggslab", [128, D], BF16)
            psD = [ps(stD, f"psD{i}", [128, 512], F32) for i in range(2)]
            psT = [ps(stD, f"psT{i}", [128, 128], BF16) for i in range(2)]
            psRT = ps(stD, "psRT", [128, 2], F32)
            psRTb = ps(stD, "psRTb", [128, 2], BF16)
            psRow = ps(stD, "psRow", [2, 128], BF16)
            MUD = lnstD[:, 0:1]; M2D = lnstD[:, 1:2]
            VARD = lnstD[:, 2:3]; RSTDD = lnstD[:, 3:4]

            t_wob = dma("sync", dLd, [cast_toks["wo"]] + C_END,
                        lambda q: q.e.dma_start(out=wob[:, :], in_=wo2[:, :]))


            chunk_order = [(0, 7), (1, 7)] + [(b_, t_) for b_ in range(2) for t_ in range(7)]
            d_pe = []
            d_dv = []
            d_ln = [None] * 16
            d_cp = []
            d_tr = []
            d_ld_tok = []
            d_store = {}
            tr_last = {}
            pend_tr = []
            ar0 = {}

            def emit_d_transposes(gc, p, b, tc):
                row0 = 1024 * b + 128 * tc
                for dc in range(DC):
                    pp = dc % 2
                    idx = len(d_tr)
                    t = one("tensor", dPe,
                            [d_ln[gc], d_cp[idx - 2] if idx >= 2 else None],
                            lambda q, p=p, dc=dc, pp=pp: q.e.transpose(
                                psT[pp][:, :], xn2st[p][:, 128 * dc:128 * (dc + 1)],
                                ident_sb[:, :]))
                    d_tr.append(t)
                    d_cp.append(one("scalar", dAc, [t],
                                    lambda q, dc=dc, pp=pp, row0=row0: q.e.copy(
                                        xn2T[:, TLOC * dc + row0: TLOC * dc + row0 + 128],
                                        psT[pp][:, :])))
                tr_last[gc] = d_tr[-1]

            for gc, (b, tc) in enumerate(chunk_order):
                p = gc % 2
                row0 = 1024 * b + 128 * tc

                def dld(q, gc=gc, b=b, tc=tc, p=p, row0=row0):
                    q.wait(oa2a[0], oa2a[1], oa2a[2], oa2a[3], last_cc["t"])
                    if gc >= 1:
                        q.wait(d_pe[4 * (gc - 1) + 3], d_dv[4 * (gc - 1) + 3])
                    for l_ in range(2):
                        seg = 2 * l_ + b
                        tok(dLd, q.e.dma_start(
                            out=osb[l_].rearrange("p (r c) -> p r c", c=128),
                            in_=o_out[seg][:, 128 * tc:128 * (tc + 1)].rearrange(
                                "(r p) c -> p r c", p=128)), 16)
                    tok(dLd, q.e.dma_start(out=xsb2[:, :],
                                           in_=x_loc[row0:row0 + 128, :]), 16)
                em("sync", dld)
                d_ld_tok.append((dLd, dLd.n))

                for ct in range(4):
                    idx = 4 * gc + ct
                    bp = idx % 2

                    def dpe(q, gc=gc, p=p, ct=ct, bp=bp, idx=idx):
                        q.wait(t_wob, d_ld_tok[gc], LD0,
                               d_dv[idx - 2] if idx >= 2 else None)
                        q.e.matmul(psD[bp][:, :], onesr_sb[:, :],
                                   biasrow[:, 512 * ct:512 * (ct + 1)],
                                   start=True, stop=False)
                        for hp in range(16):
                            inst = q.e.matmul(
                                psD[bp][:, :],
                                osb[hp % 2][:, 128 * (hp // 2):128 * (hp // 2 + 1)],
                                wob[:, D * hp + 512 * ct: D * hp + 512 * (ct + 1)],
                                start=False, stop=(hp == 15))
                        dpe.tok = tok(dPe, inst)
                    em("tensor", dpe)
                    d_pe.append(dpe.tok)
                    d_dv.append(one("vector", dDv,
                                    [d_pe[idx], d_store.get(gc - 1)],
                                    lambda q, ct=ct, bp=bp: q.e.tensor_tensor(
                                        x1sb[:, 512 * ct:512 * (ct + 1)], psD[bp][:, :],
                                        xsb2[:, 512 * ct:512 * (ct + 1)], ALU.add)))

                ln_deps = [d_dv[4 * gc + 3], tr_last.get(gc - 2)]
                if gc in (2, 3) and "lr1" in ar0:
                    ln_deps += [ar0["lr1"], ar0["lr2"]]
                d_ln[gc] = emit_ln(x1sb[:, :], xn2st[p][:, :], D, EPS1, ln_deps,
                                   xn2st[p][:, :], d_ln[gc - 1] if gc >= 1 else None,
                                   dDv, dAc, MUD, M2D, VARD, RSTDD)
                d_store[gc] = dma("sync", dSt, [d_ln[gc]],
                                  lambda q, row0=row0: q.e.dma_start(
                                      out=x1s[row0:row0 + 128, :], in_=x1sb[:, :]))

                pend_tr.append((gc, p, b, tc))
                if len(pend_tr) > 1:
                    emit_d_transposes(*pend_tr.pop(0))

                if gc == 1:
                    ar0["lr1"] = dma("gpsimd", tHd, [d_ln[0]],
                                     lambda q: q.e.dma_start(out=scrA[0:1, :],
                                                             in_=xn2st[0][127:128, :]))
                    ar0["lr2"] = dma("gpsimd", tHd, [d_ln[1]],
                                     lambda q: q.e.dma_start(out=scrA[1:2, :],
                                                             in_=xn2st[1][127:128, :]))
                    t3 = one("vector", tH, [ar0["lr1"], ar0["lr2"], LD0],
                             lambda q: q.e.tensor_scalar(
                                 scrA[:, :], scrA[:, :], islast_sb[:, 0:1], 0.0,
                                 ALU.mult, ALU.add))
                    t4 = dma("gpsimd", tHd, [t3],
                             lambda q: q.e.dma_start(out=ar_in[0][:, :], in_=scrA[:, :]))
                    ar0["cc"] = coll([t4], lambda q: q.e.collective_compute(
                                        "AllReduce", ALU.add, replica_groups=G8,
                                        ins=[ar_in[0].ap().opt()],
                                        outs=[ar_out[0].ap().opt()]))
            while pend_tr:
                emit_d_transposes(*pend_tr.pop(0))
            D_TR_END = d_cp[-1]

            if "dbg_x1" in dbg_out:
                one("sync", ySt, [d_store[15]],
                    lambda q: q.e.dma_start(out=dbg_out["dbg_x1"], in_=x1s[:, :]), k=16)

            RMU = lnrow[:, 0:1]; RM2 = lnrow[:, 1:2]
            RVAR = lnrow[:, 2:3]; RRSTD = lnrow[:, 3:4]

            def row_ln(src, gamma, beta, deps):
                tn = emit_ln(src, rowbf[:, :], D, EPS2, deps, rowbf[:, :], None,
                             tH, tH, RMU, RM2, RVAR, RRSTD)
                tg = one("vector", tH, [tn],
                         lambda q: q.e.tensor_tensor(rowbf[:, :], rowbf[:, :],
                                                     gamma[:, :], ALU.mult))
                return one("vector", tH, [tg],
                           lambda q: q.e.tensor_tensor(rowbf[:, :], rowbf[:, :],
                                                       beta[:, :], ALU.add))

            t_ar0 = dma("gpsimd", tHd, [ar0["cc"]],
                        lambda q: q.e.dma_start(out=scrA[:, :], in_=ar_out[0][:, :]))
            t_xbf = row_ln(scrA[:, :], ing2_sb, inb2_sb, [t_ar0])

            def row_T(src_bf, dst, n, dep, psum, first_extra=()):
                last = dep
                for i in range(n):
                    deps = [last] + (list(first_extra) if i == 0 else [])
                    t = one("tensor", tH, deps,
                            lambda q, i=i: q.e.transpose(
                                psum[:, :], src_bf[:, 128 * i:128 * (i + 1)],
                                ident_sb[0:2, 0:2]))
                    last = one("scalar", tH, [t],
                               lambda q, i=i: q.e.copy(dst[:, 2 * i:2 * i + 2], psum[:, :]))
                return last

            t_thT = row_T(rowbf, thT, DC, t_xbf, psRTb)

            agg_done = []
            last_tok = t_thT
            for oc in range(2):
                t_agw = dma("sync", dLd, [cast_toks["agg"]] + ([last_tok] if oc else C_END),
                            lambda q, oc=oc: q.e.dma_start(
                                out=aggslab[:, :],
                                in_=aggw3[:, D * oc:D * (oc + 1)]))

                def amm(q, oc=oc, last_tok=last_tok, t_agw=t_agw):
                    q.wait(last_tok, t_agw)
                    for dc in range(DC):
                        inst = q.e.matmul(
                            psRT[:, :],
                            aggslab[:, 128 * dc:128 * (dc + 1)],
                            thT[:, 2 * dc:2 * (dc + 1)],
                            start=(dc == 0), stop=(dc == DC - 1))
                    amm.tok = tok(tH, inst)
                em("tensor", amm)
                t_ev = one("scalar", tH, [amm.tok],
                           lambda q, oc=oc: q.e.activation(
                               thTf[:, 2 * oc:2 * (oc + 1)], psRT[:, :], AF.Identity,
                               bias=aggbr_sb[:, oc:oc + 1]))
                t_bt = one("tensor", tH, [t_ev],
                           lambda q, oc=oc: q.e.transpose(
                               psRow[:, :], thTf[:, 2 * oc:2 * (oc + 1)], ident_sb[:, :]))
                t_cp2 = one("scalar", tH, [t_bt],
                            lambda q, oc=oc: q.e.copy(scrA[0:2, 128 * oc:128 * (oc + 1)],
                                                      psRow[:, :]))
                last_tok = t_cp2
                agg_done.append(t_cp2)
            t_agst = dma("gpsimd", tHd, agg_done,
                         lambda q: q.e.dma_start(out=ag_in[0][:, :],
                                                 in_=scrA[0:2, 0:256]))
            t_agcc = coll([t_agst], lambda q: q.e.collective_compute(
                             "AllGather", ALU.bypass, replica_groups=G8,
                             ins=[ag_in[0].ap().opt()], outs=[ag_out[0].ap().opt()]))
            t_thld = None
            for r in range(NCORE):
                t_thld = dma("gpsimd", tHd, [t_agcc],
                             lambda q, r=r: q.e.dma_start(
                                 out=th_tm[:, 256 * r:256 * (r + 1)],
                                 in_=ag_out[0][2 * r:2 * r + 2, :]))
            TH0 = t_thld

            if upto == "D":
                def finD(q):
                    q.wait((ySt, ySt.n), (cc, cc.n), (dSt, dSt.n), (tHd, tHd.n),
                           (dLd, dLd.n), (dAc, dAc.n), (dPe, dPe.n), (tH, tH.n))
                em("sync", finD)
                em("gpsimd", finD)
                stD.close()
                _a2a_ctx.__exit__(None, None, None)
                return nc

            D_END = [(dPe, dPe.n), (dDv, dDv.n), (dAc, dAc.n), (dSt, dSt.n),
                     (tH, tH.n), (tHd, tHd.n)]
            DST_END = (dSt, dSt.n)
            stD.close()
            stE = ExitStack()
            wgslab = [sb(stE, f"wgslab{i}", [128, D], BF16) for i in range(2)]
            bcslab = sb(stE, "bcslab", [128, D], BF16)
            d1sb = [sb(stE, f"d1sb{i}", [128, DC * 128], BF16) for i in range(2)]
            d2sb = sb(stE, "d2sb", [128, 4096], BF16)
            blockbuf = sb(stE, "blockbuf", [128, DC * 512], BF16)
            gts = [sb(stE, f"gts{i}", [128, 512], BF16) for i in range(2)]
            out2tm = sb(stE, "out2tm", [128, D], BF16)
            n_tm = sb(stE, "n_tm", [128, D], BF16)
            x1b = [sb(stE, f"x1b{i}", [128, D], F32) for i in range(2)]
            x2sb = sb(stE, "x2sb", [128, D], F32)
            outgB_sb = sb(stE, "outgB_sb", [128, D], F32)
            outbB_sb = sb(stE, "outbB_sb", [128, D], F32)
            lnstE = sb(stE, "lnstE", [128, 8], F32)
            psE = [ps(stE, f"psE{i}", [128, 512], F32) for i in range(2)]
            psT2 = [ps(stE, f"psT2{i}", [128, 128], BF16) for i in range(2)]
            psd1 = ps(stE, "psd1", [128, 4], F32)
            psd2 = ps(stE, "psd2", [2, 512], F32)
            psRTb2 = ps(stE, "psRTb2", [128, 2], BF16)
            psRow2 = ps(stE, "psRow2", [2, 128], BF16)
            MUE = lnstE[:, 0:1]; M2E = lnstE[:, 1:2]
            VARE = lnstE[:, 2:3]; RSTDE = lnstE[:, 3:4]

            t_outg = dma("sync", eLd, D_END,
                         lambda q: q.e.dma_start(out=outgB_sb[:, :], in_=outgB))
            t_outb2 = dma("sync", eLd, D_END,
                          lambda q: q.e.dma_start(out=outbB_sb[:, :], in_=outbB))
            t_d2brow = dma("sync", eLd, D_END + [(dPe, dPe.n)],
                           lambda q: q.e.dma_start(out=biasrow[:, :], in_=d2b_row))

            if upto == "E0":
                def finE0(q):
                    q.wait((ySt, ySt.n), (cc, cc.n), (tHd, tHd.n), (eLd, eLd.n))
                em("sync", finE0)
                em("gpsimd", finE0)
                stE.close()
                _a2a_ctx.__exit__(None, None, None)
                return nc

            th_d1mm = {}
            th_gel = {}
            th_d2done = {}
            th_prev = TH0
            NSTEP = {"E1a": 1, "E1b": 2, "E1c": 3}.get(upto, 4)
            for step in range(NSTEP):
                t_bf = row_ln(th_tm[:, :], png2_sb, pnb2_sb, [th_prev])
                t_tT = row_T(rowbf, thT, DC, t_bf, psRTb2,
                             first_extra=(D_END if step == 0 else ()))
                for oc in range(8):
                    wp = oc % 2
                    t_wl = dma("sync", eLd,
                               [cast_toks["d1"], th_d1mm.get((step, oc - 2)),
                                th_d1mm.get((step - 1, oc + 6))],
                               lambda q, oc=oc, wp=wp: q.e.dma_start(
                                   out=d1sb[wp][:, :],
                                   in_=d1w3[:, 2048 * oc:2048 * (oc + 1)]))

                    def dmm(q, oc=oc, wp=wp, t_tT=t_tT, t_wl=t_wl, step=step):
                        q.wait(t_tT, t_wl, th_gel.get((step, oc - 1)),
                               th_gel.get((step - 1, 7)))
                        for dc in range(DC):
                            inst = q.e.matmul(
                                psd1[:, 0:2],
                                d1sb[wp][:, 128 * dc:128 * (dc + 1)],
                                thT[:, 2 * dc:2 * (dc + 1)],
                                start=(dc == 0), stop=(dc == DC - 1))
                        dmm.tok = tok(tH, inst)
                    em("tensor", dmm)
                    th_d1mm[(step, oc)] = dmm.tok
                    th_gel[(step, oc)] = one("scalar", tH, [dmm.tok],
                                             lambda q, oc=oc: q.e.activation(
                                                 gT[:, 2 * oc:2 * (oc + 1)],
                                                 psd1[:, 0:2],
                                                 AF.Gelu, bias=d1br_sb[:, oc:oc + 1]))
                gel_all = [th_gel[(step, oc)] for oc in range(8)]
                prev_ev = None
                ev4 = []
                prev_mm = th_d2done.get(step - 1)
                for ct in range(4):
                    t_d2w = dma("sync", eLd,
                                [cast_toks["d2"], prev_mm, t_d2brow],
                                lambda q, ct=ct: q.e.dma_start(
                                    out=d2sb[:, :],
                                    in_=d2w3[:, 4096 * ct:4096 * (ct + 1)]))

                    def d2mm(q, ct=ct, prev_ev=prev_ev, gel_all=gel_all, t_d2w=t_d2w):
                        q.wait(t_d2w, prev_ev, *gel_all)
                        q.e.matmul(psd2[:, :], onesz_sb[:, :],
                                   biasrow[:, 512 * ct:512 * (ct + 1)],
                                   start=True, stop=False)
                        for oc in range(8):
                            inst = q.e.matmul(
                                psd2[:, :], gT[:, 2 * oc:2 * (oc + 1)],
                                d2sb[:, 512 * oc:512 * (oc + 1)],
                                start=False, stop=(oc == 7))
                        d2mm.tok = tok(tH, inst)
                    em("tensor", d2mm)
                    prev_mm = d2mm.tok
                    prev_ev = one("scalar", tH, [d2mm.tok],
                                  lambda q, ct=ct: q.e.copy(
                                      scrA[:, 512 * ct:512 * (ct + 1)], psd2[:, :]))
                    ev4.append(prev_ev)
                th_d2done[step] = ev4[-1]
                t_st = dma("gpsimd", tHd, ev4,
                           lambda q, step=step: q.e.dma_start(
                               out=ar_in[1 + step][:, :], in_=scrA[:, :]))
                t_cc = coll([t_st], lambda q, step=step: q.e.collective_compute(
                               "AllReduce", ALU.add, replica_groups=G8,
                               ins=[ar_in[1 + step].ap().opt()],
                               outs=[ar_out[1 + step].ap().opt()]))
                t_ld = dma("gpsimd", tHd, [t_cc],
                           lambda q, step=step: q.e.dma_start(
                               out=scrA[:, :], in_=ar_out[1 + step][:, :]))
                th_prev = one("vector", tH, [t_ld],
                              lambda q: q.e.tensor_tensor(th_tm[:, :], th_tm[:, :],
                                                          scrA[:, :], ALU.add))

            if "dbg_th" in dbg_out:
                one("sync", ySt, [th_prev],
                    lambda q: q.e.dma_start(out=dbg_out["dbg_th"], in_=th_tm[:, :]), k=16)

            if upto in ("E1", "E1a", "E1b", "E1c"):
                def finE1(q):
                    q.wait((ySt, ySt.n), (cc, cc.n), (tHd, tHd.n), (eLd, eLd.n),
                           (tH, tH.n))
                em("sync", finE1)
                em("gpsimd", finE1)
                stE.close()
                _a2a_ctx.__exit__(None, None, None)
                return nc

            t_bf2 = one("vector", tH, [th_prev],
                        lambda q: q.e.tensor_copy(rowbf[:, :], th_tm[:, :]))
            t_tT2 = row_T(rowbf, thT, DC, t_bf2, psRTb2)
            bc_done = []
            last_tok = t_tT2
            for oc in range(2):
                t_bcw = dma("sync", eLd, [cast_toks["bc"], last_tok],
                            lambda q, oc=oc: q.e.dma_start(
                                out=bcslab[:, :], in_=bcw3[:, D * oc:D * (oc + 1)]))

                def bmm(q, oc=oc, last_tok=last_tok, t_bcw=t_bcw):
                    q.wait(last_tok, t_bcw)
                    for dc in range(DC):
                        inst = q.e.matmul(
                            psd1[:, 0:2],
                            bcslab[:, 128 * dc:128 * (dc + 1)],
                            thT[:, 2 * dc:2 * (dc + 1)],
                            start=(dc == 0), stop=(dc == DC - 1))
                    bmm.tok = tok(tH, inst)
                em("tensor", bmm)
                t_ev = one("scalar", tH, [bmm.tok],
                           lambda q, oc=oc: q.e.activation(
                               thTf[:, 2 * oc:2 * (oc + 1)], psd1[:, 0:2], AF.Identity,
                               bias=bcbr_sb[:, oc:oc + 1]))
                t_bt = one("tensor", tH, [t_ev],
                           lambda q, oc=oc: q.e.transpose(
                               psRow2[:, :], thTf[:, 2 * oc:2 * (oc + 1)], ident_sb[:, :]))
                t_cp2 = one("scalar", tH, [t_bt],
                            lambda q, oc=oc: q.e.copy(scrA[0:2, 128 * oc:128 * (oc + 1)],
                                                      psRow2[:, :]))
                last_tok = t_cp2
                bc_done.append(t_cp2)
            t_bcst = dma("gpsimd", tHd, bc_done,
                         lambda q: q.e.dma_start(out=ag_in[1][:, :],
                                                 in_=scrA[0:2, 0:256]))
            t_bccc = coll([t_bcst], lambda q: q.e.collective_compute(
                             "AllGather", ALU.bypass, replica_groups=G8,
                             ins=[ag_in[1].ap().opt()], outs=[ag_out[1].ap().opt()]))
            t_bcld = None
            for r in range(NCORE):
                t_bcld = dma("gpsimd", tHd, [t_bccc],
                             lambda q, r=r: q.e.dma_start(
                                 out=scrA[:, 256 * r:256 * (r + 1)],
                                 in_=ag_out[1][2 * r:2 * r + 2, :]))
            t_bcbf = one("vector", tH, [t_bcld],
                         lambda q: q.e.tensor_copy(rowbf[:, :], scrA[:, :]))
            last_tok = t_bcbf
            for dc in range(DC):
                t = one("tensor", tH, [last_tok],
                        lambda q, dc=dc: q.e.transpose(
                            psRTb2[:, :], rowbf[:, 128 * dc:128 * (dc + 1)],
                            ident_sb[0:2, 0:2]))
                last_tok = one("scalar", tH, [t],
                               lambda q, dc=dc: q.e.copy(bcT[:, 2 * dc:2 * dc + 2],
                                                         psRTb2[:, :]))
            BCT = last_tok

            if "dbg_bc" in dbg_out:
                one("sync", ySt, [BCT],
                    lambda q: q.e.dma_start(out=dbg_out["dbg_bc"], in_=scrA[:, :]), k=16)

            if upto == "E2":
                def finE2(q):
                    q.wait((ySt, ySt.n), (cc, cc.n), (tHd, tHd.n), (eLd, eLd.n),
                           (tH, tH.n))
                em("sync", finE2)
                em("gpsimd", finE2)
                stE.close()
                _a2a_ctx.__exit__(None, None, None)
                return nc

            e_pe = []
            e_sg = []
            e_ts = []
            e_tr_cp = []
            x1_free = {}
            x2_free = {}
            lnE_state = {"last": None, "sqfree": None}
            prev_tb_tr = {"t": None}
            gchunk = 0
            NTB = {"G1": 1, "G2": 2, "G3": 3}.get(upto, 4)
            for tb in range(NTB):
                b = tb // 2
                for cci in range(DC):
                    idx = len(e_pe)
                    bp = idx % 2
                    wp = idx % 2
                    t_wg = dma("sync", eLd,
                               [cast_toks["wg"], DST_END,
                                e_pe[idx - 2] if idx >= 2 else None],
                               lambda q, cci=cci, wp=wp: q.e.dma_start(
                                   out=wgslab[wp][:, :],
                                   in_=wg2[:, D * cci:D * (cci + 1)]))

                    def epe(q, tb=tb, cci=cci, bp=bp, idx=idx, wp=wp, t_wg=t_wg):
                        q.wait(t_wg, D_TR_END, LD0,
                               e_sg[idx - 2] if idx >= 2 else None)
                        for dc in range(DC):
                            inst = q.e.matmul(
                                psE[bp][:, :],
                                wgslab[wp][:, 128 * dc:128 * (dc + 1)],
                                xn2T[:, TLOC * dc + 512 * tb: TLOC * dc + 512 * (tb + 1)],
                                start=(dc == 0), stop=(dc == DC - 1))
                        epe.tok = tok(ePe, inst)
                    em("tensor", epe)
                    e_pe.append(epe.tok)
                    sp = idx % 2
                    e_sg.append(one("scalar", eAc,
                                    [e_pe[idx], e_ts[idx - 2] if idx >= 2 else None],
                                    lambda q, bp=bp, sp=sp, cci=cci: q.e.activation(
                                        gts[sp][:, :], psE[bp][:, :], AF.Sigmoid,
                                        bias=gatebr_sb[:, cci:cci + 1])))
                    t_t1 = one("vector", eDv, [e_sg[idx], BCT],
                               lambda q, sp=sp, cci=cci, b=b: q.e.tensor_scalar(
                                   gts[sp][:, :], gts[sp][:, :],
                                   bcT[:, 2 * cci + b:2 * cci + b + 1], 0.0,
                                   ALU.mult, ALU.add))
                    e_ts.append(one("vector", eDv, [t_t1, prev_tb_tr["t"]],
                                    lambda q, sp=sp, cci=cci, tb=tb: q.e.tensor_tensor(
                                        blockbuf[:, 512 * cci:512 * (cci + 1)],
                                        xn2T[:, TLOC * cci + 512 * tb: TLOC * cci + 512 * (tb + 1)],
                                        gts[sp][:, :], ALU.add)))
                blk_done = e_ts[-DC:]

                last_tr_this_tb = None
                for i4 in range(4):
                    row0 = 512 * tb + 128 * i4
                    for dc in range(DC):
                        pp = dc % 2
                        idx2 = gchunk * DC + dc
                        t = one("tensor", ePe,
                                blk_done + [e_tr_cp[idx2 - 2] if idx2 >= 2 else None],
                                lambda q, dc=dc, pp=pp, i4=i4: q.e.transpose(
                                    psT2[pp][:, :],
                                    blockbuf[:, 512 * dc + 128 * i4: 512 * dc + 128 * (i4 + 1)],
                                    ident_sb[:, :]))
                        last_tr_this_tb = t
                        e_tr_cp.append(one("scalar", eAc,
                                           [t, lnE_state["last"] if dc == 0 else None],
                                           lambda q, dc=dc, pp=pp: q.e.copy(
                                               out2tm[:, 128 * dc:128 * (dc + 1)],
                                               psT2[pp][:, :])))
                    t_evall = e_tr_cp[-1]
                    t_nm2 = emit_ln(out2tm[:, :], n_tm[:, :], D, EPS2, [t_evall],
                                    n_tm[:, :], lnE_state["last"],
                                    eDv, eAc, MUE, M2E, VARE, RSTDE)
                    lnE_state["last"] = t_nm2
                    xp = gchunk % 2
                    t_x1 = dma("sync", eLd, [DST_END, x1_free.get(gchunk - 2)],
                               lambda q, xp=xp, row0=row0: q.e.dma_start(
                                   out=x1b[xp][:, :], in_=x1s[row0:row0 + 128, :]))
                    t_g1 = one("vector", eDv, [t_nm2, x2_free.get(gchunk - 1), t_outg],
                               lambda q: q.e.tensor_tensor(
                                   x2sb[:, :], n_tm[:, :], outgB_sb[:, :], ALU.mult))
                    t_g1b = one("vector", eDv, [t_g1, t_outb2],
                                lambda q: q.e.tensor_tensor(
                                    x2sb[:, :], x2sb[:, :], outbB_sb[:, :], ALU.add))
                    t_g2 = one("vector", eDv, [t_g1b, t_x1],
                               lambda q, xp=xp: q.e.tensor_tensor(
                                   x2sb[:, :], x2sb[:, :], x1b[xp][:, :], ALU.add))
                    x1_free[gchunk] = t_g2
                    x2_free[gchunk] = dma("sync", ySt, [t_g2],
                                          lambda q, row0=row0: q.e.dma_start(
                                              out=y_loc[row0:row0 + 128, :],
                                              in_=x2sb[:, :]))
                    gchunk += 1
                prev_tb_tr["t"] = last_tr_this_tb

            def fin(q):
                q.wait((ySt, ySt.n), (cc, cc.n), (dSt, dSt.n), (tHd, tHd.n),
                       (kst, kst.n), (cOs, cOs.n), (eLd, eLd.n), (eDv, eDv.n),
                       (eAc, eAc.n), (ePe, ePe.n))
            em("sync", fin)
            em("gpsimd", fin)
            stE.close()

    _a2a_ctx.__exit__(None, None, None)
    return nc


def _prep_inputs(inputs, c):
    import ml_dtypes
    bf = ml_dtypes.bfloat16
    X = np.asarray(inputs["L0_attn_x"])
    g = {}
    g["x_loc"] = np.ascontiguousarray(
        np.concatenate([X[0, 1024 * c:1024 * (c + 1), :],
                        X[1, 1024 * c:1024 * (c + 1), :]], axis=0)).astype(np.float32)
    g["wq"] = np.ascontiguousarray(inputs["L0_attn_Wq"], dtype=np.float32)
    g["wk"] = np.ascontiguousarray(inputs["L0_attn_Wk"], dtype=np.float32)
    g["wv"] = np.ascontiguousarray(inputs["L0_attn_Wv"], dtype=np.float32)
    g["wo"] = np.ascontiguousarray(inputs["L0_attn_Wo"], dtype=np.float32)
    g["wg"] = np.ascontiguousarray(inputs["L0_ffn_gate_w"], dtype=np.float32)

    def resh(v, n):
        return np.ascontiguousarray(np.asarray(v, dtype=np.float32).reshape(n, 128).T)

    g["bqr"] = resh(inputs["L0_attn_bq"], DC)
    g["bkr"] = resh(inputs["L0_attn_bk"], DC)
    g["bvr"] = resh(inputs["L0_attn_bv"], DC)
    g["bo_row"] = np.asarray(inputs["L0_attn_bo"], dtype=np.float32).reshape(1, D)
    g["gatebr"] = resh(inputs["L0_ffn_gate_b"], DC)
    for nm, key in [("ing2", "L0_ffn_in_gamma"), ("inb2", "L0_ffn_in_beta"),
                    ("png2", "L0_ffn_pn_gamma"), ("pnb2", "L0_ffn_pn_beta")]:
        v = np.asarray(inputs[key], dtype=np.float32)
        g[nm] = np.ascontiguousarray(np.stack([v, v], axis=0))
    g["outgB"] = np.ascontiguousarray(np.broadcast_to(
        np.asarray(inputs["L0_ffn_out_gamma"], dtype=np.float32), (128, D)))
    g["outbB"] = np.ascontiguousarray(np.broadcast_to(
        np.asarray(inputs["L0_ffn_out_beta"], dtype=np.float32), (128, D)))
    g["d1w_s"] = np.ascontiguousarray(
        np.asarray(inputs["L0_ffn_d1_w"], dtype=np.float32)[:, 1024 * c:1024 * (c + 1)])
    g["d1br"] = resh(np.asarray(inputs["L0_ffn_d1_b"],
                                dtype=np.float32)[1024 * c:1024 * (c + 1)], 8)
    g["d2w_s"] = np.ascontiguousarray(
        np.asarray(inputs["L0_ffn_d2_w"], dtype=np.float32)[1024 * c:1024 * (c + 1), :])
    g["d2b_row"] = np.asarray(inputs["L0_ffn_d2_b"], dtype=np.float32).reshape(1, D)
    g["aggw_s"] = np.ascontiguousarray(
        np.asarray(inputs["L0_ffn_agg_w"], dtype=np.float32)[:, 256 * c:256 * (c + 1)])
    g["aggbr"] = resh(np.asarray(inputs["L0_ffn_agg_b"],
                                 dtype=np.float32)[256 * c:256 * (c + 1)], 2)
    g["bcw_s"] = np.ascontiguousarray(
        np.asarray(inputs["L0_ffn_bc_w"], dtype=np.float32)[:, 256 * c:256 * (c + 1)])
    g["bcbr"] = resh(np.asarray(inputs["L0_ffn_bc_b"],
                                dtype=np.float32)[256 * c:256 * (c + 1)], 2)
    g["ident_bf"] = np.eye(128, dtype=np.float32).astype(bf)
    g["band_bf"] = np.triu(np.ones((128, 128), dtype=np.float32)).astype(bf)
    g["allones_bf"] = np.ones((128, 128), dtype=np.float32).astype(bf)
    g["ones_f32_row"] = np.ones((1, 128), dtype=np.float32)
    g["ones_or_zero"] = np.full((1, 2), 1.0 if c == 0 else 0.0, dtype=np.float32)
    g["is_last"] = np.full((2, 1), 1.0 if c == 7 else 0.0, dtype=np.float32)
    g["cshift_col"] = np.full((128, 1), CSHIFT, dtype=np.float32)
    return g


def kernel(**inputs):
    nc = build()
    in_maps = [_prep_inputs(inputs, c) for c in range(NCORE)]
    res = run_bass_kernel_spmd(nc, in_maps, core_ids=list(range(NCORE)))
    out = np.empty((2, T, D), dtype=np.float32)
    for c in range(NCORE):
        y = res.results[c]["y_loc"]
        out[0, 1024 * c:1024 * (c + 1), :] = y[0:1024, :]
        out[1, 1024 * c:1024 * (c + 1), :] = y[1024:2048, :]
    return out


# version 04
# speedup vs baseline: 1.0069x; 1.0069x over previous
import numpy as np
import concourse.bass as bass
import concourse.mybir as mybir
from concourse.bass_utils import run_bass_kernel_spmd
from contextlib import ExitStack
from concourse.aot_env import allow_ici_all_to_all

F32 = mybir.dt.float32
BF16 = mybir.dt.bfloat16
AF = mybir.ActivationFunctionType
ALU = mybir.AluOpType
AX = mybir.AxisListType

D = 2048
DC = 16
TB = 1024
TLOC = 2048
T = 8192
NCORE = 8
NQT = 16
SCALE = float(1.0 / np.sqrt(128.0))
CSHIFT = -12.0
EPS1 = 1e-5
EPS2 = 1e-6
G8 = [[0, 1, 2, 3, 4, 5, 6, 7]]
SEG_OF_TS = [0, 2, 1, 3]
PAIR_OF_SEG = [(0, 0), (0, 1), (1, 0), (1, 1)]


class Sem:
    __slots__ = ("h", "n", "name")

    def __init__(self, h, name):
        self.h = h
        self.n = 0
        self.name = name


class Q:
    def __init__(self, eng):
        self.e = eng
        self.seen = {}

    def wait(self, *toks):
        for t in toks:
            if t is None:
                continue
            s, v = t
            if v > 0 and self.seen.get(s.name, 0) < v:
                self.e.wait_ge(s.h, v)
                self.seen[s.name] = v


def build(dbg=(), upto=None, noa2a=False):
    dbg = set(dbg)
    _a2a_ctx = allow_ici_all_to_all()
    _a2a_ctx.__enter__()
    nc = bass.Bass("TRN2", target_bir_lowering=False, debug=False)

    def din(name, shape, dtype=F32):
        return nc.dram_tensor(name, shape, dtype, kind="ExternalInput").ap()

    x_loc = din("x_loc", [TLOC, D])
    wq = din("wq", [D, D]); wk = din("wk", [D, D]); wv = din("wv", [D, D])
    wo = din("wo", [D, D]); wg = din("wg", [D, D])
    bqr = din("bqr", [128, DC]); bkr = din("bkr", [128, DC]); bvr = din("bvr", [128, DC])
    bo_row = din("bo_row", [1, D])
    gatebr = din("gatebr", [128, DC])
    ing2 = din("ing2", [2, D]); inb2 = din("inb2", [2, D])
    png2 = din("png2", [2, D]); pnb2 = din("pnb2", [2, D])
    outgB = din("outgB", [128, D]); outbB = din("outbB", [128, D])
    d1w_s = din("d1w_s", [D, 1024]); d1br = din("d1br", [128, 8])
    d2w_s = din("d2w_s", [1024, D]); d2b_row = din("d2b_row", [1, D])
    aggw_s = din("aggw_s", [D, 256]); aggbr = din("aggbr", [128, 2])
    bcw_s = din("bcw_s", [D, 256]); bcbr = din("bcbr", [128, 2])
    ident_bf = din("ident_bf", [128, 128], BF16)
    band_bf = din("band_bf", [128, 128], BF16)
    allones_bf = din("allones_bf", [128, 128], BF16)
    ones_f32_row = din("ones_f32_row", [1, 128])
    ones_or_zero = din("ones_or_zero", [1, 2])
    is_last = din("is_last", [2, 1])
    cshift_col = din("cshift_col", [128, 1])

    y_loc = nc.dram_tensor("y_loc", [TLOC, D], F32, kind="ExternalOutput").ap()

    dbg_specs = {
        "dbg_xn1T": ([128, DC * TLOC], BF16),
        "dbg_kout": ([D, TLOC], BF16),
        "dbg_vout": ([D, TLOC], BF16),
        "dbg_qt": ([128, 4 * T], BF16),
        "dbg_oout": ([NCORE * 128, TB], BF16),
        "dbg_x1": ([TLOC, D], F32),
        "dbg_vin": ([D, TLOC], BF16),
        "dbg_kin": ([D, TLOC], BF16),
        "dbg_qt2": ([128, 4 * T], BF16),
        "dbg_kout2": ([D, TLOC], BF16),
        "dbg_vout2": ([D, TLOC], BF16),
        "dbg_th": ([2, D], F32),
        "dbg_bc": ([2, D], F32),
    }
    dbg_out = {k: nc.dram_tensor(k, *dbg_specs[k], kind="ExternalOutput").ap()
               for k in dbg if k in dbg_specs}

    def dsc(name, shape, dtype):
        return nc.dram_tensor(name, shape, dtype)

    wq3 = dsc("wq3", [128, DC * D], BF16)
    wk3 = dsc("wk3", [128, DC * D], BF16)
    wv3 = dsc("wv3", [128, DC * D], BF16)
    wo2 = dsc("wo2", [128, DC * D], BF16)
    wg2 = dsc("wg2", [128, DC * D], BF16)
    d1w3 = dsc("d1w3", [128, 8 * 16 * 128], BF16)
    d2w3 = dsc("d2w3", [128, 8 * D], BF16)
    aggw3 = dsc("aggw3", [128, DC * 256], BF16)
    bcw3 = dsc("bcw3", [128, DC * 256], BF16)
    k_in = dsc("k_in", [D, TLOC], BF16)
    v_in = dsc("v_in", [D, TLOC], BF16)
    k_out = dsc("k_out", [D, TLOC], BF16)
    v_out = dsc("v_out", [D, TLOC], BF16)
    o_in = [dsc(f"o_in{s}", [NCORE * 128, TB], BF16) for s in range(4)]
    o_out = [dsc(f"o_out{s}", [NCORE * 128, TB], BF16) for s in range(4)]
    x1s = dsc("x1s", [TLOC, D], F32)
    ag_in = [dsc(f"ag_in{i}", [2, 256], F32) for i in range(2)]
    ag_out = [dsc(f"ag_out{i}", [16, 256], F32) for i in range(2)]
    ar_in = [dsc(f"ar_in{i}", [2, D], F32) for i in range(5)]
    ar_out = [dsc(f"ar_out{i}", [2, D], F32) for i in range(5)]

    st = ExitStack()
    with st:
        def sb(stk, name, shape, dtype):
            return stk.enter_context(nc.sbuf_tensor(name, shape, dtype))

        def ps(stk, name, shape, dtype):
            return stk.enter_context(nc.psum_tensor(name, shape, dtype))

        bqr_sb = sb(st, "bqr_sb", [128, DC], F32)
        bkr_sb = sb(st, "bkr_sb", [128, DC], F32)
        bvr_sb = sb(st, "bvr_sb", [128, DC], F32)
        gatebr_sb = sb(st, "gatebr_sb", [128, DC], F32)
        d1br_sb = sb(st, "d1br_sb", [128, 8], F32)
        aggbr_sb = sb(st, "aggbr_sb", [128, 2], F32)
        bcbr_sb = sb(st, "bcbr_sb", [128, 2], F32)
        ident_sb = sb(st, "ident_sb", [128, 128], BF16)
        band_sb = sb(st, "band_sb", [128, 128], BF16)
        allones_sb = sb(st, "allones_sb", [128, 128], BF16)
        onesr_sb = sb(st, "onesr_sb", [1, 128], F32)
        onesz_sb = sb(st, "onesz_sb", [1, 2], F32)
        islast_sb = sb(st, "islast_sb", [2, 1], F32)
        cshift_sb = sb(st, "cshift_sb", [128, 1], F32)
        thT = sb(st, "thT", [128, DC * 2], BF16)
        thTf = sb(st, "thTf", [128, 4], BF16)
        gT = sb(st, "gT", [128, 8 * 2], BF16)
        bcT = sb(st, "bcT", [128, DC * 2], F32)
        lnrow = sb(st, "lnrow", [2, 8], F32)
        th_tm = sb(st, "th_tm", [2, D], F32)
        scrA = sb(st, "scrA", [2, D], F32)
        biasrow = sb(st, "biasrow", [1, D], F32)
        rowbf = sb(st, "rowbf", [2, D], BF16)
        ing2_sb = sb(st, "ing2_sb", [2, D], BF16)
        inb2_sb = sb(st, "inb2_sb", [2, D], BF16)
        png2_sb = sb(st, "png2_sb", [2, D], BF16)
        pnb2_sb = sb(st, "pnb2_sb", [2, D], BF16)
        arena1 = sb(st, "arena1", [128, 4 * T], BF16)
        qt = arena1
        xn2T = arena1

        def sem(name):
            return Sem(st.enter_context(nc.semaphore(name)), name)

        ld0 = sem("ld0"); cast = sem("cast")
        xA = sem("xA"); vA = sem("vA"); sA = sem("sA"); pA = sem("pA")
        wB = sem("wB"); pB = sem("pB"); aB = sem("aB"); kst = sem("kst")
        cc = sem("cc")
        kvC = sem("kvC"); pvt = sem("pvt"); avt = sem("avt")
        cS = sem("cS"); cE = sem("cE"); cM = sem("cM"); cP = sem("cP")
        cL = sem("cL"); cLc = sem("cLc"); cLr = sem("cLr"); cNm = sem("cNm")
        cOs = sem("cOs")
        dLd = sem("dLd"); dPe = sem("dPe"); dDv = sem("dDv"); dAc = sem("dAc")
        dSt = sem("dSt")
        tH = sem("tH"); tHd = sem("tHd")
        eLd = sem("eLd"); ePe = sem("ePe"); eAc = sem("eAc"); eDv = sem("eDv")
        ySt = sem("ySt")
        DMA_SEMS = [ld0, cast, xA, wB, kst, kvC, cOs, dLd, dSt, tHd, eLd, ySt]

        with nc.Block() as block:
            wrap = {}

            def em(engname, fn):
                def body(eng):
                    q = wrap.get(engname)
                    if q is None:
                        q = wrap[engname] = Q(eng)
                    fn(q)
                getattr(block, engname)(body)

            def tok(s, inst, k=1):
                inst.then_inc(s.h, k)
                s.n += k
                return (s, s.n)

            def one(engname, s, deps, buildfn, k=1):
                out = {}

                def f(q):
                    q.wait(*deps)
                    out["t"] = tok(s, buildfn(q), k)
                em(engname, f)
                return out["t"]

            last_cc = {"t": None}

            def ccdep():
                return [(s, s.n) for s in DMA_SEMS]

            def dma(engname, s, deps, buildfn):
                return one(engname, s, list(deps) + [last_cc["t"]], buildfn, k=16)

            def coll(deps, buildfn):
                t = one("gpsimd", cc, list(deps) + ccdep() + [last_cc["t"]], buildfn)
                last_cc["t"] = t
                return t

            def p0_sync(q):
                for dst, src in [
                    (bqr_sb, bqr), (bkr_sb, bkr), (bvr_sb, bvr),
                    (gatebr_sb, gatebr),
                    (biasrow, bo_row), (d1br_sb, d1br),
                    (aggbr_sb, aggbr), (bcbr_sb, bcbr),
                    (ident_sb, ident_bf), (band_sb, band_bf),
                    (allones_sb, allones_bf), (onesr_sb, ones_f32_row),
                    (onesz_sb, ones_or_zero), (islast_sb, is_last),
                    (cshift_sb, cshift_col),
                ]:
                    tok(ld0, q.e.dma_start(out=dst[:, :], in_=src), 16)
            em("sync", p0_sync)
            LD0 = (ld0, ld0.n)

            cast_toks = {}

            def p0_g(q):
                for dst, srcp in [(ing2_sb, ing2), (inb2_sb, inb2),
                                  (png2_sb, png2), (pnb2_sb, pnb2)]:
                    tok(ld0, q.e.dma_start(out=dst[:, :], in_=srcp), 16)
                for nm, w3, w in [("wk", wk3, wk), ("wv", wv3, wv), ("wq", wq3, wq)]:
                    w3v = w3.rearrange("p (f c o) -> p f c o", c=DC, o=128)
                    for cc_ in range(DC):
                        t = tok(cast, q.e.dma_start(
                            out=w3v[:, :, cc_, :],
                            in_=w[128 * cc_:128 * (cc_ + 1), :].rearrange(
                                "p (f o) -> p f o", o=128)), 16)
                    cast_toks[nm] = t
            em("gpsimd", p0_g)

            def late_casts(q):
                q.wait(last_cc["t"])
                for hp in range(DC):
                    t = tok(cast, q.e.dma_start(
                        out=wo2[:, D * hp:D * (hp + 1)],
                        in_=wo[128 * hp:128 * (hp + 1), :]), 16)
                cast_toks["wo"] = t
                wg2v = wg2.rearrange("p (f c o) -> p f c o", c=DC, o=128)
                for dc_ in range(DC):
                    t = tok(cast, q.e.dma_start(
                        out=wg2v[:, :, dc_, :],
                        in_=wg[128 * dc_:128 * (dc_ + 1), :].rearrange(
                            "p (f o) -> p f o", o=128)), 16)
                cast_toks["wg"] = t
                for nm, w3, w in [("agg", aggw3, aggw_s), ("bc", bcw3, bcw_s)]:
                    w3v = w3.rearrange("p (g c o) -> p g c o", c=DC, o=128)
                    for dc_ in range(DC):
                        t = tok(cast, q.e.dma_start(
                            out=w3v[:, :, dc_, :],
                            in_=w[128 * dc_:128 * (dc_ + 1), :].rearrange(
                                "p (g o) -> p g o", o=128)), 16)
                    cast_toks[nm] = t
                d1v = d1w3.rearrange("p (g c o) -> p g c o", c=DC, o=128)
                for cc_ in range(DC):
                    t = tok(cast, q.e.dma_start(
                        out=d1v[:, :, cc_, :],
                        in_=d1w_s[128 * cc_:128 * (cc_ + 1), :].rearrange(
                            "p (g o) -> p g o", o=128)), 16)
                cast_toks["d1"] = t
                d2v = d2w3.rearrange("p (t g o) -> p t g o", g=8, o=512)
                for oc_ in range(8):
                    t = tok(cast, q.e.dma_start(
                        out=d2v[:, :, oc_, :],
                        in_=d2w_s[128 * oc_:128 * (oc_ + 1), :].rearrange(
                            "p (t o) -> p t o", o=512)), 16)
                cast_toks["d2"] = t

            def emit_ln(src, dst, W, eps, deps, sq_buf, serial_prev, v_sem, s_sem,
                        mu, m2, var, rstd):
                t1 = one("vector", v_sem, list(deps) + [serial_prev],
                         lambda q: q.e.reduce_sum(mu, src, axis=AX.X))
                t2s = one("vector", v_sem, list(deps) + [serial_prev, t1],
                          lambda q: q.e.tensor_tensor(sq_buf, src, src, ALU.mult))
                t2 = one("vector", v_sem, [t2s],
                         lambda q: q.e.reduce_sum(m2, sq_buf, axis=AX.X))
                t3 = one("vector", v_sem, [t2],
                         lambda q: q.e.tensor_scalar(mu, mu, 1.0 / W, 0.0, ALU.mult, ALU.add))
                t4 = one("vector", v_sem, [t3],
                         lambda q: q.e.tensor_scalar(m2, m2, 1.0 / W, 0.0, ALU.mult, ALU.add))
                t5 = one("vector", v_sem, [t4],
                         lambda q: q.e.tensor_tensor(var, mu, mu, ALU.mult))
                t7 = one("vector", v_sem, [t5],
                         lambda q: q.e.tensor_tensor(var, m2, var, ALU.subtract))
                t8 = one("vector", v_sem, [t7],
                         lambda q: q.e.tensor_scalar(var, var, eps, 0.0, ALU.add, ALU.add))
                t9 = one("vector", v_sem, [t8], lambda q: q.e.reciprocal(rstd, var))
                t10 = one("scalar", s_sem, [t9], lambda q: q.e.activation(rstd, rstd, AF.Sqrt))
                return one("vector", v_sem, [t10],
                           lambda q: q.e.tensor_scalar(dst, src, mu[:, 0:1], rstd[:, 0:1],
                                                       ALU.subtract, ALU.mult))

            _sc = nc.enter_named_scope("phAB", False)
            stAB = ExitStack()
            xn1T = sb(stAB, "xn1T", [128, DC * TLOC], BF16)
            xsbA = sb(stAB, "xsbA", [128, D], F32)
            xn1s = [sb(stAB, f"xn1s{i}", [128, D], BF16) for i in range(2)]
            lnst = sb(stAB, "lnst", [128, 8], F32)
            wstg = [sb(stAB, f"wstg{i}", [128, DC * 128], BF16) for i in range(2)]
            kvstg = [sb(stAB, f"kvstg{i}", [128, 512], BF16) for i in range(2)]
            psA = [ps(stAB, f"psA{i}", [128, 128], BF16) for i in range(2)]
            psB = [ps(stAB, f"psB{i}", [128, 512], F32) for i in range(2)]
            MU = lnst[:, 0:1]; M2 = lnst[:, 1:2]; VAR = lnst[:, 2:3]; RSTD = lnst[:, 3:4]

            a_fin = [None] * 16
            a_tr = []
            a_cp = []
            for g in range(16):
                p = g % 2
                tx = one("sync", xA, [a_fin[g - 1] if g >= 1 else None],
                         lambda q, g=g: q.e.dma_start(
                             out=xsbA[:, :], in_=x_loc[128 * g:128 * (g + 1), :]), k=16)
                a_fin[g] = emit_ln(xsbA[:, :], xn1s[p][:, :], D, EPS1,
                                   [tx, LD0], xn1s[p][:, :],
                                   a_fin[g - 1] if g >= 1 else None,
                                   vA, sA, MU, M2, VAR, RSTD)
                for dc in range(DC):
                    pp = dc % 2
                    idx = g * DC + dc
                    t = one("tensor", pA,
                            [a_fin[g], a_cp[idx - 2] if idx >= 2 else LD0],
                            lambda q, p=p, dc=dc, pp=pp: q.e.transpose(
                                psA[pp][:, :], xn1s[p][:, 128 * dc:128 * (dc + 1)],
                                ident_sb[:, :]))
                    a_tr.append(t)
                    a_cp.append(one("scalar", sA, [t],
                                    lambda q, g=g, dc=dc, pp=pp: q.e.copy(
                                        xn1T[:, TLOC * dc + 128 * g: TLOC * dc + 128 * (g + 1)],
                                        psA[pp][:, :])))

            if "dbg_xn1T" in dbg_out:
                one("sync", ySt, [a_cp[-1]],
                    lambda q: q.e.dma_start(out=dbg_out["dbg_xn1T"], in_=xn1T[:, :]), k=16)

            qseg_tok = [None] * 4
            kv_a2a = {}
            wload = []
            wmm_done = []
            proj_pe = []
            proj_ev = []
            kv_store = {}

            def proj(mat):
                w3 = {"k": wk3, "v": wv3, "q": wq3}[mat]
                br = {"k": bkr_sb, "v": bvr_sb, "q": bqr_sb}[mat]
                dst = {"k": k_in, "v": v_in}.get(mat)
                qtv = qt.rearrange("p (s k) -> p s k", k=16)
                for phi in range(DC):
                    wp = len(wload) % 2
                    wfree = wmm_done[len(wload) - 2] if len(wload) >= 2 else None
                    tw = one("sync", wB, [cast_toks["w" + mat], wfree],
                             lambda q, w3=w3, phi=phi, wp=wp: q.e.dma_start(
                                 out=wstg[wp][:, :],
                                 in_=w3[:, D * phi:D * (phi + 1)]), k=16)
                    wload.append(tw)
                    for ts4 in range(4):
                        idx = len(proj_pe)
                        bp = idx % 2
                        psum_free = proj_ev[idx - 2] if idx >= 2 else None

                        def bpe(q, phi=phi, ts4=ts4, wp=wp, bp=bp, tw=tw,
                                psum_free=psum_free):
                            q.wait(tw, a_cp[(4 * ts4 + 3) * DC + DC - 1], psum_free)
                            for dc in range(DC):
                                inst = q.e.matmul(
                                    psB[bp][:, :],
                                    wstg[wp][:, 128 * dc:128 * (dc + 1)],
                                    xn1T[:, TLOC * dc + 512 * ts4: TLOC * dc + 512 * (ts4 + 1)],
                                    start=(dc == 0), stop=(dc == DC - 1))
                            bpe.tok = tok(pB, inst)
                        em("tensor", bpe)
                        proj_pe.append(bpe.tok)

                        if mat == "q":
                            seg = SEG_OF_TS[ts4]
                            qseg_tok[seg] = one("vector", aB, [proj_pe[idx]],
                                                lambda q, bp=bp, phi=phi, seg=seg, br=br:
                                                q.e.tensor_scalar(
                                                    qtv[:, T // 16 * seg: T // 16 * seg + 512, phi],
                                                    psB[bp][:, :], br[:, phi:phi + 1], 0.0,
                                                    ALU.add, ALU.add))
                            proj_ev.append(qseg_tok[seg])
                        else:
                            sp = idx % 2
                            kscale = SCALE if mat == "k" else 1.0
                            tev = one("vector", aB,
                                      [proj_pe[idx], kv_store.get(idx - 2)],
                                      lambda q, bp=bp, phi=phi, br=br, sp=sp, kscale=kscale:
                                      q.e.tensor_scalar(kvstg[sp][:, :], psB[bp][:, :],
                                                        br[:, phi:phi + 1], kscale,
                                                        ALU.add, ALU.mult))
                            proj_ev.append(tev)
                            kv_store[idx] = one("scalar", kst, [tev],
                                                lambda q, phi=phi, ts4=ts4, sp=sp, dst=dst:
                                                q.e.dma_start(
                                                    out=dst[128 * phi:128 * (phi + 1),
                                                            512 * ts4:512 * (ts4 + 1)],
                                                    in_=kvstg[sp][:, :]), k=16)
                    wmm_done.append(proj_pe[-1])

            proj("k")
            proj("v")
            proj("q")
            for mat in ("k", "v"):
                srcb, dstb = (k_in, k_out) if mat == "k" else (v_in, v_out)
                if noa2a:
                    kv_a2a[mat] = one("gpsimd", cc, [(kst, kst.n)],
                                      lambda q, srcb=srcb, dstb=dstb: q.e.dma_start(
                                          out=dstb[:, :], in_=srcb[:, :]), k=16)
                else:
                    kv_a2a[mat] = coll([],
                                       lambda q, srcb=srcb, dstb=dstb: q.e.collective_compute(
                                           "AllToAll", ALU.bypass, replica_groups=G8,
                                           ins=[srcb.ap().opt()], outs=[dstb.ap().opt()]))
            em("gpsimd", late_casts)
            B_PE_END = proj_pe[-1]
            B_EV_END = proj_ev[-1]

            if "dbg_kout" in dbg_out:
                one("sync", ySt, [kv_a2a["k"]],
                    lambda q: q.e.dma_start(out=dbg_out["dbg_kout"], in_=k_out[:, :]), k=16)
                one("sync", ySt, [kv_a2a["v"]],
                    lambda q: q.e.dma_start(out=dbg_out["dbg_vout"], in_=v_out[:, :]), k=16)
            if "dbg_vin" in dbg_out:
                one("sync", ySt, [(kst, kst.n)],
                    lambda q: q.e.dma_start(out=dbg_out["dbg_vin"], in_=v_in[:, :]), k=16)
                one("sync", ySt, [(kst, kst.n)],
                    lambda q: q.e.dma_start(out=dbg_out["dbg_kin"], in_=k_in[:, :]), k=16)
            if "dbg_qt" in dbg_out:
                one("sync", ySt, [qseg_tok[0], qseg_tok[1], qseg_tok[2], qseg_tok[3],
                                  kv_a2a["k"], kv_a2a["v"]],
                    lambda q: q.e.dma_start(out=dbg_out["dbg_qt"], in_=qt[:, :]), k=16)

            if upto == "B":
                def finB(q):
                    q.wait((ySt, ySt.n), (cc, cc.n), (kst, kst.n), (aB, aB.n))
                em("sync", finB)
                em("gpsimd", finB)
                stAB.close()
                _a2a_ctx.__exit__(None, None, None)
                return nc

            nc.leave_named_scope("phAB", _sc[0], False)
            _sc = nc.enter_named_scope("phC", False)
            stAB.close()
            stC = ExitStack()
            kt = [sb(stC, f"kt{i}", [128, T], BF16) for i in range(2)]
            vt = sb(stC, "vt", [128, T], BF16)
            vp = [sb(stC, f"vp{i}", [128, T], BF16) for i in range(2)]
            PT = [sb(stC, f"PT{i}", [128, 1024], BF16) for i in range(2)]
            acc16 = sb(stC, "acc16", [128, 1024], BF16)
            OTs = [sb(stC, f"OTs{i}", [128, 512], BF16) for i in range(4)]
            lsb = sb(stC, "lsb", [128, 512], F32)
            binv = sb(stC, "binv", [128, 512], F32)
            s_ps = [ps(stC, f"s_ps{i}", [128, 1024], F32) for i in range(2)]
            o_ps = ps(stC, "o_ps", [128, 512], F32)
            lb_ps = ps(stC, "lb_ps", [128, 512], F32)
            vtr_ps = ps(stC, "vtr_ps", [128, 256], BF16)

            pair_pe_end = [None] * 4
            oa2a = [None] * 4
            vevac_end = [None] * 4
            vtr_pe_end = [None] * 4
            kvload_tok = [None] * 4
            sps_free = {0: None, 1: None}
            pt_free = {0: [], 1: []}
            ot_free = {i: None for i in range(4)}
            state = {"last_nm": None, "last_lmm": None, "last_lc": None, "last_lr": None}

            def emit_pv(rec, j):
                it, p2, t_m, t_e, first_wait = rec

                def cpv(q, it=it, p2=p2, t_m=t_m, j=j, first_wait=first_wait):
                    q.wait(t_m)
                    if it == 0:
                        q.wait(first_wait)
                    q.e.matmul(o_ps[:, :], vp[j][:, 256 * it:256 * it + 128],
                               PT[p2][:, 0:512], start=(it == 0), stop=False)
                    i1 = q.e.matmul(o_ps[:, :], vp[j][:, 256 * it + 128:256 * it + 256],
                                    PT[p2][:, 512:1024], start=False, stop=False)
                    cpv.tok = tok(cP, i1)
                em("tensor", cpv)
                pt_free[p2] = [t_m, cpv.tok]
                return cpv.tok

            def emit_cld(seg):
                lh, b = PAIR_OF_SEG[seg]
                j = seg % 2

                def cld(q, seg=seg, lh=lh, b=b, j=j):
                    q.wait(kv_a2a["k"], kv_a2a["v"], B_PE_END, last_cc["t"])
                    if seg >= 1:
                        q.wait(vtr_pe_end[seg - 1])
                    if seg >= 2:
                        q.wait(pair_pe_end[seg - 2], vevac_end[seg - 2])
                    for r in range(NCORE):
                        tok(kvC, q.e.dma_start(
                            out=kt[j][:, TB * r:TB * (r + 1)],
                            in_=k_out[256 * r + 128 * lh:256 * r + 128 * (lh + 1),
                                      TB * b:TB * (b + 1)]), 16)
                        tok(kvC, q.e.dma_start(
                            out=vt[:, TB * r:TB * (r + 1)],
                            in_=v_out[256 * r + 128 * lh:256 * r + 128 * (lh + 1),
                                      TB * b:TB * (b + 1)]), 16)
                em("sync", cld)
                kvload_tok[seg] = (kvC, kvC.n)

            emit_cld(0)
            for seg in range(4):
                lh, b = PAIR_OF_SEG[seg]
                j = seg % 2
                KV_TOK = kvload_tok[seg]

                vtr_e = []
                for kc in range(64):
                    pp = kc % 2
                    frees = ([vtr_e[kc - 2]] if kc >= 2
                             else [B_EV_END, (avt, avt.n), (cE, cE.n)])
                    t = one("tensor", pvt, [KV_TOK] + frees,
                            lambda q, kc=kc, pp=pp: q.e.transpose(
                                vtr_ps[:, 128 * pp:128 * (pp + 1)],
                                vt[:, 128 * kc:128 * (kc + 1)], ident_sb[:, :]))
                    vtr_e.append(one("scalar", avt, [t],
                                     lambda q, j=j, kc=kc, pp=pp: q.e.copy(
                                         vp[j][:, 128 * kc:128 * (kc + 1)],
                                         vtr_ps[:, 128 * pp:128 * (pp + 1)])))
                vevac_end[seg] = vtr_e[-1]
                vtr_pe_end[seg] = (pvt, pvt.n)

                gi = 0
                pend_pv = []
                for jq in range(NQT):
                    L = 2 * jq + 2
                    q0 = 512 * jq
                    acc_first_wait = state["last_lmm"]
                    iter_m = []
                    for it in range(L):
                        p2 = gi % 2
                        k0 = 256 * it
                        k1 = k0 + 128

                        def cs(q, it=it, p2=p2, gi=gi, j=j, q0=q0, seg=seg,
                               sfree=sps_free[p2]):
                            q.wait(qseg_tok[seg], sfree)
                            if seg == 0 and gi < 2:
                                q.wait(B_EV_END)
                            q.e.matmul(s_ps[p2][:, 0:512],
                                       kt[j][:, 256 * it:256 * it + 128],
                                       qt[:, T * seg + q0: T * seg + q0 + 512],
                                       start=True, stop=True)
                            i1 = q.e.matmul(s_ps[p2][:, 512:1024],
                                            kt[j][:, 256 * it + 128:256 * it + 256],
                                            qt[:, T * seg + q0: T * seg + q0 + 512],
                                            start=True, stop=True)
                            cs.tok = tok(cS, i1)
                        em("tensor", cs)
                        t_s = cs.tok

                        if pend_pv:
                            emit_pv(pend_pv.pop(0), j)

                        t_e = one("scalar", cE, [t_s] + pt_free[p2],
                                  lambda q, p2=p2: q.e.activation(
                                      PT[p2][:, :], s_ps[p2][:, :], AF.Exp))
                        sps_free[p2] = t_e

                        def cmk(q, p2=p2, it=it, q0=q0, k0=k0, k1=k1, t_e=t_e,
                                first=(it == 0), faw=acc_first_wait):
                            last = t_e
                            q.wait(last)
                            if k0 >= q0:
                                off = k0 - q0
                                if off > 0:
                                    last = tok(cM, q.e.memset(PT[p2][:, 0:off], 0.0))
                                    q.wait(last)
                                last = tok(cM, q.e.tensor_tensor(
                                    PT[p2][:, off:off + 128], PT[p2][:, off:off + 128],
                                    band_sb[:, :], ALU.mult))
                                q.wait(last)
                            if k1 >= q0:
                                off = k1 - q0
                                if off > 0:
                                    last = tok(cM, q.e.memset(
                                        PT[p2][:, 512:512 + off], 0.0))
                                    q.wait(last)
                                last = tok(cM, q.e.tensor_tensor(
                                    PT[p2][:, 512 + off:512 + off + 128],
                                    PT[p2][:, 512 + off:512 + off + 128],
                                    band_sb[:, :], ALU.mult))
                                q.wait(last)
                            if first:
                                q.wait(faw)
                                cmk.tok = tok(cM, q.e.tensor_copy(acc16[:, :], PT[p2][:, :]))
                            else:
                                cmk.tok = tok(cM, q.e.tensor_tensor(
                                    acc16[:, :], acc16[:, :], PT[p2][:, :], ALU.add))
                        em("vector", cmk)
                        iter_m.append(cmk.tok)
                        pend_pv.append((it, p2, cmk.tok, t_e, state["last_nm"]))
                        gi += 1

                    while pend_pv:
                        emit_pv(pend_pv.pop(0), j)
                    pv_done = (cP, cP.n)

                    def clm(q, am=iter_m[-1], lcp=state["last_lc"]):
                        q.wait(am, lcp)
                        q.e.matmul(lb_ps[:, :], allones_sb[:, :], acc16[:, 0:512],
                                   start=True, stop=False)
                        i1 = q.e.matmul(lb_ps[:, :], allones_sb[:, :], acc16[:, 512:1024],
                                        start=False, stop=True)
                        clm.tok = tok(cL, i1)
                    em("tensor", clm)
                    state["last_lmm"] = clm.tok
                    state["last_lc"] = one("scalar", cLc, [clm.tok, state["last_lr"]],
                                           lambda q: q.e.copy(lsb[:, :], lb_ps[:, :]))
                    state["last_lr"] = one("vector", cLr,
                                           [state["last_lc"], state["last_nm"]],
                                           lambda q: q.e.reciprocal(binv[:, :], lsb[:, :]))
                    t_nm = one("vector", cNm,
                               [state["last_lr"], pv_done, ot_free[jq % 4]],
                               lambda q, jq=jq: q.e.tensor_tensor(
                                   OTs[jq % 4][:, :], o_ps[:, :], binv[:, :], ALU.mult))
                    state["last_nm"] = t_nm
                    ot_free[jq % 4] = dma("sync", cOs, [t_nm],
                                          lambda q, jq=jq, seg=seg: q.e.dma_start(
                                              out=o_in[seg][128 * (jq // 2):128 * (jq // 2 + 1),
                                                            512 * (jq % 2):512 * (jq % 2) + 512],
                                              in_=OTs[jq % 4][:, :]))

                pair_pe_end[seg] = (cP, cP.n)
                if seg < 3:
                    emit_cld(seg + 1)
                oa2a[seg] = coll([], lambda q, seg=seg: q.e.collective_compute(
                    "AllToAll", ALU.bypass, replica_groups=G8,
                    ins=[o_in[seg].ap().opt()], outs=[o_out[seg].ap().opt()]))

            if upto == "C":
                def finC(q):
                    q.wait((ySt, ySt.n), (cc, cc.n), (cOs, cOs.n), (kvC, kvC.n),
                           (cast, cast.n), (cP, cP.n), (cNm, cNm.n))
                em("sync", finC)
                em("gpsimd", finC)
                stC.close()
                _a2a_ctx.__exit__(None, None, None)
                return nc

            if "dbg_oout" in dbg_out:
                one("sync", ySt, [oa2a[0]],
                    lambda q: q.e.dma_start(out=dbg_out["dbg_oout"], in_=o_out[0][:, :]), k=16)
            if "dbg_qt2" in dbg_out:
                one("sync", ySt, [oa2a[3]],
                    lambda q: q.e.dma_start(out=dbg_out["dbg_qt2"], in_=qt[:, :]), k=16)
                one("sync", ySt, [oa2a[3]],
                    lambda q: q.e.dma_start(out=dbg_out["dbg_kout2"], in_=k_out[:, :]), k=16)
                one("sync", ySt, [oa2a[3]],
                    lambda q: q.e.dma_start(out=dbg_out["dbg_vout2"], in_=v_out[:, :]), k=16)

            nc.leave_named_scope("phC", _sc[0], False)
            _sc = nc.enter_named_scope("phD", False)
            stC.close()
            C_END = [(cP, cP.n), (cNm, cNm.n), (cOs, cOs.n), (cLc, cLc.n),
                     (cLr, cLr.n), (avt, avt.n), (cM, cM.n), (cE, cE.n)]
            stD = ExitStack()
            wob = sb(stD, "wob", [128, DC * D], BF16)
            osb = [sb(stD, f"osb{l_}", [128, 1024], BF16) for l_ in range(2)]
            xsb2 = sb(stD, "xsb2", [128, D], F32)
            x1sb = sb(stD, "x1sb", [128, D], F32)
            xn2st = [sb(stD, f"xn2st{i}", [128, D], BF16) for i in range(2)]
            lnstD = sb(stD, "lnstD", [128, 8], F32)
            aggslab = sb(stD, "aggslab", [128, D], BF16)
            psD = [ps(stD, f"psD{i}", [128, 512], F32) for i in range(2)]
            psT = [ps(stD, f"psT{i}", [128, 128], BF16) for i in range(2)]
            psRT = ps(stD, "psRT", [128, 2], F32)
            psRTb = ps(stD, "psRTb", [128, 2], BF16)
            psRow = ps(stD, "psRow", [2, 128], BF16)
            MUD = lnstD[:, 0:1]; M2D = lnstD[:, 1:2]
            VARD = lnstD[:, 2:3]; RSTDD = lnstD[:, 3:4]

            t_wob = dma("sync", dLd, [cast_toks["wo"]] + C_END,
                        lambda q: q.e.dma_start(out=wob[:, :], in_=wo2[:, :]))


            chunk_order = [(0, 7), (1, 7)] + [(b_, t_) for b_ in range(2) for t_ in range(7)]
            d_pe = []
            d_dv = []
            d_ln = [None] * 16
            d_cp = []
            d_tr = []
            d_ld_tok = []
            d_store = {}
            tr_last = {}
            pend_tr = []
            ar0 = {}

            def emit_d_transposes(gc, p, b, tc):
                row0 = 1024 * b + 128 * tc
                for dc in range(DC):
                    pp = dc % 2
                    idx = len(d_tr)
                    t = one("tensor", dPe,
                            [d_ln[gc], d_cp[idx - 2] if idx >= 2 else None],
                            lambda q, p=p, dc=dc, pp=pp: q.e.transpose(
                                psT[pp][:, :], xn2st[p][:, 128 * dc:128 * (dc + 1)],
                                ident_sb[:, :]))
                    d_tr.append(t)
                    d_cp.append(one("scalar", dAc, [t],
                                    lambda q, dc=dc, pp=pp, row0=row0: q.e.copy(
                                        xn2T[:, TLOC * dc + row0: TLOC * dc + row0 + 128],
                                        psT[pp][:, :])))
                tr_last[gc] = d_tr[-1]

            for gc, (b, tc) in enumerate(chunk_order):
                p = gc % 2
                row0 = 1024 * b + 128 * tc

                def dld(q, gc=gc, b=b, tc=tc, p=p, row0=row0):
                    q.wait(oa2a[0], oa2a[1], oa2a[2], oa2a[3], last_cc["t"])
                    if gc >= 1:
                        q.wait(d_pe[4 * (gc - 1) + 3], d_dv[4 * (gc - 1) + 3])
                    for l_ in range(2):
                        seg = 2 * l_ + b
                        tok(dLd, q.e.dma_start(
                            out=osb[l_].rearrange("p (r c) -> p r c", c=128),
                            in_=o_out[seg][:, 128 * tc:128 * (tc + 1)].rearrange(
                                "(r p) c -> p r c", p=128)), 16)
                    tok(dLd, q.e.dma_start(out=xsb2[:, :],
                                           in_=x_loc[row0:row0 + 128, :]), 16)
                em("sync", dld)
                d_ld_tok.append((dLd, dLd.n))

                for ct in range(4):
                    idx = 4 * gc + ct
                    bp = idx % 2

                    def dpe(q, gc=gc, p=p, ct=ct, bp=bp, idx=idx):
                        q.wait(t_wob, d_ld_tok[gc], LD0,
                               d_dv[idx - 2] if idx >= 2 else None)
                        q.e.matmul(psD[bp][:, :], onesr_sb[:, :],
                                   biasrow[:, 512 * ct:512 * (ct + 1)],
                                   start=True, stop=False)
                        for hp in range(16):
                            inst = q.e.matmul(
                                psD[bp][:, :],
                                osb[hp % 2][:, 128 * (hp // 2):128 * (hp // 2 + 1)],
                                wob[:, D * hp + 512 * ct: D * hp + 512 * (ct + 1)],
                                start=False, stop=(hp == 15))
                        dpe.tok = tok(dPe, inst)
                    em("tensor", dpe)
                    d_pe.append(dpe.tok)
                    d_dv.append(one("vector", dDv,
                                    [d_pe[idx], d_store.get(gc - 1)],
                                    lambda q, ct=ct, bp=bp: q.e.tensor_tensor(
                                        x1sb[:, 512 * ct:512 * (ct + 1)], psD[bp][:, :],
                                        xsb2[:, 512 * ct:512 * (ct + 1)], ALU.add)))

                ln_deps = [d_dv[4 * gc + 3], tr_last.get(gc - 2)]
                if gc in (2, 3) and "lr1" in ar0:
                    ln_deps += [ar0["lr1"], ar0["lr2"]]
                d_ln[gc] = emit_ln(x1sb[:, :], xn2st[p][:, :], D, EPS1, ln_deps,
                                   xn2st[p][:, :], d_ln[gc - 1] if gc >= 1 else None,
                                   dDv, dAc, MUD, M2D, VARD, RSTDD)
                d_store[gc] = dma("sync", dSt, [d_ln[gc]],
                                  lambda q, row0=row0: q.e.dma_start(
                                      out=x1s[row0:row0 + 128, :], in_=x1sb[:, :]))

                pend_tr.append((gc, p, b, tc))
                if len(pend_tr) > 1:
                    emit_d_transposes(*pend_tr.pop(0))

                if gc == 1:
                    ar0["lr1"] = dma("gpsimd", tHd, [d_ln[0]],
                                     lambda q: q.e.dma_start(out=scrA[0:1, :],
                                                             in_=xn2st[0][127:128, :]))
                    ar0["lr2"] = dma("gpsimd", tHd, [d_ln[1]],
                                     lambda q: q.e.dma_start(out=scrA[1:2, :],
                                                             in_=xn2st[1][127:128, :]))
                    t3 = one("vector", tH, [ar0["lr1"], ar0["lr2"], LD0],
                             lambda q: q.e.tensor_scalar(
                                 scrA[:, :], scrA[:, :], islast_sb[:, 0:1], 0.0,
                                 ALU.mult, ALU.add))
                    t4 = dma("gpsimd", tHd, [t3],
                             lambda q: q.e.dma_start(out=ar_in[0][:, :], in_=scrA[:, :]))
                    ar0["cc"] = coll([t4], lambda q: q.e.collective_compute(
                                        "AllReduce", ALU.add, replica_groups=G8,
                                        ins=[ar_in[0].ap().opt()],
                                        outs=[ar_out[0].ap().opt()]))
            while pend_tr:
                emit_d_transposes(*pend_tr.pop(0))
            D_TR_END = d_cp[-1]

            if "dbg_x1" in dbg_out:
                one("sync", ySt, [d_store[15]],
                    lambda q: q.e.dma_start(out=dbg_out["dbg_x1"], in_=x1s[:, :]), k=16)

            RMU = lnrow[:, 0:1]; RM2 = lnrow[:, 1:2]
            RVAR = lnrow[:, 2:3]; RRSTD = lnrow[:, 3:4]

            def row_ln(src, gamma, beta, deps):
                tn = emit_ln(src, rowbf[:, :], D, EPS2, deps, rowbf[:, :], None,
                             tH, tH, RMU, RM2, RVAR, RRSTD)
                tg = one("vector", tH, [tn],
                         lambda q: q.e.tensor_tensor(rowbf[:, :], rowbf[:, :],
                                                     gamma[:, :], ALU.mult))
                return one("vector", tH, [tg],
                           lambda q: q.e.tensor_tensor(rowbf[:, :], rowbf[:, :],
                                                       beta[:, :], ALU.add))

            t_ar0 = dma("gpsimd", tHd, [ar0["cc"]],
                        lambda q: q.e.dma_start(out=scrA[:, :], in_=ar_out[0][:, :]))
            t_xbf = row_ln(scrA[:, :], ing2_sb, inb2_sb, [t_ar0])

            def row_T(src_bf, dst, n, dep, psum, first_extra=()):
                last = dep
                for i in range(n):
                    deps = [last] + (list(first_extra) if i == 0 else [])
                    t = one("tensor", tH, deps,
                            lambda q, i=i: q.e.transpose(
                                psum[:, :], src_bf[:, 128 * i:128 * (i + 1)],
                                ident_sb[0:2, 0:2]))
                    last = one("scalar", tH, [t],
                               lambda q, i=i: q.e.copy(dst[:, 2 * i:2 * i + 2], psum[:, :]))
                return last

            t_thT = row_T(rowbf, thT, DC, t_xbf, psRTb)

            agg_done = []
            last_tok = t_thT
            for oc in range(2):
                t_agw = dma("sync", dLd, [cast_toks["agg"]] + ([last_tok] if oc else C_END),
                            lambda q, oc=oc: q.e.dma_start(
                                out=aggslab[:, :],
                                in_=aggw3[:, D * oc:D * (oc + 1)]))

                def amm(q, oc=oc, last_tok=last_tok, t_agw=t_agw):
                    q.wait(last_tok, t_agw)
                    for dc in range(DC):
                        inst = q.e.matmul(
                            psRT[:, :],
                            aggslab[:, 128 * dc:128 * (dc + 1)],
                            thT[:, 2 * dc:2 * (dc + 1)],
                            start=(dc == 0), stop=(dc == DC - 1))
                    amm.tok = tok(tH, inst)
                em("tensor", amm)
                t_ev = one("vector", tH, [amm.tok],
                           lambda q, oc=oc: q.e.tensor_scalar(
                               thTf[:, 2 * oc:2 * (oc + 1)], psRT[:, :],
                               aggbr_sb[:, oc:oc + 1], 0.0, ALU.add, ALU.add))
                t_bt = one("tensor", tH, [t_ev],
                           lambda q, oc=oc: q.e.transpose(
                               psRow[:, :], thTf[:, 2 * oc:2 * (oc + 1)], ident_sb[:, :]))
                t_cp2 = one("scalar", tH, [t_bt],
                            lambda q, oc=oc: q.e.copy(scrA[0:2, 128 * oc:128 * (oc + 1)],
                                                      psRow[:, :]))
                last_tok = t_cp2
                agg_done.append(t_cp2)
            t_agst = dma("gpsimd", tHd, agg_done,
                         lambda q: q.e.dma_start(out=ag_in[0][:, :],
                                                 in_=scrA[0:2, 0:256]))
            t_agcc = coll([t_agst], lambda q: q.e.collective_compute(
                             "AllGather", ALU.bypass, replica_groups=G8,
                             ins=[ag_in[0].ap().opt()], outs=[ag_out[0].ap().opt()]))
            t_thld = None
            for r in range(NCORE):
                t_thld = dma("gpsimd", tHd, [t_agcc],
                             lambda q, r=r: q.e.dma_start(
                                 out=th_tm[:, 256 * r:256 * (r + 1)],
                                 in_=ag_out[0][2 * r:2 * r + 2, :]))
            TH0 = t_thld

            nc.leave_named_scope("phD", _sc[0], False)
            _sc = nc.enter_named_scope("phE", False)
            if upto == "D":
                def finD(q):
                    q.wait((ySt, ySt.n), (cc, cc.n), (dSt, dSt.n), (tHd, tHd.n),
                           (dLd, dLd.n), (dAc, dAc.n), (dPe, dPe.n), (tH, tH.n))
                em("sync", finD)
                em("gpsimd", finD)
                stD.close()
                _a2a_ctx.__exit__(None, None, None)
                return nc

            D_END = [(dPe, dPe.n), (dDv, dDv.n), (dAc, dAc.n), (dSt, dSt.n),
                     (tH, tH.n), (tHd, tHd.n)]
            DST_END = (dSt, dSt.n)
            stD.close()
            stE = ExitStack()
            wgslab = [sb(stE, f"wgslab{i}", [128, D], BF16) for i in range(2)]
            bcslab = sb(stE, "bcslab", [128, D], BF16)
            d1sb = [sb(stE, f"d1sb{i}", [128, DC * 128], BF16) for i in range(2)]
            d2sb = sb(stE, "d2sb", [128, 4096], BF16)
            blockbuf = sb(stE, "blockbuf", [128, DC * 512], BF16)
            gts = [sb(stE, f"gts{i}", [128, 512], BF16) for i in range(2)]
            out2tm = sb(stE, "out2tm", [128, D], BF16)
            n_tm = sb(stE, "n_tm", [128, D], BF16)
            x1b = [sb(stE, f"x1b{i}", [128, D], F32) for i in range(2)]
            x2sb = sb(stE, "x2sb", [128, D], F32)
            outgB_sb = sb(stE, "outgB_sb", [128, D], F32)
            outbB_sb = sb(stE, "outbB_sb", [128, D], F32)
            lnstE = sb(stE, "lnstE", [128, 8], F32)
            psE = [ps(stE, f"psE{i}", [128, 512], F32) for i in range(2)]
            psT2 = [ps(stE, f"psT2{i}", [128, 128], BF16) for i in range(2)]
            psd1 = ps(stE, "psd1", [128, 4], F32)
            psd2 = ps(stE, "psd2", [2, 512], F32)
            psRTb2 = ps(stE, "psRTb2", [128, 2], BF16)
            psRow2 = ps(stE, "psRow2", [2, 128], BF16)
            MUE = lnstE[:, 0:1]; M2E = lnstE[:, 1:2]
            VARE = lnstE[:, 2:3]; RSTDE = lnstE[:, 3:4]

            t_outg = dma("sync", eLd, D_END,
                         lambda q: q.e.dma_start(out=outgB_sb[:, :], in_=outgB))
            t_outb2 = dma("sync", eLd, D_END,
                          lambda q: q.e.dma_start(out=outbB_sb[:, :], in_=outbB))
            t_d2brow = dma("sync", eLd, D_END + [(dPe, dPe.n)],
                           lambda q: q.e.dma_start(out=biasrow[:, :], in_=d2b_row))

            if upto == "E0":
                def finE0(q):
                    q.wait((ySt, ySt.n), (cc, cc.n), (tHd, tHd.n), (eLd, eLd.n))
                em("sync", finE0)
                em("gpsimd", finE0)
                stE.close()
                _a2a_ctx.__exit__(None, None, None)
                return nc

            th_d1mm = {}
            th_gel = {}
            th_d2done = {}
            th_prev = TH0
            NSTEP = {"E1a": 1, "E1b": 2, "E1c": 3}.get(upto, 4)
            for step in range(NSTEP):
                t_bf = row_ln(th_tm[:, :], png2_sb, pnb2_sb, [th_prev])
                t_tT = row_T(rowbf, thT, DC, t_bf, psRTb2,
                             first_extra=(D_END if step == 0 else ()))
                for oc in range(8):
                    wp = oc % 2
                    t_wl = dma("sync", eLd,
                               [cast_toks["d1"], th_d1mm.get((step, oc - 2)),
                                th_d1mm.get((step - 1, oc + 6))],
                               lambda q, oc=oc, wp=wp: q.e.dma_start(
                                   out=d1sb[wp][:, :],
                                   in_=d1w3[:, 2048 * oc:2048 * (oc + 1)]))

                    def dmm(q, oc=oc, wp=wp, t_tT=t_tT, t_wl=t_wl, step=step):
                        q.wait(t_tT, t_wl, th_gel.get((step, oc - 1)),
                               th_gel.get((step - 1, 7)))
                        for dc in range(DC):
                            inst = q.e.matmul(
                                psd1[:, 0:2],
                                d1sb[wp][:, 128 * dc:128 * (dc + 1)],
                                thT[:, 2 * dc:2 * (dc + 1)],
                                start=(dc == 0), stop=(dc == DC - 1))
                        dmm.tok = tok(tH, inst)
                    em("tensor", dmm)
                    th_d1mm[(step, oc)] = dmm.tok
                    th_gel[(step, oc)] = one("scalar", tH, [dmm.tok],
                                             lambda q, oc=oc: q.e.activation(
                                                 gT[:, 2 * oc:2 * (oc + 1)],
                                                 psd1[:, 0:2],
                                                 AF.Gelu, bias=d1br_sb[:, oc:oc + 1]))
                gel_all = [th_gel[(step, oc)] for oc in range(8)]
                prev_ev = None
                ev4 = []
                prev_mm = th_d2done.get(step - 1)
                for ct in range(4):
                    t_d2w = dma("sync", eLd,
                                [cast_toks["d2"], prev_mm, t_d2brow],
                                lambda q, ct=ct: q.e.dma_start(
                                    out=d2sb[:, :],
                                    in_=d2w3[:, 4096 * ct:4096 * (ct + 1)]))

                    def d2mm(q, ct=ct, prev_ev=prev_ev, gel_all=gel_all, t_d2w=t_d2w):
                        q.wait(t_d2w, prev_ev, *gel_all)
                        q.e.matmul(psd2[:, :], onesz_sb[:, :],
                                   biasrow[:, 512 * ct:512 * (ct + 1)],
                                   start=True, stop=False)
                        for oc in range(8):
                            inst = q.e.matmul(
                                psd2[:, :], gT[:, 2 * oc:2 * (oc + 1)],
                                d2sb[:, 512 * oc:512 * (oc + 1)],
                                start=False, stop=(oc == 7))
                        d2mm.tok = tok(tH, inst)
                    em("tensor", d2mm)
                    prev_mm = d2mm.tok
                    prev_ev = one("scalar", tH, [d2mm.tok],
                                  lambda q, ct=ct: q.e.copy(
                                      scrA[:, 512 * ct:512 * (ct + 1)], psd2[:, :]))
                    ev4.append(prev_ev)
                th_d2done[step] = ev4[-1]
                t_st = dma("gpsimd", tHd, ev4,
                           lambda q, step=step: q.e.dma_start(
                               out=ar_in[1 + step][:, :], in_=scrA[:, :]))
                t_cc = coll([t_st], lambda q, step=step: q.e.collective_compute(
                               "AllReduce", ALU.add, replica_groups=G8,
                               ins=[ar_in[1 + step].ap().opt()],
                               outs=[ar_out[1 + step].ap().opt()]))
                t_ld = dma("gpsimd", tHd, [t_cc],
                           lambda q, step=step: q.e.dma_start(
                               out=scrA[:, :], in_=ar_out[1 + step][:, :]))
                th_prev = one("vector", tH, [t_ld],
                              lambda q: q.e.tensor_tensor(th_tm[:, :], th_tm[:, :],
                                                          scrA[:, :], ALU.add))

            if "dbg_th" in dbg_out:
                one("sync", ySt, [th_prev],
                    lambda q: q.e.dma_start(out=dbg_out["dbg_th"], in_=th_tm[:, :]), k=16)

            if upto in ("E1", "E1a", "E1b", "E1c"):
                def finE1(q):
                    q.wait((ySt, ySt.n), (cc, cc.n), (tHd, tHd.n), (eLd, eLd.n),
                           (tH, tH.n))
                em("sync", finE1)
                em("gpsimd", finE1)
                stE.close()
                _a2a_ctx.__exit__(None, None, None)
                return nc

            t_bf2 = one("vector", tH, [th_prev],
                        lambda q: q.e.tensor_copy(rowbf[:, :], th_tm[:, :]))
            t_tT2 = row_T(rowbf, thT, DC, t_bf2, psRTb2)
            bc_done = []
            last_tok = t_tT2
            for oc in range(2):
                t_bcw = dma("sync", eLd, [cast_toks["bc"], last_tok],
                            lambda q, oc=oc: q.e.dma_start(
                                out=bcslab[:, :], in_=bcw3[:, D * oc:D * (oc + 1)]))

                def bmm(q, oc=oc, last_tok=last_tok, t_bcw=t_bcw):
                    q.wait(last_tok, t_bcw)
                    for dc in range(DC):
                        inst = q.e.matmul(
                            psd1[:, 0:2],
                            bcslab[:, 128 * dc:128 * (dc + 1)],
                            thT[:, 2 * dc:2 * (dc + 1)],
                            start=(dc == 0), stop=(dc == DC - 1))
                    bmm.tok = tok(tH, inst)
                em("tensor", bmm)
                t_ev = one("vector", tH, [bmm.tok],
                           lambda q, oc=oc: q.e.tensor_scalar(
                               thTf[:, 2 * oc:2 * (oc + 1)], psd1[:, 0:2],
                               bcbr_sb[:, oc:oc + 1], 0.0, ALU.add, ALU.add))
                t_bt = one("tensor", tH, [t_ev],
                           lambda q, oc=oc: q.e.transpose(
                               psRow2[:, :], thTf[:, 2 * oc:2 * (oc + 1)], ident_sb[:, :]))
                t_cp2 = one("scalar", tH, [t_bt],
                            lambda q, oc=oc: q.e.copy(scrA[0:2, 128 * oc:128 * (oc + 1)],
                                                      psRow2[:, :]))
                last_tok = t_cp2
                bc_done.append(t_cp2)
            t_bcst = dma("gpsimd", tHd, bc_done,
                         lambda q: q.e.dma_start(out=ag_in[1][:, :],
                                                 in_=scrA[0:2, 0:256]))
            t_bccc = coll([t_bcst], lambda q: q.e.collective_compute(
                             "AllGather", ALU.bypass, replica_groups=G8,
                             ins=[ag_in[1].ap().opt()], outs=[ag_out[1].ap().opt()]))
            t_bcld = None
            for r in range(NCORE):
                t_bcld = dma("gpsimd", tHd, [t_bccc],
                             lambda q, r=r: q.e.dma_start(
                                 out=scrA[:, 256 * r:256 * (r + 1)],
                                 in_=ag_out[1][2 * r:2 * r + 2, :]))
            t_bcbf = one("vector", tH, [t_bcld],
                         lambda q: q.e.tensor_copy(rowbf[:, :], scrA[:, :]))
            last_tok = t_bcbf
            for dc in range(DC):
                t = one("tensor", tH, [last_tok],
                        lambda q, dc=dc: q.e.transpose(
                            psRTb2[:, :], rowbf[:, 128 * dc:128 * (dc + 1)],
                            ident_sb[0:2, 0:2]))
                last_tok = one("scalar", tH, [t],
                               lambda q, dc=dc: q.e.copy(bcT[:, 2 * dc:2 * dc + 2],
                                                         psRTb2[:, :]))
            BCT = last_tok

            if "dbg_bc" in dbg_out:
                one("sync", ySt, [BCT],
                    lambda q: q.e.dma_start(out=dbg_out["dbg_bc"], in_=scrA[:, :]), k=16)

            if upto == "E2":
                def finE2(q):
                    q.wait((ySt, ySt.n), (cc, cc.n), (tHd, tHd.n), (eLd, eLd.n),
                           (tH, tH.n))
                em("sync", finE2)
                em("gpsimd", finE2)
                stE.close()
                _a2a_ctx.__exit__(None, None, None)
                return nc

            e_pe = []
            e_sg = []
            e_ts = []
            e_tr_cp = []
            x1_free = {}
            x2_free = {}
            lnE_state = {"last": None, "sqfree": None}
            prev_tb_tr = {"t": None}
            gchunk = 0
            NTB = {"G1": 1, "G2": 2, "G3": 3}.get(upto, 4)
            for tb in range(NTB):
                b = tb // 2
                for cci in range(DC):
                    idx = len(e_pe)
                    bp = idx % 2
                    wp = idx % 2
                    t_wg = dma("sync", eLd,
                               [cast_toks["wg"], DST_END,
                                e_pe[idx - 2] if idx >= 2 else None],
                               lambda q, cci=cci, wp=wp: q.e.dma_start(
                                   out=wgslab[wp][:, :],
                                   in_=wg2[:, D * cci:D * (cci + 1)]))

                    def epe(q, tb=tb, cci=cci, bp=bp, idx=idx, wp=wp, t_wg=t_wg):
                        q.wait(t_wg, D_TR_END, LD0,
                               e_sg[idx - 2] if idx >= 2 else None)
                        for dc in range(DC):
                            inst = q.e.matmul(
                                psE[bp][:, :],
                                wgslab[wp][:, 128 * dc:128 * (dc + 1)],
                                xn2T[:, TLOC * dc + 512 * tb: TLOC * dc + 512 * (tb + 1)],
                                start=(dc == 0), stop=(dc == DC - 1))
                        epe.tok = tok(ePe, inst)
                    em("tensor", epe)
                    e_pe.append(epe.tok)
                    sp = idx % 2
                    e_sg.append(one("scalar", eAc,
                                    [e_pe[idx], e_ts[idx - 2] if idx >= 2 else None],
                                    lambda q, bp=bp, sp=sp, cci=cci: q.e.activation(
                                        gts[sp][:, :], psE[bp][:, :], AF.Sigmoid,
                                        bias=gatebr_sb[:, cci:cci + 1])))
                    t_t1 = one("vector", eDv, [e_sg[idx], BCT],
                               lambda q, sp=sp, cci=cci, b=b: q.e.tensor_scalar(
                                   gts[sp][:, :], gts[sp][:, :],
                                   bcT[:, 2 * cci + b:2 * cci + b + 1], 0.0,
                                   ALU.mult, ALU.add))
                    e_ts.append(one("vector", eDv, [t_t1, prev_tb_tr["t"]],
                                    lambda q, sp=sp, cci=cci, tb=tb: q.e.tensor_tensor(
                                        blockbuf[:, 512 * cci:512 * (cci + 1)],
                                        xn2T[:, TLOC * cci + 512 * tb: TLOC * cci + 512 * (tb + 1)],
                                        gts[sp][:, :], ALU.add)))
                blk_done = e_ts[-DC:]

                last_tr_this_tb = None
                for i4 in range(4):
                    row0 = 512 * tb + 128 * i4
                    for dc in range(DC):
                        pp = dc % 2
                        idx2 = gchunk * DC + dc
                        t = one("tensor", ePe,
                                blk_done + [e_tr_cp[idx2 - 2] if idx2 >= 2 else None],
                                lambda q, dc=dc, pp=pp, i4=i4: q.e.transpose(
                                    psT2[pp][:, :],
                                    blockbuf[:, 512 * dc + 128 * i4: 512 * dc + 128 * (i4 + 1)],
                                    ident_sb[:, :]))
                        last_tr_this_tb = t
                        e_tr_cp.append(one("scalar", eAc,
                                           [t, lnE_state["last"] if dc == 0 else None],
                                           lambda q, dc=dc, pp=pp: q.e.copy(
                                               out2tm[:, 128 * dc:128 * (dc + 1)],
                                               psT2[pp][:, :])))
                    t_evall = e_tr_cp[-1]
                    t_nm2 = emit_ln(out2tm[:, :], n_tm[:, :], D, EPS2, [t_evall],
                                    n_tm[:, :], lnE_state["last"],
                                    eDv, eAc, MUE, M2E, VARE, RSTDE)
                    lnE_state["last"] = t_nm2
                    xp = gchunk % 2
                    t_x1 = dma("sync", eLd, [DST_END, x1_free.get(gchunk - 2)],
                               lambda q, xp=xp, row0=row0: q.e.dma_start(
                                   out=x1b[xp][:, :], in_=x1s[row0:row0 + 128, :]))
                    t_g1 = one("vector", eDv, [t_nm2, x2_free.get(gchunk - 1), t_outg],
                               lambda q: q.e.tensor_tensor(
                                   x2sb[:, :], n_tm[:, :], outgB_sb[:, :], ALU.mult))
                    t_g1b = one("vector", eDv, [t_g1, t_outb2],
                                lambda q: q.e.tensor_tensor(
                                    x2sb[:, :], x2sb[:, :], outbB_sb[:, :], ALU.add))
                    t_g2 = one("vector", eDv, [t_g1b, t_x1],
                               lambda q, xp=xp: q.e.tensor_tensor(
                                   x2sb[:, :], x2sb[:, :], x1b[xp][:, :], ALU.add))
                    x1_free[gchunk] = t_g2
                    x2_free[gchunk] = dma("sync", ySt, [t_g2],
                                          lambda q, row0=row0: q.e.dma_start(
                                              out=y_loc[row0:row0 + 128, :],
                                              in_=x2sb[:, :]))
                    gchunk += 1
                prev_tb_tr["t"] = last_tr_this_tb

            nc.leave_named_scope("phE", _sc[0], False)

            def fin(q):
                q.wait((ySt, ySt.n), (cc, cc.n), (dSt, dSt.n), (tHd, tHd.n),
                       (kst, kst.n), (cOs, cOs.n), (eLd, eLd.n), (eDv, eDv.n),
                       (eAc, eAc.n), (ePe, ePe.n))
            em("sync", fin)
            em("gpsimd", fin)
            stE.close()

    _a2a_ctx.__exit__(None, None, None)
    return nc


def _prep_inputs(inputs, c):
    import ml_dtypes
    bf = ml_dtypes.bfloat16
    X = np.asarray(inputs["L0_attn_x"])
    g = {}
    g["x_loc"] = np.ascontiguousarray(
        np.concatenate([X[0, 1024 * c:1024 * (c + 1), :],
                        X[1, 1024 * c:1024 * (c + 1), :]], axis=0)).astype(np.float32)
    g["wq"] = np.ascontiguousarray(inputs["L0_attn_Wq"], dtype=np.float32)
    g["wk"] = np.ascontiguousarray(inputs["L0_attn_Wk"], dtype=np.float32)
    g["wv"] = np.ascontiguousarray(inputs["L0_attn_Wv"], dtype=np.float32)
    g["wo"] = np.ascontiguousarray(inputs["L0_attn_Wo"], dtype=np.float32)
    g["wg"] = np.ascontiguousarray(inputs["L0_ffn_gate_w"], dtype=np.float32)

    def resh(v, n):
        return np.ascontiguousarray(np.asarray(v, dtype=np.float32).reshape(n, 128).T)

    g["bqr"] = resh(inputs["L0_attn_bq"], DC)
    g["bkr"] = resh(inputs["L0_attn_bk"], DC)
    g["bvr"] = resh(inputs["L0_attn_bv"], DC)
    g["bo_row"] = np.asarray(inputs["L0_attn_bo"], dtype=np.float32).reshape(1, D)
    g["gatebr"] = resh(inputs["L0_ffn_gate_b"], DC)
    for nm, key in [("ing2", "L0_ffn_in_gamma"), ("inb2", "L0_ffn_in_beta"),
                    ("png2", "L0_ffn_pn_gamma"), ("pnb2", "L0_ffn_pn_beta")]:
        v = np.asarray(inputs[key], dtype=np.float32)
        g[nm] = np.ascontiguousarray(np.stack([v, v], axis=0))
    g["outgB"] = np.ascontiguousarray(np.broadcast_to(
        np.asarray(inputs["L0_ffn_out_gamma"], dtype=np.float32), (128, D)))
    g["outbB"] = np.ascontiguousarray(np.broadcast_to(
        np.asarray(inputs["L0_ffn_out_beta"], dtype=np.float32), (128, D)))
    g["d1w_s"] = np.ascontiguousarray(
        np.asarray(inputs["L0_ffn_d1_w"], dtype=np.float32)[:, 1024 * c:1024 * (c + 1)])
    g["d1br"] = resh(np.asarray(inputs["L0_ffn_d1_b"],
                                dtype=np.float32)[1024 * c:1024 * (c + 1)], 8)
    g["d2w_s"] = np.ascontiguousarray(
        np.asarray(inputs["L0_ffn_d2_w"], dtype=np.float32)[1024 * c:1024 * (c + 1), :])
    g["d2b_row"] = np.asarray(inputs["L0_ffn_d2_b"], dtype=np.float32).reshape(1, D)
    g["aggw_s"] = np.ascontiguousarray(
        np.asarray(inputs["L0_ffn_agg_w"], dtype=np.float32)[:, 256 * c:256 * (c + 1)])
    g["aggbr"] = resh(np.asarray(inputs["L0_ffn_agg_b"],
                                 dtype=np.float32)[256 * c:256 * (c + 1)], 2)
    g["bcw_s"] = np.ascontiguousarray(
        np.asarray(inputs["L0_ffn_bc_w"], dtype=np.float32)[:, 256 * c:256 * (c + 1)])
    g["bcbr"] = resh(np.asarray(inputs["L0_ffn_bc_b"],
                                dtype=np.float32)[256 * c:256 * (c + 1)], 2)
    g["ident_bf"] = np.eye(128, dtype=np.float32).astype(bf)
    g["band_bf"] = np.triu(np.ones((128, 128), dtype=np.float32)).astype(bf)
    g["allones_bf"] = np.ones((128, 128), dtype=np.float32).astype(bf)
    g["ones_f32_row"] = np.ones((1, 128), dtype=np.float32)
    g["ones_or_zero"] = np.full((1, 2), 1.0 if c == 0 else 0.0, dtype=np.float32)
    g["is_last"] = np.full((2, 1), 1.0 if c == 7 else 0.0, dtype=np.float32)
    g["cshift_col"] = np.full((128, 1), CSHIFT, dtype=np.float32)
    return g


def kernel(**inputs):
    nc = build()
    in_maps = [_prep_inputs(inputs, c) for c in range(NCORE)]
    res = run_bass_kernel_spmd(nc, in_maps, core_ids=list(range(NCORE)))
    out = np.empty((2, T, D), dtype=np.float32)
    for c in range(NCORE):
        y = res.results[c]["y_loc"]
        out[0, 1024 * c:1024 * (c + 1), :] = y[0:1024, :]
        out[1, 1024 * c:1024 * (c + 1), :] = y[1024:2048, :]
    return out


# version 07 (final)
# speedup vs baseline: 1.3776x; 1.3681x over previous
import numpy as np
import concourse.bass as bass
import concourse.mybir as mybir
from concourse.bass_utils import run_bass_kernel_spmd
from contextlib import ExitStack
from concourse.aot_env import allow_ici_all_to_all

F32 = mybir.dt.float32
BF16 = mybir.dt.bfloat16
AF = mybir.ActivationFunctionType
ALU = mybir.AluOpType
AX = mybir.AxisListType

D = 2048
DC = 16
TB = 1024
TLOC = 2048
T = 8192
NCORE = 8
NQT = 16
SCALE = float(1.0 / np.sqrt(128.0))
CSHIFT = -12.0
EPS1 = 1e-5
EPS2 = 1e-6
G8 = [[0, 1, 2, 3, 4, 5, 6, 7]]
SEG_OF_TS = [0, 2, 1, 3]
PAIR_OF_SEG = [(0, 0), (0, 1), (1, 0), (1, 1)]


class Sem:
    __slots__ = ("h", "n", "name")

    def __init__(self, h, name):
        self.h = h
        self.n = 0
        self.name = name


class Q:
    def __init__(self, eng):
        self.e = eng
        self.seen = {}

    def wait(self, *toks):
        for t in toks:
            if t is None:
                continue
            s, v = t
            if v > 0 and self.seen.get(s.name, 0) < v:
                self.e.wait_ge(s.h, v)
                self.seen[s.name] = v


def build(dbg=(), upto=None, noa2a=False, fine=()):
    dbg = set(dbg)
    fine = set(fine)
    _a2a_ctx = allow_ici_all_to_all()
    _a2a_ctx.__enter__()
    nc = bass.Bass("TRN2", target_bir_lowering=False, debug=False)

    def din(name, shape, dtype=F32):
        return nc.dram_tensor(name, shape, dtype, kind="ExternalInput").ap()

    x_loc = din("x_loc", [TLOC, D])
    wq = din("wq", [D, D]); wk = din("wk", [D, D]); wv = din("wv", [D, D])
    wo = din("wo", [D, D]); wg = din("wg", [D, D])
    bqr = din("bqr", [128, DC]); bkr = din("bkr", [128, DC]); bvr = din("bvr", [128, DC])
    bo_row = din("bo_row", [1, D])
    gatebr = din("gatebr", [128, DC])
    ing2 = din("ing2", [2, D]); inb2 = din("inb2", [2, D])
    png2 = din("png2", [2, D]); pnb2 = din("pnb2", [2, D])
    outgB = din("outgB", [128, D]); outbB = din("outbB", [128, D])
    d1w_s = din("d1w_s", [D, 1024]); d1br = din("d1br", [128, 8])
    d2w_s = din("d2w_s", [1024, D]); d2b_row = din("d2b_row", [1, D])
    aggw_s = din("aggw_s", [D, 256]); aggbr = din("aggbr", [128, 2])
    bcw_s = din("bcw_s", [D, 256]); bcbr = din("bcbr", [128, 2])
    ident_bf = din("ident_bf", [128, 128], BF16)
    band_bf = din("band_bf", [128, 128], BF16)
    allones_bf = din("allones_bf", [128, 128], BF16)
    ones_f32_row = din("ones_f32_row", [1, 128])
    ones_or_zero = din("ones_or_zero", [1, 2])
    is_last = din("is_last", [2, 1])
    cshift_col = din("cshift_col", [128, 1])

    y_loc = nc.dram_tensor("y_loc", [TLOC, D], F32, kind="ExternalOutput").ap()

    dbg_specs = {
        "dbg_xn1T": ([128, DC * TLOC], BF16),
        "dbg_kout": ([D, TLOC], BF16),
        "dbg_vout": ([D, TLOC], BF16),
        "dbg_qt": ([128, 4 * T], BF16),
        "dbg_oout": ([NCORE * 128, TB], BF16),
        "dbg_x1": ([TLOC, D], F32),
        "dbg_vin": ([D, TLOC], BF16),
        "dbg_kin": ([D, TLOC], BF16),
        "dbg_qt2": ([128, 4 * T], BF16),
        "dbg_kout2": ([D, TLOC], BF16),
        "dbg_vout2": ([D, TLOC], BF16),
        "dbg_th": ([2, D], F32),
        "dbg_bc": ([2, D], F32),
    }
    dbg_out = {k: nc.dram_tensor(k, *dbg_specs[k], kind="ExternalOutput").ap()
               for k in dbg if k in dbg_specs}

    def dsc(name, shape, dtype):
        return nc.dram_tensor(name, shape, dtype)

    wq3 = dsc("wq3", [128, DC * D], BF16)
    wk3 = dsc("wk3", [128, DC * D], BF16)
    wv3 = dsc("wv3", [128, DC * D], BF16)
    wo2 = dsc("wo2", [128, DC * D], BF16)
    wg2 = dsc("wg2", [128, DC * D], BF16)
    d1w3 = dsc("d1w3", [128, 8 * 16 * 128], BF16)
    d2w3 = dsc("d2w3", [128, 8 * D], BF16)
    aggw3 = dsc("aggw3", [128, DC * 256], BF16)
    bcw3 = dsc("bcw3", [128, DC * 256], BF16)
    k_in = dsc("k_in", [D, TLOC], BF16)
    v_in = dsc("v_in", [D, TLOC], BF16)
    k_out = dsc("k_out", [D, TLOC], BF16)
    v_out = dsc("v_out", [D, TLOC], BF16)
    o_in = [dsc(f"o_in{s}", [NCORE * 128, TB], BF16) for s in range(4)]
    o_out = [dsc(f"o_out{s}", [NCORE * 128, TB], BF16) for s in range(4)]
    x1s = dsc("x1s", [TLOC, D], F32)
    ag_in = [dsc(f"ag_in{i}", [2, 256], F32) for i in range(2)]
    ag_out = [dsc(f"ag_out{i}", [16, 256], F32) for i in range(2)]
    ar_in = [dsc(f"ar_in{i}", [2, D], F32) for i in range(5)]
    ar_out = [dsc(f"ar_out{i}", [2, D], F32) for i in range(5)]

    st = ExitStack()
    with st:
        def sb(stk, name, shape, dtype):
            return stk.enter_context(nc.sbuf_tensor(name, shape, dtype))

        def ps(stk, name, shape, dtype):
            return stk.enter_context(nc.psum_tensor(name, shape, dtype))

        bqr_sb = sb(st, "bqr_sb", [128, DC], F32)
        bkr_sb = sb(st, "bkr_sb", [128, DC], F32)
        bvr_sb = sb(st, "bvr_sb", [128, DC], F32)
        gatebr_sb = sb(st, "gatebr_sb", [128, DC], F32)
        d1br_sb = sb(st, "d1br_sb", [128, 8], F32)
        aggbr_sb = sb(st, "aggbr_sb", [128, 2], F32)
        bcbr_sb = sb(st, "bcbr_sb", [128, 2], F32)
        ident_sb = sb(st, "ident_sb", [128, 128], BF16)
        band_sb = sb(st, "band_sb", [128, 128], BF16)
        allones_sb = sb(st, "allones_sb", [128, 128], BF16)
        onesr_sb = sb(st, "onesr_sb", [1, 128], F32)
        onesz_sb = sb(st, "onesz_sb", [1, 2], F32)
        islast_sb = sb(st, "islast_sb", [2, 1], F32)
        cshift_sb = sb(st, "cshift_sb", [128, 1], F32)
        thT = sb(st, "thT", [128, DC * 2], BF16)
        thTf = sb(st, "thTf", [128, 4], BF16)
        gT = sb(st, "gT", [128, 8 * 2], BF16)
        bcT = sb(st, "bcT", [128, DC * 2], F32)
        lnrow = sb(st, "lnrow", [2, 8], F32)
        th_tm = sb(st, "th_tm", [2, D], F32)
        scrA = sb(st, "scrA", [2, D], F32)
        biasrow = sb(st, "biasrow", [1, D], F32)
        rowbf = sb(st, "rowbf", [2, D], BF16)
        ing2_sb = sb(st, "ing2_sb", [2, D], BF16)
        inb2_sb = sb(st, "inb2_sb", [2, D], BF16)
        png2_sb = sb(st, "png2_sb", [2, D], BF16)
        pnb2_sb = sb(st, "pnb2_sb", [2, D], BF16)
        arena1 = sb(st, "arena1", [128, 4 * T], BF16)
        qt = arena1
        xn2T = arena1

        def sem(name):
            return Sem(st.enter_context(nc.semaphore(name)), name)

        ld0 = sem("ld0"); cast = sem("cast")
        xA = sem("xA"); vA = sem("vA"); sA = sem("sA"); pA = sem("pA")
        wB = sem("wB"); pB = sem("pB"); aB = sem("aB"); kst = sem("kst")
        cc = sem("cc")
        kvC = sem("kvC"); pvt = sem("pvt"); avt = sem("avt")
        cS = sem("cS"); cE = sem("cE"); cM = sem("cM"); cP = sem("cP")
        cL = sem("cL"); cLc = sem("cLc"); cLr = sem("cLr"); cNm = sem("cNm")
        cOs = sem("cOs")
        dLd = sem("dLd"); dPe = sem("dPe"); dDv = sem("dDv"); dAc = sem("dAc")
        dSt = sem("dSt")
        tH = sem("tH"); tHd = sem("tHd")
        eLd = sem("eLd"); ePe = sem("ePe"); eAc = sem("eAc"); eDv = sem("eDv")
        ySt = sem("ySt")
        DMA_SEMS = [ld0, cast, xA, wB, kst, kvC, cOs, dLd, dSt, tHd, eLd, ySt]

        with nc.Block() as block:
            wrap = {}
            pending = {}

            def em(engname, fn):
                pending.setdefault(engname, []).append(fn)

            def flush():
                for engname in ("tensor", "scalar", "vector", "sync", "gpsimd"):
                    fns = pending.pop(engname, None)
                    if not fns:
                        continue

                    def body(eng, fns=fns, engname=engname):
                        q = wrap.get(engname)
                        if q is None:
                            q = wrap[engname] = Q(eng)
                        for f in fns:
                            f(q)
                    getattr(block, engname)(body)

            def one(engname, s, deps, buildfn, k=1):
                s.n += k
                t = (s, s.n)

                def f(q):
                    q.wait(*deps)
                    buildfn(q).then_inc(s.h, k)
                em(engname, f)
                return t

            last_cc = {"t": None}

            def ccdep():
                return [(s, s.n) for s in DMA_SEMS]

            def dma(engname, s, deps, buildfn):
                return one(engname, s, list(deps) + [last_cc["t"]], buildfn, k=16)

            def coll(deps, buildfn):
                t = one("gpsimd", cc, list(deps) + ccdep() + [last_cc["t"]], buildfn)
                last_cc["t"] = t
                return t

            P0_LIST = None

            P0_PAIRS = [
                (bqr_sb, bqr), (bkr_sb, bkr), (bvr_sb, bvr),
                (gatebr_sb, gatebr),
                (biasrow, bo_row), (d1br_sb, d1br),
                (aggbr_sb, aggbr), (bcbr_sb, bcbr),
                (ident_sb, ident_bf), (band_sb, band_bf),
                (allones_sb, allones_bf), (onesr_sb, ones_f32_row),
                (onesz_sb, ones_or_zero), (islast_sb, is_last),
                (cshift_sb, cshift_col),
            ]

            def p0_sync(q):
                for dst, src in P0_PAIRS:
                    q.e.dma_start(out=dst[:, :], in_=src).then_inc(ld0.h, 16)
            ld0.n += len(P0_PAIRS) * 16
            em("sync", p0_sync)

            cast_toks = {}

            def p0_g(q):
                for dst, srcp in [(ing2_sb, ing2), (inb2_sb, inb2),
                                  (png2_sb, png2), (pnb2_sb, pnb2)]:
                    q.e.dma_start(out=dst[:, :], in_=srcp).then_inc(ld0.h, 16)
                for nm, w3, w in [("wk", wk3, wk), ("wv", wv3, wv), ("wq", wq3, wq)]:
                    w3v = w3.rearrange("p (f c o) -> p f c o", c=DC, o=128)
                    for cc_ in range(DC):
                        q.e.dma_start(
                            out=w3v[:, :, cc_, :],
                            in_=w[128 * cc_:128 * (cc_ + 1), :].rearrange(
                                "p (f o) -> p f o", o=128)).then_inc(cast.h, 16)
            ld0.n += 4 * 16
            for nm in ("wk", "wv", "wq"):
                cast.n += 16 * 16
                cast_toks[nm] = (cast, cast.n)
            em("gpsimd", p0_g)
            LD0 = (ld0, ld0.n)
            flush()

            def late_casts(q, _lc=None):
                q.wait(_lc)
                for hp in range(DC):
                    q.e.dma_start(
                        out=wo2[:, D * hp:D * (hp + 1)],
                        in_=wo[128 * hp:128 * (hp + 1), :]).then_inc(cast.h, 16)
                wg2v = wg2.rearrange("p (f c o) -> p f c o", c=DC, o=128)
                for dc_ in range(DC):
                    q.e.dma_start(
                        out=wg2v[:, :, dc_, :],
                        in_=wg[128 * dc_:128 * (dc_ + 1), :].rearrange(
                            "p (f o) -> p f o", o=128)).then_inc(cast.h, 16)
                for nm, w3, w in [("agg", aggw3, aggw_s), ("bc", bcw3, bcw_s)]:
                    w3v = w3.rearrange("p (g c o) -> p g c o", c=DC, o=128)
                    for dc_ in range(DC):
                        q.e.dma_start(
                            out=w3v[:, :, dc_, :],
                            in_=w[128 * dc_:128 * (dc_ + 1), :].rearrange(
                                "p (g o) -> p g o", o=128)).then_inc(cast.h, 16)
                d1v = d1w3.rearrange("p (g c o) -> p g c o", c=DC, o=128)
                for cc_ in range(DC):
                    q.e.dma_start(
                        out=d1v[:, :, cc_, :],
                        in_=d1w_s[128 * cc_:128 * (cc_ + 1), :].rearrange(
                            "p (g o) -> p g o", o=128)).then_inc(cast.h, 16)
                d2v = d2w3.rearrange("p (t g o) -> p t g o", g=8, o=512)
                for oc_ in range(8):
                    q.e.dma_start(
                        out=d2v[:, :, oc_, :],
                        in_=d2w_s[128 * oc_:128 * (oc_ + 1), :].rearrange(
                            "p (t o) -> p t o", o=512)).then_inc(cast.h, 16)

            def emit_ln(src, dst, W, eps, deps, sq_buf, serial_prev, v_sem, s_sem,
                        mu, m2, var, rstd):
                t1 = one("vector", v_sem, list(deps) + [serial_prev],
                         lambda q: q.e.reduce_sum(mu, src, axis=AX.X))
                t2s = one("vector", v_sem, list(deps) + [serial_prev, t1],
                          lambda q: q.e.tensor_tensor(sq_buf, src, src, ALU.mult))
                t2 = one("vector", v_sem, [t2s],
                         lambda q: q.e.reduce_sum(m2, sq_buf, axis=AX.X))
                t3 = one("vector", v_sem, [t2],
                         lambda q: q.e.tensor_scalar(mu, mu, 1.0 / W, 0.0, ALU.mult, ALU.add))
                t4 = one("vector", v_sem, [t3],
                         lambda q: q.e.tensor_scalar(m2, m2, 1.0 / W, 0.0, ALU.mult, ALU.add))
                t5 = one("vector", v_sem, [t4],
                         lambda q: q.e.tensor_tensor(var, mu, mu, ALU.mult))
                t7 = one("vector", v_sem, [t5],
                         lambda q: q.e.tensor_tensor(var, m2, var, ALU.subtract))
                t8 = one("vector", v_sem, [t7],
                         lambda q: q.e.tensor_scalar(var, var, eps, 0.0, ALU.add, ALU.add))
                t9 = one("vector", v_sem, [t8], lambda q: q.e.reciprocal(rstd, var))
                t10 = one("scalar", s_sem, [t9], lambda q: q.e.activation(rstd, rstd, AF.Sqrt))
                return one("vector", v_sem, [t10],
                           lambda q: q.e.tensor_scalar(dst, src, mu[:, 0:1], rstd[:, 0:1],
                                                       ALU.subtract, ALU.mult))

            _sc = nc.enter_named_scope("phAB", False)
            stAB = ExitStack()
            xn1T = sb(stAB, "xn1T", [128, DC * TLOC], BF16)
            xsbA = sb(stAB, "xsbA", [128, D], F32)
            xn1s = [sb(stAB, f"xn1s{i}", [128, D], BF16) for i in range(2)]
            lnst = sb(stAB, "lnst", [128, 8], F32)
            wstg = [sb(stAB, f"wstg{i}", [128, DC * 128], BF16) for i in range(2)]
            kvstg = [sb(stAB, f"kvstg{i}", [128, 512], BF16) for i in range(2)]
            psA = [ps(stAB, f"psA{i}", [128, 128], BF16) for i in range(2)]
            psB = [ps(stAB, f"psB{i}", [128, 512], F32) for i in range(2)]
            MU = lnst[:, 0:1]; M2 = lnst[:, 1:2]; VAR = lnst[:, 2:3]; RSTD = lnst[:, 3:4]

            a_fin = [None] * 16
            a_tr = []
            a_cp = []
            for g in range(16):
                p = g % 2
                tx = one("sync", xA, [a_fin[g - 1] if g >= 1 else None],
                         lambda q, g=g: q.e.dma_start(
                             out=xsbA[:, :], in_=x_loc[128 * g:128 * (g + 1), :]), k=16)
                a_fin[g] = emit_ln(xsbA[:, :], xn1s[p][:, :], D, EPS1,
                                   [tx, LD0], xn1s[p][:, :],
                                   a_fin[g - 1] if g >= 1 else None,
                                   vA, sA, MU, M2, VAR, RSTD)
                for dc in range(DC):
                    pp = dc % 2
                    idx = g * DC + dc
                    t = one("tensor", pA,
                            [a_fin[g], a_cp[idx - 2] if idx >= 2 else LD0],
                            lambda q, p=p, dc=dc, pp=pp: q.e.transpose(
                                psA[pp][:, :], xn1s[p][:, 128 * dc:128 * (dc + 1)],
                                ident_sb[:, :]))
                    a_tr.append(t)
                    a_cp.append(one("scalar", sA, [t],
                                    lambda q, g=g, dc=dc, pp=pp: q.e.copy(
                                        xn1T[:, TLOC * dc + 128 * g: TLOC * dc + 128 * (g + 1)],
                                        psA[pp][:, :])))
                if g % 4 == 3 or "A" in fine:
                    flush()

            if "dbg_xn1T" in dbg_out:
                one("sync", ySt, [a_cp[-1]],
                    lambda q: q.e.dma_start(out=dbg_out["dbg_xn1T"], in_=xn1T[:, :]), k=16)

            qseg_tok = [None] * 4
            kv_a2a = {}
            wload = []
            wmm_done = []
            proj_pe = []
            proj_ev = []
            kv_store = {}

            def proj(mat):
                w3 = {"k": wk3, "v": wv3, "q": wq3}[mat]
                br = {"k": bkr_sb, "v": bvr_sb, "q": bqr_sb}[mat]
                dst = {"k": k_in, "v": v_in}.get(mat)
                qtv = qt.rearrange("p (s k) -> p s k", k=16)
                for phi in range(DC):
                    wp = len(wload) % 2
                    wfree = wmm_done[len(wload) - 2] if len(wload) >= 2 else None
                    tw = one("sync", wB, [cast_toks["w" + mat], wfree],
                             lambda q, w3=w3, phi=phi, wp=wp: q.e.dma_start(
                                 out=wstg[wp][:, :],
                                 in_=w3[:, D * phi:D * (phi + 1)]), k=16)
                    wload.append(tw)
                    for ts4 in range(4):
                        idx = len(proj_pe)
                        bp = idx % 2
                        psum_free = proj_ev[idx - 2] if idx >= 2 else None

                        pB.n += 1
                        proj_pe.append((pB, pB.n))

                        def bpe(q, phi=phi, ts4=ts4, wp=wp, bp=bp, tw=tw,
                                psum_free=psum_free):
                            q.wait(tw, a_cp[(4 * ts4 + 3) * DC + DC - 1], psum_free)
                            for dc in range(DC):
                                inst = q.e.matmul(
                                    psB[bp][:, :],
                                    wstg[wp][:, 128 * dc:128 * (dc + 1)],
                                    xn1T[:, TLOC * dc + 512 * ts4: TLOC * dc + 512 * (ts4 + 1)],
                                    start=(dc == 0), stop=(dc == DC - 1))
                            inst.then_inc(pB.h, 1)
                        em("tensor", bpe)

                        if mat == "q":
                            seg = SEG_OF_TS[ts4]
                            qseg_tok[seg] = one("vector", aB, [proj_pe[idx]],
                                                lambda q, bp=bp, phi=phi, seg=seg, br=br:
                                                q.e.tensor_scalar(
                                                    qtv[:, T // 16 * seg: T // 16 * seg + 512, phi],
                                                    psB[bp][:, :], br[:, phi:phi + 1], 0.0,
                                                    ALU.add, ALU.add))
                            proj_ev.append(qseg_tok[seg])
                        else:
                            sp = idx % 2
                            kscale = SCALE if mat == "k" else 1.0
                            tev = one("vector", aB,
                                      [proj_pe[idx], kv_store.get(idx - 2)],
                                      lambda q, bp=bp, phi=phi, br=br, sp=sp, kscale=kscale:
                                      q.e.tensor_scalar(kvstg[sp][:, :], psB[bp][:, :],
                                                        br[:, phi:phi + 1], kscale,
                                                        ALU.add, ALU.mult))
                            proj_ev.append(tev)
                            kv_store[idx] = one("scalar", kst, [tev],
                                                lambda q, phi=phi, ts4=ts4, sp=sp, dst=dst:
                                                q.e.dma_start(
                                                    out=dst[128 * phi:128 * (phi + 1),
                                                            512 * ts4:512 * (ts4 + 1)],
                                                    in_=kvstg[sp][:, :]), k=16)
                    wmm_done.append(proj_pe[-1])
                    if "B" in fine:
                        flush()

            proj("k")
            proj("v")
            proj("q")
            for mat in ("k", "v"):
                srcb, dstb = (k_in, k_out) if mat == "k" else (v_in, v_out)
                if noa2a:
                    kv_a2a[mat] = one("gpsimd", cc, [(kst, kst.n)],
                                      lambda q, srcb=srcb, dstb=dstb: q.e.dma_start(
                                          out=dstb[:, :], in_=srcb[:, :]), k=16)
                else:
                    kv_a2a[mat] = coll([],
                                       lambda q, srcb=srcb, dstb=dstb: q.e.collective_compute(
                                           "AllToAll", ALU.bypass, replica_groups=G8,
                                           ins=[srcb.ap().opt()], outs=[dstb.ap().opt()]))
            for nm, cnt in [("wo", DC), ("wg", DC), ("agg", DC), ("bc", DC),
                            ("d1", DC), ("d2", 8)]:
                cast.n += cnt * 16
                cast_toks[nm] = (cast, cast.n)
            import functools as _ft
            em("gpsimd", _ft.partial(late_casts, _lc=last_cc["t"]))
            flush()
            B_PE_END = proj_pe[-1]
            B_EV_END = proj_ev[-1]

            if "dbg_kout" in dbg_out:
                one("sync", ySt, [kv_a2a["k"]],
                    lambda q: q.e.dma_start(out=dbg_out["dbg_kout"], in_=k_out[:, :]), k=16)
                one("sync", ySt, [kv_a2a["v"]],
                    lambda q: q.e.dma_start(out=dbg_out["dbg_vout"], in_=v_out[:, :]), k=16)
            if "dbg_vin" in dbg_out:
                one("sync", ySt, [(kst, kst.n)],
                    lambda q: q.e.dma_start(out=dbg_out["dbg_vin"], in_=v_in[:, :]), k=16)
                one("sync", ySt, [(kst, kst.n)],
                    lambda q: q.e.dma_start(out=dbg_out["dbg_kin"], in_=k_in[:, :]), k=16)
            if "dbg_qt" in dbg_out:
                one("sync", ySt, [qseg_tok[0], qseg_tok[1], qseg_tok[2], qseg_tok[3],
                                  kv_a2a["k"], kv_a2a["v"]],
                    lambda q: q.e.dma_start(out=dbg_out["dbg_qt"], in_=qt[:, :]), k=16)

            if upto == "B":
                def finB(q):
                    q.wait((ySt, ySt.n), (cc, cc.n), (kst, kst.n), (aB, aB.n))
                em("sync", finB)
                em("gpsimd", finB)
                flush()
                stAB.close()
                _a2a_ctx.__exit__(None, None, None)
                return nc

            nc.leave_named_scope("phAB", _sc[0], False)
            _sc = nc.enter_named_scope("phC", False)
            stAB.close()
            stC = ExitStack()
            kt = [sb(stC, f"kt{i}", [128, T], BF16) for i in range(2)]
            vt = sb(stC, "vt", [128, T], BF16)
            vp = [sb(stC, f"vp{i}", [128, T], BF16) for i in range(2)]
            PT = [sb(stC, f"PT{i}", [128, 1024], BF16) for i in range(2)]
            acc16 = sb(stC, "acc16", [128, 1024], BF16)
            OTs = [sb(stC, f"OTs{i}", [128, 512], BF16) for i in range(4)]
            lsb = sb(stC, "lsb", [128, 512], F32)
            binv = sb(stC, "binv", [128, 512], F32)
            s_ps = [ps(stC, f"s_ps{i}", [128, 1024], F32) for i in range(2)]
            o_ps = ps(stC, "o_ps", [128, 512], F32)
            lb_ps = ps(stC, "lb_ps", [128, 512], F32)
            vtr_ps = ps(stC, "vtr_ps", [128, 256], BF16)

            pair_pe_end = [None] * 4
            oa2a = [None] * 4
            vevac_end = [None] * 4
            vtr_pe_end = [None] * 4
            kvload_tok = [None] * 4
            sps_free = {0: None, 1: None}
            pt_free = {0: [], 1: []}
            ot_free = {i: None for i in range(4)}
            state = {"last_nm": None, "last_lmm": None, "last_lc": None, "last_lr": None}

            def emit_pv(rec, j):
                it, p2, t_m, t_e, first_wait = rec
                cP.n += 1
                mytok = (cP, cP.n)

                def cpv(q, it=it, p2=p2, t_m=t_m, j=j, first_wait=first_wait):
                    q.wait(t_m)
                    if it == 0:
                        q.wait(first_wait)
                    q.e.matmul(o_ps[:, :], vp[j][:, 256 * it:256 * it + 128],
                               PT[p2][:, 0:512], start=(it == 0), stop=False)
                    i1 = q.e.matmul(o_ps[:, :], vp[j][:, 256 * it + 128:256 * it + 256],
                                    PT[p2][:, 512:1024], start=False, stop=False)
                    i1.then_inc(cP.h, 1)
                em("tensor", cpv)
                pt_free[p2] = [t_m, mytok]
                return mytok

            def emit_cld(seg):
                lh, b = PAIR_OF_SEG[seg]
                j = seg % 2

                def cld(q, seg=seg, lh=lh, b=b, j=j, _lc=last_cc["t"],
                        _vfree=(vtr_pe_end[seg - 1] if seg >= 1 else None),
                        _kfree=(pair_pe_end[seg - 2] if seg >= 2 else None),
                        _pfree=(vevac_end[seg - 2] if seg >= 2 else None)):
                    q.wait(kv_a2a["k"], kv_a2a["v"], B_PE_END, _lc, _vfree,
                           _kfree, _pfree)
                    for r in range(NCORE):
                        q.e.dma_start(
                            out=kt[j][:, TB * r:TB * (r + 1)],
                            in_=k_out[256 * r + 128 * lh:256 * r + 128 * (lh + 1),
                                      TB * b:TB * (b + 1)]).then_inc(kvC.h, 16)
                        q.e.dma_start(
                            out=vt[:, TB * r:TB * (r + 1)],
                            in_=v_out[256 * r + 128 * lh:256 * r + 128 * (lh + 1),
                                      TB * b:TB * (b + 1)]).then_inc(kvC.h, 16)
                em("sync", cld)
                kvC.n += 16 * 16
                kvload_tok[seg] = (kvC, kvC.n)

            emit_cld(0)
            for seg in range(4):
                lh, b = PAIR_OF_SEG[seg]
                j = seg % 2
                KV_TOK = kvload_tok[seg]

                vtr_e = []
                for kc in range(64):
                    pp = kc % 2
                    frees = ([vtr_e[kc - 2]] if kc >= 2
                             else [B_EV_END, (avt, avt.n), (cE, cE.n)])
                    t = one("tensor", pvt, [KV_TOK] + frees,
                            lambda q, kc=kc, pp=pp: q.e.transpose(
                                vtr_ps[:, 128 * pp:128 * (pp + 1)],
                                vt[:, 128 * kc:128 * (kc + 1)], ident_sb[:, :]))
                    vtr_e.append(one("scalar", avt, [t],
                                     lambda q, j=j, kc=kc, pp=pp: q.e.copy(
                                         vp[j][:, 128 * kc:128 * (kc + 1)],
                                         vtr_ps[:, 128 * pp:128 * (pp + 1)])))
                vevac_end[seg] = vtr_e[-1]
                vtr_pe_end[seg] = (pvt, pvt.n)

                gi = 0
                pend_pv = []
                for jq in range(NQT):
                    L = 2 * jq + 2
                    q0 = 512 * jq
                    acc_first_wait = state["last_lmm"]
                    iter_m = []
                    for it in range(L):
                        p2 = gi % 2
                        k0 = 256 * it
                        k1 = k0 + 128

                        cS.n += 1
                        t_s = (cS, cS.n)

                        def cs(q, it=it, p2=p2, gi=gi, j=j, q0=q0, seg=seg,
                               sfree=sps_free[p2]):
                            q.wait(qseg_tok[seg], sfree)
                            if seg == 0 and gi < 2:
                                q.wait(B_EV_END)
                            q.e.matmul(s_ps[p2][:, 0:512],
                                       kt[j][:, 256 * it:256 * it + 128],
                                       qt[:, T * seg + q0: T * seg + q0 + 512],
                                       start=True, stop=True)
                            i1 = q.e.matmul(s_ps[p2][:, 512:1024],
                                            kt[j][:, 256 * it + 128:256 * it + 256],
                                            qt[:, T * seg + q0: T * seg + q0 + 512],
                                            start=True, stop=True)
                            i1.then_inc(cS.h, 1)
                        em("tensor", cs)

                        if pend_pv:
                            emit_pv(pend_pv.pop(0), j)

                        t_e = one("scalar", cE, [t_s] + pt_free[p2],
                                  lambda q, p2=p2: q.e.activation(
                                      PT[p2][:, :], s_ps[p2][:, :], AF.Exp))
                        sps_free[p2] = t_e

                        nops = 1
                        if k0 >= q0:
                            nops += (1 if k0 - q0 > 0 else 0) + 1
                        if k1 >= q0:
                            nops += (1 if k1 - q0 > 0 else 0) + 1
                        cm_base = cM.n
                        cM.n += nops
                        cmk_tok = (cM, cM.n)

                        def cmk(q, p2=p2, it=it, q0=q0, k0=k0, k1=k1, t_e=t_e,
                                first=(it == 0), faw=acc_first_wait, cm_base=cm_base):
                            v = cm_base
                            q.wait(t_e)
                            if k0 >= q0:
                                off = k0 - q0
                                if off > 0:
                                    q.e.memset(PT[p2][:, 0:off], 0.0).then_inc(cM.h, 1)
                                    v += 1
                                    q.wait((cM, v))
                                q.e.tensor_tensor(
                                    PT[p2][:, off:off + 128], PT[p2][:, off:off + 128],
                                    band_sb[:, :], ALU.mult).then_inc(cM.h, 1)
                                v += 1
                                q.wait((cM, v))
                            if k1 >= q0:
                                off = k1 - q0
                                if off > 0:
                                    q.e.memset(PT[p2][:, 512:512 + off], 0.0).then_inc(cM.h, 1)
                                    v += 1
                                    q.wait((cM, v))
                                q.e.tensor_tensor(
                                    PT[p2][:, 512 + off:512 + off + 128],
                                    PT[p2][:, 512 + off:512 + off + 128],
                                    band_sb[:, :], ALU.mult).then_inc(cM.h, 1)
                                v += 1
                                q.wait((cM, v))
                            if first:
                                q.wait(faw)
                                q.e.tensor_copy(acc16[:, :], PT[p2][:, :]).then_inc(cM.h, 1)
                            else:
                                q.e.tensor_tensor(acc16[:, :], acc16[:, :],
                                                  PT[p2][:, :], ALU.add).then_inc(cM.h, 1)
                        em("vector", cmk)
                        iter_m.append(cmk_tok)
                        pend_pv.append((it, p2, cmk_tok, t_e, state["last_nm"]))
                        gi += 1

                    while pend_pv:
                        emit_pv(pend_pv.pop(0), j)
                    pv_done = (cP, cP.n)

                    cL.n += 1
                    clm_tok = (cL, cL.n)

                    def clm(q, am=iter_m[-1], lcp=state["last_lc"]):
                        q.wait(am, lcp)
                        q.e.matmul(lb_ps[:, :], allones_sb[:, :], acc16[:, 0:512],
                                   start=True, stop=False)
                        i1 = q.e.matmul(lb_ps[:, :], allones_sb[:, :], acc16[:, 512:1024],
                                        start=False, stop=True)
                        i1.then_inc(cL.h, 1)
                    em("tensor", clm)
                    state["last_lmm"] = clm_tok
                    state["last_lc"] = one("scalar", cLc, [clm_tok, state["last_lr"]],
                                           lambda q: q.e.copy(lsb[:, :], lb_ps[:, :]))
                    state["last_lr"] = one("vector", cLr,
                                           [state["last_lc"], state["last_nm"]],
                                           lambda q: q.e.reciprocal(binv[:, :], lsb[:, :]))
                    t_nm = one("vector", cNm,
                               [state["last_lr"], pv_done, ot_free[jq % 4]],
                               lambda q, jq=jq: q.e.tensor_tensor(
                                   OTs[jq % 4][:, :], o_ps[:, :], binv[:, :], ALU.mult))
                    state["last_nm"] = t_nm
                    ot_free[jq % 4] = dma("sync", cOs, [t_nm],
                                          lambda q, jq=jq, seg=seg: q.e.dma_start(
                                              out=o_in[seg][128 * (jq // 2):128 * (jq // 2 + 1),
                                                            512 * (jq % 2):512 * (jq % 2) + 512],
                                              in_=OTs[jq % 4][:, :]))
                    flush()

                pair_pe_end[seg] = (cP, cP.n)
                if seg < 3:
                    emit_cld(seg + 1)
                oa2a[seg] = coll([], lambda q, seg=seg: q.e.collective_compute(
                    "AllToAll", ALU.bypass, replica_groups=G8,
                    ins=[o_in[seg].ap().opt()], outs=[o_out[seg].ap().opt()]))
                flush()

            if upto == "C":
                def finC(q):
                    q.wait((ySt, ySt.n), (cc, cc.n), (cOs, cOs.n), (kvC, kvC.n),
                           (cast, cast.n), (cP, cP.n), (cNm, cNm.n))
                em("sync", finC)
                em("gpsimd", finC)
                flush()
                stC.close()
                _a2a_ctx.__exit__(None, None, None)
                return nc

            if "dbg_oout" in dbg_out:
                one("sync", ySt, [oa2a[0]],
                    lambda q: q.e.dma_start(out=dbg_out["dbg_oout"], in_=o_out[0][:, :]), k=16)
            if "dbg_qt2" in dbg_out:
                one("sync", ySt, [oa2a[3]],
                    lambda q: q.e.dma_start(out=dbg_out["dbg_qt2"], in_=qt[:, :]), k=16)
                one("sync", ySt, [oa2a[3]],
                    lambda q: q.e.dma_start(out=dbg_out["dbg_kout2"], in_=k_out[:, :]), k=16)
                one("sync", ySt, [oa2a[3]],
                    lambda q: q.e.dma_start(out=dbg_out["dbg_vout2"], in_=v_out[:, :]), k=16)

            nc.leave_named_scope("phC", _sc[0], False)
            _sc = nc.enter_named_scope("phD", False)
            stC.close()
            C_END = [(cP, cP.n), (cNm, cNm.n), (cOs, cOs.n), (cLc, cLc.n),
                     (cLr, cLr.n), (avt, avt.n), (cM, cM.n), (cE, cE.n)]
            stD = ExitStack()
            wob = sb(stD, "wob", [128, DC * D], BF16)
            osb = [sb(stD, f"osb{l_}", [128, 1024], BF16) for l_ in range(2)]
            xsb2 = sb(stD, "xsb2", [128, D], F32)
            x1sb = sb(stD, "x1sb", [128, D], F32)
            xn2st = [sb(stD, f"xn2st{i}", [128, D], BF16) for i in range(2)]
            lnstD = sb(stD, "lnstD", [128, 8], F32)
            aggslab = sb(stD, "aggslab", [128, D], BF16)
            psD = [ps(stD, f"psD{i}", [128, 512], F32) for i in range(2)]
            psT = [ps(stD, f"psT{i}", [128, 128], BF16) for i in range(2)]
            psRT = ps(stD, "psRT", [128, 2], F32)
            psRTb = ps(stD, "psRTb", [128, 2], BF16)
            psRow = ps(stD, "psRow", [2, 128], BF16)
            MUD = lnstD[:, 0:1]; M2D = lnstD[:, 1:2]
            VARD = lnstD[:, 2:3]; RSTDD = lnstD[:, 3:4]

            t_wob = dma("sync", dLd, [cast_toks["wo"]] + C_END,
                        lambda q: q.e.dma_start(out=wob[:, :], in_=wo2[:, :]))


            chunk_order = [(0, 7), (1, 7)] + [(b_, t_) for b_ in range(2) for t_ in range(7)]
            d_pe = []
            d_dv = []
            d_ln = [None] * 16
            d_cp = []
            d_tr = []
            d_ld_tok = []
            d_store = {}
            tr_last = {}
            pend_tr = []
            ar0 = {}

            def emit_d_transposes(gc, p, b, tc):
                row0 = 1024 * b + 128 * tc
                for dc in range(DC):
                    pp = dc % 2
                    idx = len(d_tr)
                    t = one("tensor", dPe,
                            [d_ln[gc], d_cp[idx - 2] if idx >= 2 else None],
                            lambda q, p=p, dc=dc, pp=pp: q.e.transpose(
                                psT[pp][:, :], xn2st[p][:, 128 * dc:128 * (dc + 1)],
                                ident_sb[:, :]))
                    d_tr.append(t)
                    d_cp.append(one("scalar", dAc, [t],
                                    lambda q, dc=dc, pp=pp, row0=row0: q.e.copy(
                                        xn2T[:, TLOC * dc + row0: TLOC * dc + row0 + 128],
                                        psT[pp][:, :])))
                tr_last[gc] = d_tr[-1]

            for gc, (b, tc) in enumerate(chunk_order):
                p = gc % 2
                row0 = 1024 * b + 128 * tc

                def dld(q, gc=gc, b=b, tc=tc, p=p, row0=row0, _lc=last_cc["t"],
                        _f1=(d_pe[4 * (gc - 1) + 3] if gc >= 1 else None),
                        _f2=(d_dv[4 * (gc - 1) + 3] if gc >= 1 else None)):
                    q.wait(oa2a[0], oa2a[1], oa2a[2], oa2a[3], _lc, _f1, _f2)
                    for l_ in range(2):
                        seg = 2 * l_ + b
                        q.e.dma_start(
                            out=osb[l_].rearrange("p (r c) -> p r c", c=128),
                            in_=o_out[seg][:, 128 * tc:128 * (tc + 1)].rearrange(
                                "(r p) c -> p r c", p=128)).then_inc(dLd.h, 16)
                    q.e.dma_start(out=xsb2[:, :],
                                  in_=x_loc[row0:row0 + 128, :]).then_inc(dLd.h, 16)
                em("sync", dld)
                dLd.n += 3 * 16
                d_ld_tok.append((dLd, dLd.n))

                for ct in range(4):
                    idx = 4 * gc + ct
                    bp = idx % 2

                    dPe.n += 1
                    d_pe.append((dPe, dPe.n))

                    def dpe(q, gc=gc, p=p, ct=ct, bp=bp, idx=idx,
                            _f=(d_dv[idx - 2] if idx >= 2 else None)):
                        q.wait(t_wob, d_ld_tok[gc], LD0, _f)
                        q.e.matmul(psD[bp][:, :], onesr_sb[:, :],
                                   biasrow[:, 512 * ct:512 * (ct + 1)],
                                   start=True, stop=False)
                        for hp in range(16):
                            inst = q.e.matmul(
                                psD[bp][:, :],
                                osb[hp % 2][:, 128 * (hp // 2):128 * (hp // 2 + 1)],
                                wob[:, D * hp + 512 * ct: D * hp + 512 * (ct + 1)],
                                start=False, stop=(hp == 15))
                        inst.then_inc(dPe.h, 1)
                    em("tensor", dpe)
                    d_dv.append(one("vector", dDv,
                                    [d_pe[idx], d_store.get(gc - 1)],
                                    lambda q, ct=ct, bp=bp: q.e.tensor_tensor(
                                        x1sb[:, 512 * ct:512 * (ct + 1)], psD[bp][:, :],
                                        xsb2[:, 512 * ct:512 * (ct + 1)], ALU.add)))

                ln_deps = [d_dv[4 * gc + 3], tr_last.get(gc - 2)]
                if gc in (2, 3) and "lr1" in ar0:
                    ln_deps += [ar0["lr1"], ar0["lr2"]]
                d_ln[gc] = emit_ln(x1sb[:, :], xn2st[p][:, :], D, EPS1, ln_deps,
                                   xn2st[p][:, :], d_ln[gc - 1] if gc >= 1 else None,
                                   dDv, dAc, MUD, M2D, VARD, RSTDD)
                d_store[gc] = dma("sync", dSt, [d_ln[gc]],
                                  lambda q, row0=row0: q.e.dma_start(
                                      out=x1s[row0:row0 + 128, :], in_=x1sb[:, :]))

                pend_tr.append((gc, p, b, tc))
                if len(pend_tr) > 1:
                    emit_d_transposes(*pend_tr.pop(0))
                if gc % 4 == 3:
                    flush()

                if gc == 1:
                    ar0["lr1"] = dma("gpsimd", tHd, [d_ln[0]],
                                     lambda q: q.e.dma_start(out=scrA[0:1, :],
                                                             in_=xn2st[0][127:128, :]))
                    ar0["lr2"] = dma("gpsimd", tHd, [d_ln[1]],
                                     lambda q: q.e.dma_start(out=scrA[1:2, :],
                                                             in_=xn2st[1][127:128, :]))
                    t3 = one("vector", tH, [ar0["lr1"], ar0["lr2"], LD0],
                             lambda q: q.e.tensor_scalar(
                                 scrA[:, :], scrA[:, :], islast_sb[:, 0:1], 0.0,
                                 ALU.mult, ALU.add))
                    t4 = dma("gpsimd", tHd, [t3],
                             lambda q: q.e.dma_start(out=ar_in[0][:, :], in_=scrA[:, :]))
                    ar0["cc"] = coll([t4], lambda q: q.e.collective_compute(
                                        "AllReduce", ALU.add, replica_groups=G8,
                                        ins=[ar_in[0].ap().opt()],
                                        outs=[ar_out[0].ap().opt()]))
            while pend_tr:
                emit_d_transposes(*pend_tr.pop(0))
            D_TR_END = d_cp[-1]
            flush()

            if "dbg_x1" in dbg_out:
                one("sync", ySt, [d_store[15]],
                    lambda q: q.e.dma_start(out=dbg_out["dbg_x1"], in_=x1s[:, :]), k=16)

            RMU = lnrow[:, 0:1]; RM2 = lnrow[:, 1:2]
            RVAR = lnrow[:, 2:3]; RRSTD = lnrow[:, 3:4]

            def row_ln(src, gamma, beta, deps):
                tn = emit_ln(src, rowbf[:, :], D, EPS2, deps, rowbf[:, :], None,
                             tH, tH, RMU, RM2, RVAR, RRSTD)
                tg = one("vector", tH, [tn],
                         lambda q: q.e.tensor_tensor(rowbf[:, :], rowbf[:, :],
                                                     gamma[:, :], ALU.mult))
                return one("vector", tH, [tg],
                           lambda q: q.e.tensor_tensor(rowbf[:, :], rowbf[:, :],
                                                       beta[:, :], ALU.add))

            t_ar0 = dma("gpsimd", tHd, [ar0["cc"]],
                        lambda q: q.e.dma_start(out=scrA[:, :], in_=ar_out[0][:, :]))
            t_xbf = row_ln(scrA[:, :], ing2_sb, inb2_sb, [t_ar0])

            def row_T(src_bf, dst, n, dep, psum, first_extra=()):
                last = dep
                for i in range(n):
                    deps = [last] + (list(first_extra) if i == 0 else [])
                    t = one("tensor", tH, deps,
                            lambda q, i=i: q.e.transpose(
                                psum[:, :], src_bf[:, 128 * i:128 * (i + 1)],
                                ident_sb[0:2, 0:2]))
                    last = one("scalar", tH, [t],
                               lambda q, i=i: q.e.copy(dst[:, 2 * i:2 * i + 2], psum[:, :]))
                return last

            t_thT = row_T(rowbf, thT, DC, t_xbf, psRTb)

            agg_done = []
            last_tok = t_thT
            for oc in range(2):
                t_agw = dma("sync", dLd, [cast_toks["agg"]] + ([last_tok] if oc else C_END),
                            lambda q, oc=oc: q.e.dma_start(
                                out=aggslab[:, :],
                                in_=aggw3[:, D * oc:D * (oc + 1)]))

                tH.n += 1
                amm_tok = (tH, tH.n)

                def amm(q, oc=oc, last_tok=last_tok, t_agw=t_agw):
                    q.wait(last_tok, t_agw)
                    for dc in range(DC):
                        inst = q.e.matmul(
                            psRT[:, :],
                            aggslab[:, 128 * dc:128 * (dc + 1)],
                            thT[:, 2 * dc:2 * (dc + 1)],
                            start=(dc == 0), stop=(dc == DC - 1))
                    inst.then_inc(tH.h, 1)
                em("tensor", amm)
                t_ev = one("vector", tH, [amm_tok],
                           lambda q, oc=oc: q.e.tensor_scalar(
                               thTf[:, 2 * oc:2 * (oc + 1)], psRT[:, :],
                               aggbr_sb[:, oc:oc + 1], 0.0, ALU.add, ALU.add))
                t_bt = one("tensor", tH, [t_ev],
                           lambda q, oc=oc: q.e.transpose(
                               psRow[:, :], thTf[:, 2 * oc:2 * (oc + 1)], ident_sb[:, :]))
                t_cp2 = one("scalar", tH, [t_bt],
                            lambda q, oc=oc: q.e.copy(scrA[0:2, 128 * oc:128 * (oc + 1)],
                                                      psRow[:, :]))
                last_tok = t_cp2
                agg_done.append(t_cp2)
            t_agst = dma("gpsimd", tHd, agg_done,
                         lambda q: q.e.dma_start(out=ag_in[0][:, :],
                                                 in_=scrA[0:2, 0:256]))
            t_agcc = coll([t_agst], lambda q: q.e.collective_compute(
                             "AllGather", ALU.bypass, replica_groups=G8,
                             ins=[ag_in[0].ap().opt()], outs=[ag_out[0].ap().opt()]))
            t_thld = None
            for r in range(NCORE):
                t_thld = dma("gpsimd", tHd, [t_agcc],
                             lambda q, r=r: q.e.dma_start(
                                 out=th_tm[:, 256 * r:256 * (r + 1)],
                                 in_=ag_out[0][2 * r:2 * r + 2, :]))
            TH0 = t_thld
            flush()

            nc.leave_named_scope("phD", _sc[0], False)
            _sc = nc.enter_named_scope("phE", False)
            if upto == "D":
                def finD(q):
                    q.wait((ySt, ySt.n), (cc, cc.n), (dSt, dSt.n), (tHd, tHd.n),
                           (dLd, dLd.n), (dAc, dAc.n), (dPe, dPe.n), (tH, tH.n))
                em("sync", finD)
                em("gpsimd", finD)
                flush()
                stD.close()
                _a2a_ctx.__exit__(None, None, None)
                return nc

            D_END = [(dPe, dPe.n), (dDv, dDv.n), (dAc, dAc.n), (dSt, dSt.n),
                     (tH, tH.n), (tHd, tHd.n)]
            DST_END = (dSt, dSt.n)
            stD.close()
            stE = ExitStack()
            wgslab = [sb(stE, f"wgslab{i}", [128, D], BF16) for i in range(2)]
            bcslab = sb(stE, "bcslab", [128, D], BF16)
            d1sb = [sb(stE, f"d1sb{i}", [128, DC * 128], BF16) for i in range(2)]
            d2sb = sb(stE, "d2sb", [128, 4096], BF16)
            blockbuf = sb(stE, "blockbuf", [128, DC * 512], BF16)
            gts = [sb(stE, f"gts{i}", [128, 512], BF16) for i in range(2)]
            out2tm = sb(stE, "out2tm", [128, D], BF16)
            n_tm = sb(stE, "n_tm", [128, D], BF16)
            x1b = [sb(stE, f"x1b{i}", [128, D], F32) for i in range(2)]
            x2sb = sb(stE, "x2sb", [128, D], F32)
            outgB_sb = sb(stE, "outgB_sb", [128, D], F32)
            outbB_sb = sb(stE, "outbB_sb", [128, D], F32)
            lnstE = sb(stE, "lnstE", [128, 8], F32)
            psE = [ps(stE, f"psE{i}", [128, 512], F32) for i in range(2)]
            psT2 = [ps(stE, f"psT2{i}", [128, 128], BF16) for i in range(2)]
            psd1 = ps(stE, "psd1", [128, 4], F32)
            psd2 = ps(stE, "psd2", [2, 512], F32)
            psRTb2 = ps(stE, "psRTb2", [128, 2], BF16)
            psRow2 = ps(stE, "psRow2", [2, 128], BF16)
            MUE = lnstE[:, 0:1]; M2E = lnstE[:, 1:2]
            VARE = lnstE[:, 2:3]; RSTDE = lnstE[:, 3:4]

            t_outg = dma("sync", eLd, D_END,
                         lambda q: q.e.dma_start(out=outgB_sb[:, :], in_=outgB))
            t_outb2 = dma("sync", eLd, D_END,
                          lambda q: q.e.dma_start(out=outbB_sb[:, :], in_=outbB))
            t_d2brow = dma("sync", eLd, D_END + [(dPe, dPe.n)],
                           lambda q: q.e.dma_start(out=biasrow[:, :], in_=d2b_row))

            if upto == "E0":
                def finE0(q):
                    q.wait((ySt, ySt.n), (cc, cc.n), (tHd, tHd.n), (eLd, eLd.n))
                em("sync", finE0)
                em("gpsimd", finE0)
                flush()
                stE.close()
                _a2a_ctx.__exit__(None, None, None)
                return nc

            th_d1mm = {}
            th_gel = {}
            th_d2done = {}
            th_prev = TH0
            NSTEP = {"E1a": 1, "E1b": 2, "E1c": 3}.get(upto, 4)
            for step in range(NSTEP):
                t_bf = row_ln(th_tm[:, :], png2_sb, pnb2_sb, [th_prev])
                t_tT = row_T(rowbf, thT, DC, t_bf, psRTb2,
                             first_extra=(D_END if step == 0 else ()))
                for oc in range(8):
                    wp = oc % 2
                    t_wl = dma("sync", eLd,
                               [cast_toks["d1"], th_d1mm.get((step, oc - 2)),
                                th_d1mm.get((step - 1, oc + 6))],
                               lambda q, oc=oc, wp=wp: q.e.dma_start(
                                   out=d1sb[wp][:, :],
                                   in_=d1w3[:, 2048 * oc:2048 * (oc + 1)]))

                    tH.n += 1
                    dmm_tok = (tH, tH.n)

                    def dmm(q, oc=oc, wp=wp, t_tT=t_tT, t_wl=t_wl, step=step,
                            _g1=th_gel.get((step, oc - 1)),
                            _g2=th_gel.get((step - 1, 7))):
                        q.wait(t_tT, t_wl, _g1, _g2)
                        for dc in range(DC):
                            inst = q.e.matmul(
                                psd1[:, 0:2],
                                d1sb[wp][:, 128 * dc:128 * (dc + 1)],
                                thT[:, 2 * dc:2 * (dc + 1)],
                                start=(dc == 0), stop=(dc == DC - 1))
                        inst.then_inc(tH.h, 1)
                    em("tensor", dmm)
                    th_d1mm[(step, oc)] = dmm_tok
                    th_gel[(step, oc)] = one("scalar", tH, [dmm_tok],
                                             lambda q, oc=oc: q.e.activation(
                                                 gT[:, 2 * oc:2 * (oc + 1)],
                                                 psd1[:, 0:2],
                                                 AF.Gelu, bias=d1br_sb[:, oc:oc + 1]))
                gel_all = [th_gel[(step, oc)] for oc in range(8)]
                prev_ev = None
                ev4 = []
                prev_mm = th_d2done.get(step - 1)
                for ct in range(4):
                    t_d2w = dma("sync", eLd,
                                [cast_toks["d2"], prev_mm, t_d2brow],
                                lambda q, ct=ct: q.e.dma_start(
                                    out=d2sb[:, :],
                                    in_=d2w3[:, 4096 * ct:4096 * (ct + 1)]))

                    tH.n += 1
                    d2mm_tok = (tH, tH.n)

                    def d2mm(q, ct=ct, prev_ev=prev_ev, gel_all=gel_all, t_d2w=t_d2w):
                        q.wait(t_d2w, prev_ev, *gel_all)
                        q.e.matmul(psd2[:, :], onesz_sb[:, :],
                                   biasrow[:, 512 * ct:512 * (ct + 1)],
                                   start=True, stop=False)
                        for oc in range(8):
                            inst = q.e.matmul(
                                psd2[:, :], gT[:, 2 * oc:2 * (oc + 1)],
                                d2sb[:, 512 * oc:512 * (oc + 1)],
                                start=False, stop=(oc == 7))
                        inst.then_inc(tH.h, 1)
                    em("tensor", d2mm)
                    prev_mm = d2mm_tok
                    prev_ev = one("scalar", tH, [d2mm_tok],
                                  lambda q, ct=ct: q.e.copy(
                                      scrA[:, 512 * ct:512 * (ct + 1)], psd2[:, :]))
                    ev4.append(prev_ev)
                th_d2done[step] = ev4[-1]
                t_st = dma("gpsimd", tHd, ev4,
                           lambda q, step=step: q.e.dma_start(
                               out=ar_in[1 + step][:, :], in_=scrA[:, :]))
                t_cc = coll([t_st], lambda q, step=step: q.e.collective_compute(
                               "AllReduce", ALU.add, replica_groups=G8,
                               ins=[ar_in[1 + step].ap().opt()],
                               outs=[ar_out[1 + step].ap().opt()]))
                t_ld = dma("gpsimd", tHd, [t_cc],
                           lambda q, step=step: q.e.dma_start(
                               out=scrA[:, :], in_=ar_out[1 + step][:, :]))
                th_prev = one("vector", tH, [t_ld],
                              lambda q: q.e.tensor_tensor(th_tm[:, :], th_tm[:, :],
                                                          scrA[:, :], ALU.add))
                flush()

            if "dbg_th" in dbg_out:
                one("sync", ySt, [th_prev],
                    lambda q: q.e.dma_start(out=dbg_out["dbg_th"], in_=th_tm[:, :]), k=16)

            if upto in ("E1", "E1a", "E1b", "E1c"):
                def finE1(q):
                    q.wait((ySt, ySt.n), (cc, cc.n), (tHd, tHd.n), (eLd, eLd.n),
                           (tH, tH.n))
                em("sync", finE1)
                em("gpsimd", finE1)
                flush()
                stE.close()
                _a2a_ctx.__exit__(None, None, None)
                return nc

            t_bf2 = one("vector", tH, [th_prev],
                        lambda q: q.e.tensor_copy(rowbf[:, :], th_tm[:, :]))
            t_tT2 = row_T(rowbf, thT, DC, t_bf2, psRTb2)
            bc_done = []
            last_tok = t_tT2
            for oc in range(2):
                t_bcw = dma("sync", eLd, [cast_toks["bc"], last_tok],
                            lambda q, oc=oc: q.e.dma_start(
                                out=bcslab[:, :], in_=bcw3[:, D * oc:D * (oc + 1)]))

                tH.n += 1
                bmm_tok = (tH, tH.n)

                def bmm(q, oc=oc, last_tok=last_tok, t_bcw=t_bcw):
                    q.wait(last_tok, t_bcw)
                    for dc in range(DC):
                        inst = q.e.matmul(
                            psd1[:, 0:2],
                            bcslab[:, 128 * dc:128 * (dc + 1)],
                            thT[:, 2 * dc:2 * (dc + 1)],
                            start=(dc == 0), stop=(dc == DC - 1))
                    inst.then_inc(tH.h, 1)
                em("tensor", bmm)
                t_ev = one("vector", tH, [bmm_tok],
                           lambda q, oc=oc: q.e.tensor_scalar(
                               thTf[:, 2 * oc:2 * (oc + 1)], psd1[:, 0:2],
                               bcbr_sb[:, oc:oc + 1], 0.0, ALU.add, ALU.add))
                t_bt = one("tensor", tH, [t_ev],
                           lambda q, oc=oc: q.e.transpose(
                               psRow2[:, :], thTf[:, 2 * oc:2 * (oc + 1)], ident_sb[:, :]))
                t_cp2 = one("scalar", tH, [t_bt],
                            lambda q, oc=oc: q.e.copy(scrA[0:2, 128 * oc:128 * (oc + 1)],
                                                      psRow2[:, :]))
                last_tok = t_cp2
                bc_done.append(t_cp2)
            t_bcst = dma("gpsimd", tHd, bc_done,
                         lambda q: q.e.dma_start(out=ag_in[1][:, :],
                                                 in_=scrA[0:2, 0:256]))
            t_bccc = coll([t_bcst], lambda q: q.e.collective_compute(
                             "AllGather", ALU.bypass, replica_groups=G8,
                             ins=[ag_in[1].ap().opt()], outs=[ag_out[1].ap().opt()]))
            t_bcld = None
            for r in range(NCORE):
                t_bcld = dma("gpsimd", tHd, [t_bccc],
                             lambda q, r=r: q.e.dma_start(
                                 out=scrA[:, 256 * r:256 * (r + 1)],
                                 in_=ag_out[1][2 * r:2 * r + 2, :]))
            t_bcbf = one("vector", tH, [t_bcld],
                         lambda q: q.e.tensor_copy(rowbf[:, :], scrA[:, :]))
            last_tok = t_bcbf
            for dc in range(DC):
                t = one("tensor", tH, [last_tok],
                        lambda q, dc=dc: q.e.transpose(
                            psRTb2[:, :], rowbf[:, 128 * dc:128 * (dc + 1)],
                            ident_sb[0:2, 0:2]))
                last_tok = one("scalar", tH, [t],
                               lambda q, dc=dc: q.e.copy(bcT[:, 2 * dc:2 * dc + 2],
                                                         psRTb2[:, :]))
            BCT = last_tok
            flush()

            if "dbg_bc" in dbg_out:
                one("sync", ySt, [BCT],
                    lambda q: q.e.dma_start(out=dbg_out["dbg_bc"], in_=scrA[:, :]), k=16)

            if upto == "E2":
                def finE2(q):
                    q.wait((ySt, ySt.n), (cc, cc.n), (tHd, tHd.n), (eLd, eLd.n),
                           (tH, tH.n))
                em("sync", finE2)
                em("gpsimd", finE2)
                flush()
                stE.close()
                _a2a_ctx.__exit__(None, None, None)
                return nc

            e_pe = []
            e_sg = []
            e_ts = []
            e_tr_cp = []
            x1_free = {}
            x2_free = {}
            lnE_state = {"last": None, "sqfree": None}
            prev_tb_tr = {"t": None}
            gchunk = 0
            NTB = {"G1": 1, "G2": 2, "G3": 3}.get(upto, 4)
            for tb in range(NTB):
                b = tb // 2
                for cci in range(DC):
                    idx = len(e_pe)
                    bp = idx % 2
                    wp = idx % 2
                    t_wg = dma("sync", eLd,
                               [cast_toks["wg"], DST_END,
                                e_pe[idx - 2] if idx >= 2 else None],
                               lambda q, cci=cci, wp=wp: q.e.dma_start(
                                   out=wgslab[wp][:, :],
                                   in_=wg2[:, D * cci:D * (cci + 1)]))

                    ePe.n += 1
                    e_pe.append((ePe, ePe.n))

                    def epe(q, tb=tb, cci=cci, bp=bp, idx=idx, wp=wp, t_wg=t_wg,
                            _f=(e_sg[idx - 2] if idx >= 2 else None)):
                        q.wait(t_wg, D_TR_END, LD0, _f)
                        for dc in range(DC):
                            inst = q.e.matmul(
                                psE[bp][:, :],
                                wgslab[wp][:, 128 * dc:128 * (dc + 1)],
                                xn2T[:, TLOC * dc + 512 * tb: TLOC * dc + 512 * (tb + 1)],
                                start=(dc == 0), stop=(dc == DC - 1))
                        inst.then_inc(ePe.h, 1)
                    em("tensor", epe)
                    sp = idx % 2
                    e_sg.append(one("scalar", eAc,
                                    [e_pe[idx], e_ts[idx - 2] if idx >= 2 else None],
                                    lambda q, bp=bp, sp=sp, cci=cci: q.e.activation(
                                        gts[sp][:, :], psE[bp][:, :], AF.Sigmoid,
                                        bias=gatebr_sb[:, cci:cci + 1])))
                    t_t1 = one("vector", eDv, [e_sg[idx], BCT],
                               lambda q, sp=sp, cci=cci, b=b: q.e.tensor_scalar(
                                   gts[sp][:, :], gts[sp][:, :],
                                   bcT[:, 2 * cci + b:2 * cci + b + 1], 0.0,
                                   ALU.mult, ALU.add))
                    e_ts.append(one("vector", eDv, [t_t1, prev_tb_tr["t"]],
                                    lambda q, sp=sp, cci=cci, tb=tb: q.e.tensor_tensor(
                                        blockbuf[:, 512 * cci:512 * (cci + 1)],
                                        xn2T[:, TLOC * cci + 512 * tb: TLOC * cci + 512 * (tb + 1)],
                                        gts[sp][:, :], ALU.add)))
                blk_done = e_ts[-DC:]

                last_tr_this_tb = None
                for i4 in range(4):
                    row0 = 512 * tb + 128 * i4
                    for dc in range(DC):
                        pp = dc % 2
                        idx2 = gchunk * DC + dc
                        t = one("tensor", ePe,
                                blk_done + [e_tr_cp[idx2 - 2] if idx2 >= 2 else None],
                                lambda q, dc=dc, pp=pp, i4=i4: q.e.transpose(
                                    psT2[pp][:, :],
                                    blockbuf[:, 512 * dc + 128 * i4: 512 * dc + 128 * (i4 + 1)],
                                    ident_sb[:, :]))
                        last_tr_this_tb = t
                        e_tr_cp.append(one("scalar", eAc,
                                           [t, lnE_state["last"] if dc == 0 else None],
                                           lambda q, dc=dc, pp=pp: q.e.copy(
                                               out2tm[:, 128 * dc:128 * (dc + 1)],
                                               psT2[pp][:, :])))
                    t_evall = e_tr_cp[-1]
                    t_nm2 = emit_ln(out2tm[:, :], n_tm[:, :], D, EPS2, [t_evall],
                                    n_tm[:, :], lnE_state["last"],
                                    eDv, eAc, MUE, M2E, VARE, RSTDE)
                    lnE_state["last"] = t_nm2
                    xp = gchunk % 2
                    t_x1 = dma("sync", eLd, [DST_END, x1_free.get(gchunk - 2)],
                               lambda q, xp=xp, row0=row0: q.e.dma_start(
                                   out=x1b[xp][:, :], in_=x1s[row0:row0 + 128, :]))
                    t_g1 = one("vector", eDv, [t_nm2, x2_free.get(gchunk - 1), t_outg],
                               lambda q: q.e.tensor_tensor(
                                   x2sb[:, :], n_tm[:, :], outgB_sb[:, :], ALU.mult))
                    t_g1b = one("vector", eDv, [t_g1, t_outb2],
                                lambda q: q.e.tensor_tensor(
                                    x2sb[:, :], x2sb[:, :], outbB_sb[:, :], ALU.add))
                    t_g2 = one("vector", eDv, [t_g1b, t_x1],
                               lambda q, xp=xp: q.e.tensor_tensor(
                                   x2sb[:, :], x2sb[:, :], x1b[xp][:, :], ALU.add))
                    x1_free[gchunk] = t_g2
                    x2_free[gchunk] = dma("sync", ySt, [t_g2],
                                          lambda q, row0=row0: q.e.dma_start(
                                              out=y_loc[row0:row0 + 128, :],
                                              in_=x2sb[:, :]))
                    gchunk += 1
                prev_tb_tr["t"] = last_tr_this_tb
                flush()

            nc.leave_named_scope("phE", _sc[0], False)

            def fin(q):
                q.wait((ySt, ySt.n), (cc, cc.n), (dSt, dSt.n), (tHd, tHd.n),
                       (kst, kst.n), (cOs, cOs.n), (eLd, eLd.n), (eDv, eDv.n),
                       (eAc, eAc.n), (ePe, ePe.n))
            em("sync", fin)
            em("gpsimd", fin)
            flush()
            stE.close()

    _a2a_ctx.__exit__(None, None, None)
    return nc


def _prep_inputs(inputs, c):
    import ml_dtypes
    bf = ml_dtypes.bfloat16
    X = np.asarray(inputs["L0_attn_x"])
    g = {}
    g["x_loc"] = np.ascontiguousarray(
        np.concatenate([X[0, 1024 * c:1024 * (c + 1), :],
                        X[1, 1024 * c:1024 * (c + 1), :]], axis=0)).astype(np.float32)
    g["wq"] = np.ascontiguousarray(inputs["L0_attn_Wq"], dtype=np.float32)
    g["wk"] = np.ascontiguousarray(inputs["L0_attn_Wk"], dtype=np.float32)
    g["wv"] = np.ascontiguousarray(inputs["L0_attn_Wv"], dtype=np.float32)
    g["wo"] = np.ascontiguousarray(inputs["L0_attn_Wo"], dtype=np.float32)
    g["wg"] = np.ascontiguousarray(inputs["L0_ffn_gate_w"], dtype=np.float32)

    def resh(v, n):
        return np.ascontiguousarray(np.asarray(v, dtype=np.float32).reshape(n, 128).T)

    g["bqr"] = resh(inputs["L0_attn_bq"], DC)
    g["bkr"] = resh(inputs["L0_attn_bk"], DC)
    g["bvr"] = resh(inputs["L0_attn_bv"], DC)
    g["bo_row"] = np.asarray(inputs["L0_attn_bo"], dtype=np.float32).reshape(1, D)
    g["gatebr"] = resh(inputs["L0_ffn_gate_b"], DC)
    for nm, key in [("ing2", "L0_ffn_in_gamma"), ("inb2", "L0_ffn_in_beta"),
                    ("png2", "L0_ffn_pn_gamma"), ("pnb2", "L0_ffn_pn_beta")]:
        v = np.asarray(inputs[key], dtype=np.float32)
        g[nm] = np.ascontiguousarray(np.stack([v, v], axis=0))
    g["outgB"] = np.ascontiguousarray(np.broadcast_to(
        np.asarray(inputs["L0_ffn_out_gamma"], dtype=np.float32), (128, D)))
    g["outbB"] = np.ascontiguousarray(np.broadcast_to(
        np.asarray(inputs["L0_ffn_out_beta"], dtype=np.float32), (128, D)))
    g["d1w_s"] = np.ascontiguousarray(
        np.asarray(inputs["L0_ffn_d1_w"], dtype=np.float32)[:, 1024 * c:1024 * (c + 1)])
    g["d1br"] = resh(np.asarray(inputs["L0_ffn_d1_b"],
                                dtype=np.float32)[1024 * c:1024 * (c + 1)], 8)
    g["d2w_s"] = np.ascontiguousarray(
        np.asarray(inputs["L0_ffn_d2_w"], dtype=np.float32)[1024 * c:1024 * (c + 1), :])
    g["d2b_row"] = np.asarray(inputs["L0_ffn_d2_b"], dtype=np.float32).reshape(1, D)
    g["aggw_s"] = np.ascontiguousarray(
        np.asarray(inputs["L0_ffn_agg_w"], dtype=np.float32)[:, 256 * c:256 * (c + 1)])
    g["aggbr"] = resh(np.asarray(inputs["L0_ffn_agg_b"],
                                 dtype=np.float32)[256 * c:256 * (c + 1)], 2)
    g["bcw_s"] = np.ascontiguousarray(
        np.asarray(inputs["L0_ffn_bc_w"], dtype=np.float32)[:, 256 * c:256 * (c + 1)])
    g["bcbr"] = resh(np.asarray(inputs["L0_ffn_bc_b"],
                                dtype=np.float32)[256 * c:256 * (c + 1)], 2)
    g["ident_bf"] = np.eye(128, dtype=np.float32).astype(bf)
    g["band_bf"] = np.triu(np.ones((128, 128), dtype=np.float32)).astype(bf)
    g["allones_bf"] = np.ones((128, 128), dtype=np.float32).astype(bf)
    g["ones_f32_row"] = np.ones((1, 128), dtype=np.float32)
    g["ones_or_zero"] = np.full((1, 2), 1.0 if c == 0 else 0.0, dtype=np.float32)
    g["is_last"] = np.full((2, 1), 1.0 if c == 7 else 0.0, dtype=np.float32)
    g["cshift_col"] = np.full((128, 1), CSHIFT, dtype=np.float32)
    return g


def kernel(**inputs):
    nc = build()
    in_maps = [_prep_inputs(inputs, c) for c in range(NCORE)]
    res = run_bass_kernel_spmd(nc, in_maps, core_ids=list(range(NCORE)))
    out = np.empty((2, T, D), dtype=np.float32)
    for c in range(NCORE):
        y = res.results[c]["y_loc"]
        out[0, 1024 * c:1024 * (c + 1), :] = y[0:1024, :]
        out[1, 1024 * c:1024 * (c + 1), :] = y[1024:2048, :]
    return out
